# version 04
# speedup vs baseline: 1.0056x; 1.0056x over previous
import contextlib
import math
import numpy as np
import concourse.bass as bass
import concourse.mybir as mybir
from concourse.bass import ds
from concourse.bass_core_rust import IndirectOffsetOnAxis
from concourse.bass_utils import run_bass_kernel_spmd
from concourse.library_overlay import lower_extended_insts
from concourse.expressions_rust import RuntimeValue

F32 = mybir.dt.float32
BF16 = mybir.dt.bfloat16
I32 = mybir.dt.int32
U32 = mybir.dt.uint32
AF = mybir.ActivationFunctionType
ALU = mybir.AluOpType
AX = mybir.AxisListType

T = 4096
C = 1024
NCORES = 8


class Sem:
    def __init__(self, h):
        self.h = h
        self.n = 0

    def inc(self, k):
        self.n += k
        return (self, self.n)


class Prog:

    def __init__(self, nc):
        self.nc = nc
        self.es = contextlib.ExitStack()
        self.ops = {e: [] for e in ("sync", "tensor", "vector", "scalar", "gpsimd")}
        self.sems = {}
        for name in ("tensor", "vector", "scalar", "gpsimd", "coll"):
            self.sems[name] = Sem(self.es.enter_context(nc.semaphore("sem_" + name)))
        self.NQ = 3
        self.qsems = {}
        self.qidx = {}
        for q in ("sync", "gpsimd_q", "scalar_q"):
            self.qsems[q] = [Sem(self.es.enter_context(nc.semaphore(f"sem_{q}{i}")))
                             for i in range(self.NQ)]
            self.qidx[q] = 0
        self._waited = {}
        self._uid = 0
        self._scopes = []
        self._consts = {}
        self._notes = {}

    def uid(self):
        self._uid += 1
        return self._uid

    def _es(self):
        return self._scopes[-1] if self._scopes else self.es

    def sbuf(self, name, shape, dtype):
        return self._es().enter_context(self.nc.sbuf_tensor(f"{name}_{self.uid()}", shape, dtype))

    def psum(self, name, shape, dtype=F32):
        return self._es().enter_context(self.nc.psum_tensor(f"{name}_{self.uid()}", shape, dtype))

    def dram(self, name, shape, dtype):
        return self.nc.dram_tensor(f"{name}_{self.uid()}", shape, dtype,
                                   kind="Internal").ap()

    def note(self, obj):
        return self._notes.get(id(obj))

    def setn(self, obj, tok):
        self._notes[id(obj)] = tok

    def cur(self, eng):
        s = self.sems[eng]
        return (s, s.n)

    def constf(self, val, dtype=F32):
        key = (str(dtype), float(val))
        if key not in self._consts:
            t = self.es.enter_context(self.nc.sbuf_tensor(f"const_{self.uid()}", [128, 1], dtype))
            tok = self.op("vector", lambda v, t=t, val=val: v.memset(t[:], val))
            self._consts[key] = (t, tok)
        return self._consts[key]

    @contextlib.contextmanager
    def scope(self):
        s = contextlib.ExitStack()
        self._scopes.append(s)
        try:
            yield s
        finally:
            self._scopes.pop()
            self.barrier()
            s.close()

    def _emit_waits(self, eng_name, waits):
        out = []
        for wt in waits:
            if wt is None:
                continue
            (sem, tgt) = wt
            if tgt <= 0:
                continue
            key = (eng_name, id(sem))
            if self._waited.get(key, -1) >= tgt:
                continue
            self._waited[key] = tgt
            out.append((sem.h, tgt))
        return out

    def op(self, eng, build, waits=(), inc=1):
        sem = self.sems[eng]
        tok = sem.inc(inc)
        ws = self._emit_waits(eng, waits)

        def closure(e, ws=ws, build=build, sem=sem, inc=inc):
            for (h, t) in ws:
                e.wait_ge(h, t)
            ins = build(e)
            ins.then_inc(sem.h, inc)
        self.ops[eng].append(closure)
        return tok

    def raw(self, eng, build, waits=()):
        ws = self._emit_waits(eng, waits)

        def closure(e, ws=ws, build=build):
            for (h, t) in ws:
                e.wait_ge(h, t)
            build(e)
        self.ops[eng].append(closure)

    def dma(self, queue, out, in_, waits=(), **kw):
        eng = {"sync": "sync", "gpsimd_q": "gpsimd", "scalar_q": "scalar"}[queue]
        slot = self.qidx[queue] % self.NQ
        self.qidx[queue] += 1
        sem = self.qsems[queue][slot]
        ws = self._emit_waits(eng, list(waits) + [(sem, sem.n)])
        tok = sem.inc(16)

        import traceback as _tb
        rec_site = "".join(_tb.format_stack(limit=8)[:-2])
        dbg_id = self._uid

        def closure(e, ws=ws, out=out, in_=in_, sem=sem, kw=kw):
            for (h, t) in ws:
                e.wait_ge(h, t)
            in_ap = in_(e) if callable(in_) else in_
            try:
                e.dma_start(out=out, in_=in_ap, **kw).then_inc(sem.h, 16)
            except Exception as ex:
                raise RuntimeError(f"dma failed uid={dbg_id} ({ex!r}) out={out.tensor.name}\nsite:{rec_site}") from ex
        self.ops[eng].append(closure)
        return tok

    def indirect_dma(self, out, out_offset, in_, in_offset, waits=(), **kw):
        slot = self.qidx["gpsimd_q"] % self.NQ
        self.qidx["gpsimd_q"] += 1
        sem = self.qsems["gpsimd_q"][slot]
        ws = self._emit_waits("gpsimd", list(waits) + [(sem, sem.n)])
        tok = sem.inc(16)

        def closure(e, ws=ws):
            for (h, t) in ws:
                e.wait_ge(h, t)
            e.indirect_dma_start(out=out, out_offset=out_offset, in_=in_,
                                 in_offset=in_offset, **kw).then_inc(sem.h, 16)
        self.ops["gpsimd"].append(closure)
        return tok

    def collective(self, kind, op, rg, ins_aps, outs_aps, waits=()):
        sem = self.sems["coll"]
        tok = sem.inc(1)
        ws = self._emit_waits("gpsimd", waits)

        def closure(e, ws=ws):
            for (h, t) in ws:
                e.wait_ge(h, t)
            e.collective_compute(kind, op, replica_groups=rg,
                                 ins=[a.opt() for a in ins_aps],
                                 outs=[a.opt() for a in outs_aps]).then_inc(sem.h, 1)
        self.ops["gpsimd"].append(closure)
        return tok

    def barrier(self):
        marks = [(s, s.n) for s in self.sems.values()]
        for q in self.qsems.values():
            marks += [(s, s.n) for s in q]
        for eng in ("sync", "tensor", "vector", "scalar", "gpsimd"):
            ws = self._emit_waits(eng, marks)
            if not ws:
                continue

            def closure(e, ws=ws):
                for (h, t) in ws:
                    e.wait_ge(h, t)
            self.ops[eng].append(closure)

    def finalize(self, final_waits=()):
        nc = self.nc
        ws = self._emit_waits("gpsimd", list(final_waits))
        with nc.Block() as block:
            @block.sync
            def _(e):
                for f in self.ops["sync"]:
                    f(e)

            @block.tensor
            def _(e):
                for f in self.ops["tensor"]:
                    f(e)

            @block.vector
            def _(e):
                for f in self.ops["vector"]:
                    f(e)

            @block.scalar
            def _(e):
                for f in self.ops["scalar"]:
                    f(e)

            @block.gpsimd
            def _(e):
                for f in self.ops["gpsimd"]:
                    f(e)
                for (h, t) in ws:
                    e.wait_ge(h, t)
        self.es.close()
        lower_extended_insts(nc)
        return nc


def make_identity(P):
    idi = P.sbuf("identi", [128, 128], I32)
    idf = P.sbuf("identf", [128, 128], F32)
    idb = P.sbuf("identb", [128, 128], BF16)
    t0 = P.op("gpsimd", lambda g: g.iota(idi[:], pattern=[[1, 128]], base=0,
                                         channel_multiplier=-1))
    t1 = P.op("vector", lambda v: v.tensor_single_scalar(idf[:], idi[:], 0.0, ALU.is_equal),
              waits=[t0])
    t2 = P.op("vector", lambda v: v.tensor_copy(idb[:], idf[:]), waits=[t1])
    return idf, idb, t2



def emit_ln(P, x_tile, rows, ln_out, scratch, waits=(), eps=1e-5):
    sq, m, ss, var, rstd, msq = (scratch[k] for k in ("sq", "m", "ss", "var", "rstd", "msq"))
    cols = x_tile.shape[-1]
    w = list(waits) + [scratch.get("_last")]
    t_s = P.op("vector", lambda v: v.tensor_reduce(m[:rows], x_tile, AX.X, ALU.add), waits=w)
    t_sq = P.op("scalar", lambda s: s.activation(sq[:rows, :cols], x_tile, AF.Square,
                                                 accum_out=ss[:rows]), waits=w)
    t_m = P.op("vector", lambda v: v.tensor_single_scalar(m[:rows], m[:rows], 1.0 / cols, ALU.mult),
               waits=[t_s])
    t_msq = P.op("vector", lambda v: v.tensor_tensor(msq[:rows], m[:rows], m[:rows], ALU.mult),
                 waits=[t_m])
    t_v1 = P.op("vector", lambda v: v.tensor_single_scalar(var[:rows], ss[:rows], 1.0 / cols, ALU.mult),
                waits=[t_sq])
    t_v2 = P.op("vector", lambda v: v.tensor_tensor(var[:rows], var[:rows], msq[:rows], ALU.subtract),
                waits=[t_msq, t_v1])
    ceps, ceps_tok = P.constf(eps)
    t_l = P.op("scalar", lambda s: s.activation(rstd[:rows], var[:rows], AF.Ln, bias=ceps[:rows]),
               waits=[t_v2, ceps_tok])
    t_e = P.op("scalar", lambda s: s.activation(rstd[:rows], rstd[:rows], AF.Exp, scale=-0.5),
               waits=[t_l])
    t_o = P.op("vector", lambda v: v.tensor_scalar(ln_out[:rows, :cols], x_tile, m[:rows], rstd[:rows],
                                                   ALU.subtract, ALU.mult), waits=[t_e, t_m])
    scratch["_last"] = t_o
    return t_o


def emit_fp4_quant(P, x_tile, q_out, rows, K, scr, waits=()):
    B = K // 16
    scale, recip = scr["scale"], scr["recip"]
    xn, s1, mm, u = scr["xn"], scr["s1"], scr["mm"], scr["u"]
    w = list(waits) + [scr.get("_last")]
    t_r = P.op("vector", lambda v: v.tensor_reduce(
        scale[:rows, :B], x_tile.rearrange("p (b k) -> p b k", k=16), AX.X, ALU.max,
        apply_absolute_value=True), waits=w)
    t_sc = P.op("vector", lambda v: v.tensor_scalar(scale[:rows, :B], scale[:rows, :B], 1.0 / 6.0,
                                                    1e-12, ALU.mult, ALU.max), waits=[t_r])
    t_rc = P.op("vector", lambda v: v.reciprocal(recip[:rows, :B], scale[:rows, :B]), waits=[t_sc])
    rb = recip[:rows, :B].broadcast_to([rows, B, 16])
    t_abs = P.op("scalar", lambda s: s.activation(xn[:rows, :K], x_tile, AF.Abs), waits=w)
    xnv = xn[:rows, :K].rearrange("p (b k) -> p b k", k=16)
    t_xn = P.op("gpsimd", lambda g: g.tensor_tensor(xnv, xnv, rb, ALU.mult), waits=[t_abs, t_rc])
    t_s1 = P.op("scalar", lambda s: s.sign(s1[:rows, :K], xn[:rows, :K], bias=scr["cm025"][:rows]),
                waits=[t_xn, scr.get("cm025_tok")])
    t_mm = P.op("gpsimd", lambda g: g.tensor_scalar(mm[:rows, :K], s1[:rows, :K], 1.0, 0.5,
                                                    ALU.add, ALU.mult), waits=[t_s1])
    t_s3 = P.op("scalar", lambda s: s.sign(s1[:rows, :K], xn[:rows, :K], bias=scr["cm075"][:rows]),
                waits=[t_mm, scr.get("cm075_tok")])
    t_i2 = P.op("gpsimd", lambda g: g.tensor_scalar(s1[:rows, :K], s1[:rows, :K], 1.0, 0.25,
                                                    ALU.add, ALU.mult), waits=[t_s3])
    t_ql = P.op("vector", lambda v: v.scalar_tensor_tensor(mm[:rows, :K], mm[:rows, :K], 0.5,
                                                           s1[:rows, :K], ALU.mult, ALU.add),
                waits=[t_i2])
    xn_u = xn.bitcast(U32)
    t_u1 = P.op("vector", lambda v: v.tensor_single_scalar(u[:rows, :K], xn_u[:rows, :K], 16,
                                                           ALU.logical_shift_right), waits=[t_xn])
    t_u2 = P.op("gpsimd", lambda g: g.tensor_single_scalar(u[:rows, :K], u[:rows, :K], 32.0,
                                                           ALU.add), waits=[t_u1, t_i2])
    t_u3 = P.op("vector", lambda v: v.tensor_scalar(u[:rows, :K], u[:rows, :K], 0x0000FFC0, 16,
                                                    ALU.bitwise_and, ALU.logical_shift_left),
                waits=[t_u2])
    qb = u.bitcast(F32)
    msk = scr["msk"]
    t_cond = P.op("vector", lambda v: v.tensor_single_scalar(msk[:rows, :K], xn[:rows, :K], 1.0,
                                                             ALU.is_ge), waits=[t_ql, t_u3])
    t_sel = P.op("vector", lambda v: v.select(xn[:rows, :K], msk[:rows, :K], qb[:rows, :K],
                                              mm[:rows, :K]), waits=[t_cond])
    t_s2 = P.op("scalar", lambda s: s.sign(s1[:rows, :K], x_tile, bias=scr["c0"][:rows]),
                waits=[t_sel, scr.get("c0_tok")])
    t_q2 = P.op("gpsimd", lambda g: g.tensor_tensor(xn[:rows, :K], xn[:rows, :K], s1[:rows, :K],
                                                    ALU.mult), waits=[t_s2])
    sbb = scale[:rows, :B].broadcast_to([rows, B, 16])
    qv = q_out[:rows, :K].rearrange("p (b k) -> p b k", k=16)
    t_q3 = P.op("vector", lambda v: v.tensor_tensor(qv, xnv, sbb, ALU.mult), waits=[t_q2])
    scr["_last"] = t_q3
    return t_q3


def emit_transpose_to_dram(P, sb_bf16, rows, cols, dram_T, dram_row0, dram_col0,
                           idb, ps_tr, sb_tr, waits=()):
    toks = []
    n_ch = (cols + 127) // 128
    for ch in range(n_ch):
        c0 = ch * 128
        cw = min(128, cols - c0)
        t_t = P.op("tensor", lambda t, c0=c0, cw=cw: t.transpose(
            ps_tr[0:cw, 0:rows], sb_bf16[0:rows, c0:c0 + cw], idb[:]),
            waits=list(waits) + [P.note(ps_tr)])
        t_c = P.op("vector", lambda v, cw=cw: v.tensor_copy(sb_tr[0:cw, 0:rows], ps_tr[0:cw, 0:rows]),
                   waits=[t_t, P.note(sb_tr)])
        P.setn(ps_tr, t_c)
        t_d = P.dma("sync", dram_T[dram_row0 + c0:dram_row0 + c0 + cw,
                            dram_col0:dram_col0 + rows], sb_tr[0:cw, 0:rows], waits=[t_c])
        P.setn(sb_tr, t_d)
        toks.append(t_d)
    return toks[-1]



def stage_l1ffn(P, xin, xout, w, coef=0.5):
    I = 2816
    idf, idb, _ = make_identity(P)
    P.constf(1e-5)
    P.constf(-0.25)
    P.constf(-0.75)
    P.constf(0.0)
    ps_tr = P.psum("ps_tr", [128, 128], BF16)

    wgqT = P.dram("l1_wgqT", [C, I], BF16)
    wuqT = P.dram("l1_wuqT", [C, I], BF16)
    wdqT = P.dram("l1_wdqT", [I, C], BF16)
    xqT = P.dram("l1_xqT", [C, T], BF16)

    KMAX = I
    sb_gate = P.sbuf("b_gate", [128, I], F32)
    scr = {
        "scale": P.sbuf("q_scale", [128, KMAX // 16], F32),
        "recip": P.sbuf("q_recip", [128, KMAX // 16], F32),
        "xn": P.sbuf("q_xn", [128, KMAX], F32),
        "s1": P.sbuf("q_s1", [128, KMAX], BF16),
        "mm": P.sbuf("q_mm", [128, KMAX], BF16),
        "u": sb_gate.bitcast(U32),
        "msk": P.sbuf("q_msk", [128, KMAX], mybir.dt.uint8),
    }
    sb_q = P.sbuf("w_q", [128, KMAX], BF16)
    sb_tr = P.sbuf("w_tr", [128, 128], BF16)
    scr["cm025"], scr["cm025_tok"] = P.constf(-0.25)
    scr["cm075"], scr["cm075_tok"] = P.constf(-0.75)
    scr["c0"], scr["c0_tok"] = P.constf(0.0)

    with P.scope():
        sb_in = P.sbuf("w_in", [128, KMAX], F32)

        def quant_weight(wap, R, K, outT):
            last = None
            for rt in range(R // 128):
                t_l = P.dma("sync", sb_in[:, :K], wap[rt * 128:(rt + 1) * 128, :],
                            waits=[last])
                t_q = emit_fp4_quant(P, sb_in[:, :K], sb_q[:, :K], 128, K, scr, waits=[t_l])
                last = emit_transpose_to_dram(P, sb_q, 128, K, outT, 0, rt * 128,
                                              idb, ps_tr, sb_tr, waits=[t_q])
            return last

        quant_weight(w["w_gate"], I, C, wgqT)
        quant_weight(w["w_up"], I, C, wuqT)
        quant_weight(w["w_down"], C, I, wdqT)

        sb_x = P.sbuf("a_x", [128, C], F32)
        sb_ln = P.sbuf("a_ln", [128, C], F32)
        lnscr = {
            "sq": scr["xn"],
            "m": P.sbuf("ln_m", [128, 1], F32),
            "ss": P.sbuf("ln_ss", [128, 1], F32),
            "var": P.sbuf("ln_var", [128, 1], F32),
            "rstd": P.sbuf("ln_rstd", [128, 1], F32),
            "msq": P.sbuf("ln_msq", [128, 1], F32),
        }
        last = None
        for tt in range(T // 128):
            t_l = P.dma("sync", sb_x[:], xin[tt * 128:(tt + 1) * 128, :], waits=[last])
            t_ln = emit_ln(P, sb_x[:], 128, sb_ln[:], lnscr, waits=[t_l, scr.get("_last")])
            t_q = emit_fp4_quant(P, sb_ln[:], sb_q[:, :C], 128, C, scr, waits=[t_ln])
            last = emit_transpose_to_dram(P, sb_q, 128, C, xqT, 0, tt * 128,
                                          idb, ps_tr, sb_tr, waits=[t_q])

    P.barrier()

    sb_wg = P.sbuf("b_wg", [128, 8 * I], BF16)
    sb_wu = P.sbuf("b_wu", [128, 8 * I], BF16)
    sb_wd = P.sbuf("b_wd", [128, 22 * C], BF16)
    w_toks = []
    for cc in range(8):
        w_toks.append(P.dma("sync", sb_wg[:, cc * I:(cc + 1) * I], wgqT[cc * 128:(cc + 1) * 128, :]))
        w_toks.append(P.dma("sync", sb_wu[:, cc * I:(cc + 1) * I], wuqT[cc * 128:(cc + 1) * 128, :]))
    for kc in range(22):
        w_toks.append(P.dma("sync", sb_wd[:, kc * C:(kc + 1) * C], wdqT[kc * 128:(kc + 1) * 128, :]))

    sb_xq = P.sbuf("b_xq", [128, 8 * 128], BF16)
    ps_ab = [P.psum(f"ps_ab{i}", [128, 512], F32) for i in range(2)]
    ps_o = [P.psum(f"ps_o{i}", [128, 512], F32) for i in range(2)]
    sb_h = P.sbuf("b_h", [128, I], F32)
    sb_hqT = P.sbuf("b_hqT", [128, 22 * 128], BF16)
    sb_xin = P.sbuf("b_xin", [128, C], F32)
    sb_out = P.sbuf("b_out", [128, C], F32)

    n_chunks = [(i * 512, min(512, I - i * 512)) for i in range((I + 511) // 512)]

    prev_store = None
    prev_tile_tensor = None
    prev_t_h = None
    prev_res = []
    bank_free = {0: None, 1: None}
    obank_free = {0: None, 1: None}
    for tt in range(T // 128):
        tb = tt * 128
        t_x = []
        for cc in range(8):
            t_x.append(P.dma("sync", sb_xq[:, cc * 128:(cc + 1) * 128],
                             xqT[cc * 128:(cc + 1) * 128, tb:tb + 128],
                             waits=[prev_tile_tensor]))
        t_xi = P.dma("sync", sb_xin[:], xin[tb:tb + 128, :], waits=[prev_store])
        waits_w = w_toks + t_x
        evac_toks = []
        for which, dst in enumerate(("gate", "up")):
            for ci, (n0, nw) in enumerate(n_chunks):
                bi = ci % 2
                bank = ps_ab[bi]
                sb_w = sb_wg if which == 0 else sb_wu
                mm_last = None
                for cc in range(8):
                    mm_last = P.op("tensor", lambda t, cc=cc, n0=n0, nw=nw, bank=bank, sb_w=sb_w:
                                   t.matmul(bank[:, :nw], sb_xq[:, cc * 128:(cc + 1) * 128],
                                            sb_w[:, cc * I + n0:cc * I + n0 + nw],
                                            start=(cc == 0), stop=(cc == 7)),
                                   waits=waits_w + [bank_free[bi]])
                if which == 0:
                    t_e = P.op("scalar", lambda s, n0=n0, nw=nw, bank=bank:
                               s.activation(sb_gate[:, n0:n0 + nw], bank[:, :nw], AF.Silu),
                               waits=[mm_last, prev_t_h, scr.get("_last")])
                else:
                    t_e = P.op("vector", lambda v, n0=n0, nw=nw, bank=bank:
                               v.tensor_copy(sb_h[:, n0:n0 + nw], bank[:, :nw]),
                               waits=[mm_last, prev_t_h, scr.get("_last")])
                bank_free[bi] = t_e
                evac_toks.append(t_e)
        t_h = P.op("vector", lambda v: v.tensor_tensor(sb_h[:], sb_gate[:], sb_h[:], ALU.mult),
                   waits=evac_toks)
        prev_t_h = t_h
        t_hq = emit_fp4_quant(P, sb_h[:], sb_q[:, :I], 128, I, scr, waits=[t_h])
        tr_toks = []
        for kc in range(22):
            t_t = P.op("tensor", lambda t, kc=kc: t.transpose(
                ps_tr[:, :], sb_q[:, kc * 128:(kc + 1) * 128], idb[:]),
                waits=[t_hq, P.note(ps_tr)])
            t_c = P.op("vector", lambda v, kc=kc: v.tensor_copy(
                sb_hqT[:, kc * 128:(kc + 1) * 128], ps_tr[:, :]), waits=[t_t])
            P.setn(ps_tr, t_c)
            tr_toks.append(t_c)
        dn_toks = []
        for ci in range(2):
            mm_last = None
            for kc in range(22):
                mm_last = P.op("tensor", lambda t, kc=kc, ci=ci: t.matmul(
                    ps_o[ci][:, :], sb_hqT[:, kc * 128:(kc + 1) * 128],
                    sb_wd[:, kc * C + ci * 512:kc * C + ci * 512 + 512],
                    start=(kc == 0), stop=(kc == 21)),
                    waits=tr_toks + w_toks + [obank_free[ci]])
            t_res = P.op("vector", lambda v, ci=ci: v.scalar_tensor_tensor(
                sb_out[:, ci * 512:(ci + 1) * 512], ps_o[ci][:, :], coef,
                sb_xin[:, ci * 512:(ci + 1) * 512], ALU.mult, ALU.add),
                waits=[mm_last, t_xi, prev_store])
            obank_free[ci] = t_res
            dn_toks.append(t_res)
            prev_tile_tensor = mm_last
        prev_store = P.dma("gpsimd_q", xout[tb:tb + 128, :], sb_out[:], waits=dn_toks)
    return prev_store



def build_l1ffn_launch():
    nc = bass.Bass("TRN2", target_bir_lowering=False)
    xin = nc.dram_tensor("x", [T, C], F32, kind="ExternalInput").ap()
    wg = nc.dram_tensor("w_gate", [2816, 1024], F32, kind="ExternalInput").ap()
    wu = nc.dram_tensor("w_up", [2816, 1024], F32, kind="ExternalInput").ap()
    wd = nc.dram_tensor("w_down", [1024, 2816], F32, kind="ExternalInput").ap()
    xout = nc.dram_tensor("xo", [T, C], F32, kind="ExternalOutput").ap()
    P = Prog(nc)
    tok = stage_l1ffn(P, xin, xout, {"w_gate": wg, "w_up": wu, "w_down": wd})
    P.finalize(final_waits=[tok])
    return nc



def emit_col_load(P, vec_dram_ap, n, sb_cols, idf, ps_f32, waits=()):
    stage = P.sbuf("colstage", [1, n], F32)
    t_l = P.dma("sync", stage[:], vec_dram_ap.rearrange("(a b) -> a b", a=1), waits=waits)
    toks = []
    for cc in range(n // 128):
        t_t = P.op("tensor", lambda t, cc=cc: t.transpose(
            ps_f32[0:128, 0:1], stage[0:1, cc * 128:(cc + 1) * 128], idf[0:1, 0:1]),
            waits=[t_l, P.note(ps_f32)])
        t_c = P.op("vector", lambda v, cc=cc: v.tensor_copy(sb_cols[:, cc:cc + 1],
                                                            ps_f32[0:128, 0:1]), waits=[t_t])
        P.setn(ps_f32, t_c)
        toks.append(t_c)
    return toks[-1]


def emit_wprep_T(P, w_ap, R, K, outT, idb, ps_trb, scale=None, scale_rows=None, waits=()):
    with P.scope():
        sb_w = P.sbuf("wp_in", [128, K], BF16)
        sb_t = P.sbuf("wp_t", [128, 128], BF16)
        last = list(waits) + [P.cur("vector")]
        for rt in range(R // 128):
            t_l = P.dma("gpsimd_q", sb_w[:, :K], w_ap[rt * 128:(rt + 1) * 128, :], waits=last)
            pre = [t_l]
            if scale is not None and scale_rows[0] <= rt * 128 < scale_rows[1]:
                pre = [P.op("vector", lambda v: v.tensor_single_scalar(sb_w[:, :K], sb_w[:, :K],
                                                                      scale, ALU.mult), waits=[t_l])]
            toks = []
            for ch in range(K // 128):
                t_t = P.op("tensor", lambda t, ch=ch: t.transpose(
                    ps_trb[:, :], sb_w[:, ch * 128:(ch + 1) * 128], idb[:]),
                    waits=pre + [P.note(ps_trb)])
                t_c = P.op("vector", lambda v: v.tensor_copy(sb_t[:, :], ps_trb[:, :]),
                           waits=[t_t, P.note(sb_t)])
                P.setn(ps_trb, t_c)
                t_d = P.dma("sync", outT[ch * 128:(ch + 1) * 128, rt * 128:(rt + 1) * 128],
                            sb_t[:, :], waits=[t_c])
                P.setn(sb_t, t_d)
                toks.append(t_d)
            last = toks
        return last[-1]


def emit_ln_affine_T(P, src_dram_f32, dst_T_bf16, gcols, bcols, lnscr, idb, ps_trb,
                     also_plain_f32=None, waits=()):
    with P.scope():
        sb_x = P.sbuf("lat_x", [128, C], F32)
        sb_ln = P.sbuf("lat_ln", [128, C], BF16)
        sb_t = P.sbuf("lat_t", [128, 128], BF16)
        last = list(waits)
        for tt in range(T // 128):
            t_l = P.dma("sync", sb_x[:], src_dram_f32[tt * 128:(tt + 1) * 128, :], waits=last)
            t_ln = emit_ln(P, sb_x[:], 128, sb_ln[:], lnscr, waits=[t_l])
            toks = []
            for cc in range(8):
                t_t = P.op("tensor", lambda t, cc=cc: t.transpose(
                    ps_trb[:, :], sb_ln[:, cc * 128:(cc + 1) * 128], idb[:]),
                    waits=[t_ln] + toks[-1:])
                if gcols is not None:
                    t_c = P.op("vector", lambda v, cc=cc: v.tensor_scalar(
                        sb_t[:, :], ps_trb[:, :], gcols[:, cc:cc + 1], bcols[:, cc:cc + 1],
                        ALU.mult, ALU.add), waits=[t_t])
                else:
                    t_c = P.op("vector", lambda v: v.tensor_copy(sb_t[:, :], ps_trb[:, :]),
                               waits=[t_t])
                t_d = P.dma("sync", dst_T_bf16[cc * 128:(cc + 1) * 128, tt * 128:(tt + 1) * 128],
                            sb_t[:, :], waits=[t_c])
                toks.append(t_d)
            last = [toks[-1]]
        return last[0]



def stage_l2attn(P, xin, xout, w, upto="G"):
    H, hs, F = 16, 64, 4
    s = hs ** (-0.5)
    idf, idb, t_id = make_identity(P)
    P.constf(1e-5)
    ps_trb = P.psum("ps_trb", [128, 128], BF16)
    ps_trf = P.psum("ps_trf", [128, 128], F32)

    y_d = P.dram("l2_y", [T, C], F32)
    z_d = P.dram("l2_z", [T, C], F32)
    h1T = P.dram("l2_h1T", [C, T], BF16)
    h2T = P.dram("l2_h2T", [C, T], BF16)
    WqT = P.dram("l2_WqT", [C, 4 * C], BF16)
    WpT = P.dram("l2_WpT", [C, C], BF16)
    qkvfT = P.dram("l2_qkvfT", [4 * C, T], BF16)
    negmh_d = P.dram("l2_negmh", [H, T], BF16)
    mh_d = P.dram("l2_mh", [H, T], F32)
    den_d = P.dram("l2_den", [H, T], F32)
    causT = P.dram("l2_causT", [C, T], BF16)
    attT = P.dram("l2_attT", [C, T], BF16)
    ffT = P.dram("l2_ffT", [C, T], BF16)

    lnscr = {
        "sq": P.sbuf("l2ln_sq", [128, C], F32),
        "m": P.sbuf("l2ln_m", [128, 1], F32),
        "ss": P.sbuf("l2ln_ss", [128, 1], F32),
        "var": P.sbuf("l2ln_var", [128, 1], F32),
        "rstd": P.sbuf("l2ln_rstd", [128, 1], F32),
        "msq": P.sbuf("l2ln_msq", [128, 1], F32),
    }
    g1c = P.sbuf("l2_g1c", [128, 8], F32)
    b1c = P.sbuf("l2_b1c", [128, 8], F32)
    g2c = P.sbuf("l2_g2c", [128, 8], F32)
    b2c = P.sbuf("l2_b2c", [128, 8], F32)
    emit_col_load(P, w["ln1_g"], C, g1c, idf, ps_trf, waits=[t_id])
    emit_col_load(P, w["ln1_b"], C, b1c, idf, ps_trf)
    emit_col_load(P, w["ln2_g"], C, g2c, idf, ps_trf)
    emit_col_load(P, w["ln2_b"], C, b2c, idf, ps_trf)

    def _phaseA():
      with P.scope():
        sb_x = P.sbuf("a_x", [128, C], F32)
        sb_y = P.sbuf("a_y", [128, C], F32)
        sb_ln = P.sbuf("a_ln", [128, C], BF16)
        sb_t = P.sbuf("a_t", [128, 128], BF16)
        last = []
        for tt in range(T // 128):
            t_l = P.dma("sync", sb_x[:], xin[tt * 128:(tt + 1) * 128, :], waits=last)
            t_y = emit_ln(P, sb_x[:], 128, sb_y[:], lnscr, waits=[t_l])
            t_sy = P.dma("gpsimd_q", y_d[tt * 128:(tt + 1) * 128, :], sb_y[:], waits=[t_y])
            t_ln = emit_ln(P, sb_y[:], 128, sb_ln[:], lnscr, waits=[t_y])
            toks = []
            for cc in range(8):
                t_t = P.op("tensor", lambda t, cc=cc: t.transpose(
                    ps_trb[:, :], sb_ln[:, cc * 128:(cc + 1) * 128], idb[:]),
                    waits=[t_ln, P.note(ps_trb)])
                t_c = P.op("vector", lambda v, cc=cc: v.tensor_scalar(
                    sb_t[:, :], ps_trb[:, :], g1c[:, cc:cc + 1], b1c[:, cc:cc + 1],
                    ALU.mult, ALU.add), waits=[t_t, P.note(sb_t)])
                P.setn(ps_trb, t_c)
                t_d = P.dma("sync", h1T[cc * 128:(cc + 1) * 128, tt * 128:(tt + 1) * 128],
                            sb_t[:, :], waits=[t_c])
                P.setn(sb_t, t_d)
                toks.append(t_d)
            last = [toks[-1], t_sy]
    _phaseA()

    if upto == "A":
        return P.cur("vector")
    emit_wprep_T(P, w["W_qkvf"], 4 * C, C, WqT, idb, ps_trb)
    emit_wprep_T(P, w["W_proj"], C, C, WpT, idb, ps_trb)
    P.barrier()
    if upto == "B":
        return P.cur("vector")

    with P.scope():
        pdiv = P.sbuf("c_pdiv", [128, 1], F32)
        pio = P.sbuf("c_pio", [128, 1], I32)
        hgrid = P.sbuf("c_hg", [128, 16], I32)
        E16 = P.sbuf("c_E16", [128, 8 * 16], BF16)
        E16T = P.sbuf("c_E16T", [16, 8 * 128], BF16)
        t0 = P.op("gpsimd", lambda g: g.iota(pio[:], pattern=[[1, 1]], base=0, channel_multiplier=1))
        t1 = P.op("vector", lambda v: v.tensor_single_scalar(pdiv[:], pio[:], 63.5, ALU.is_gt), waits=[t0])
        t2 = P.op("gpsimd", lambda g: g.iota(hgrid[:], pattern=[[1, 16]], base=0, channel_multiplier=0))
        e_toks = []
        for cc in range(8):
            tE = P.op("vector", lambda v, cc=cc: v.tensor_scalar(
                E16[:, cc * 16:(cc + 1) * 16], hgrid[:], float(2 * cc), pdiv[:],
                ALU.subtract, ALU.is_equal), waits=[t1, t2])
            tT = P.op("tensor", lambda t, cc=cc: t.transpose(
                ps_trb[0:16, 0:128], E16[:, cc * 16:(cc + 1) * 16], idb[:]),
                waits=[tE, P.note(ps_trb)])
            tC = P.op("vector", lambda v, cc=cc: v.tensor_copy(E16T[:, cc * 128:(cc + 1) * 128],
                                                               ps_trb[0:16, 0:128]), waits=[tT])
            P.setn(ps_trb, tC)
            e_toks.append(tC)
        e_tok = e_toks[-1]

        sb_Wq = P.sbuf("c_Wq", [128, 8 * 4 * C], BF16)
        wq_toks = [P.dma("sync", sb_Wq[:, cc * 4 * C:(cc + 1) * 4 * C],
                         WqT[cc * 128:(cc + 1) * 128, :]) for cc in range(8)]
        qn2_d = P.dram("l2_qn2", [H, T], F32)

        def _phaseC():
          with P.scope():
            sb_h1 = P.sbuf("c_h1", [128, 8 * 512], BF16)
            sb_ev = P.sbuf("c_ev", [128, 512], BF16)
            sb_sq = P.sbuf("c_sq", [128, 512], BF16)
            sb_qn = P.sbuf("c_qn", [16, 512], F32)
            sb_kn = P.sbuf("c_kn", [16, 512], F32)
            kmax2 = P.sbuf("c_kmax2", [16, 1], F32)
            kmax_t = P.sbuf("c_kmaxt", [16, 1], F32)
            ps_mm = P.psum("c_ps", [128, 512], F32)
            ps_n = P.psum("c_psn", [16, 512], F32)
            t_kz = P.op("vector", lambda v: v.memset(kmax2[:], 0.0))
            prev_ev = None
            prev_sq = None
            prev_cp = None
            prev_km = t_kz
            prev_loads = [e_tok]
            for tc in range(T // 512):
                t0c = tc * 512
                t_x = [P.dma("sync", sb_h1[:, cc * 512:(cc + 1) * 512],
                             h1T[cc * 128:(cc + 1) * 128, t0c:t0c + 512], waits=prev_loads)
                       for cc in range(8)]
                for ec in range(32):
                    mm_last = None
                    for cc in range(8):
                        mm_last = P.op("tensor", lambda t, cc=cc, ec=ec: t.matmul(
                            ps_mm[:, :], sb_Wq[:, cc * 4 * C + ec * 128:cc * 4 * C + ec * 128 + 128],
                            sb_h1[:, cc * 512:(cc + 1) * 512],
                            start=(cc == 0), stop=(cc == 7)),
                            waits=wq_toks + t_x + [prev_ev, prev_sq])
                    t_ev = P.op("vector", lambda v: v.tensor_copy(sb_ev[:, :], ps_mm[:, :]),
                                waits=[mm_last, P.note(sb_ev)])
                    t_d = P.dma("gpsimd_q", qkvfT[ec * 128:(ec + 1) * 128, t0c:t0c + 512],
                                sb_ev[:, :], waits=[t_ev])
                    P.setn(sb_ev, t_d)
                    prev_ev = t_ev
                    prev_loads = [t_ev]
                    if ec < 16 and upto != "C1":
                        t_sq = P.op("scalar", lambda sc: sc.activation(sb_sq[:, :], sb_ev[:, :],
                                                                       AF.Square),
                                    waits=[t_ev, prev_cp])
                        prev_sq = t_sq
                        prev_cp = t_sq
                        if upto == "C2":
                            continue
                        acc = sb_qn if ec < 8 else sb_kn
                        sub = ec % 8
                        t_nm = P.op("tensor", lambda t, sub=sub: t.matmul(
                            ps_n[:, :], E16[:, sub * 16:(sub + 1) * 16], sb_sq[:, :],
                            start=True, stop=True), waits=[t_sq, prev_cp])
                        prev_sq = t_sq
                        if sub == 0:
                            t_ac = P.op("vector", lambda v, acc=acc: v.tensor_copy(
                                acc[:, :], ps_n[:, :]), waits=[t_nm])
                        else:
                            t_ac = P.op("vector", lambda v, acc=acc: v.tensor_add(
                                acc[:, :], acc[:, :], ps_n[:, :]), waits=[t_nm])
                        prev_cp = t_ac
                        if sub == 7 and upto != "C3":
                            if ec < 8:
                                t_qd = P.dma("sync", qn2_d[:, t0c:t0c + 512], sb_qn[:, :],
                                             waits=[t_ac])
                                prev_cp = t_qd
                            else:
                                t_kt = P.op("vector", lambda v: v.tensor_reduce(
                                    kmax_t[:], sb_kn[:, :], AX.X, ALU.max), waits=[t_ac, prev_km])
                                prev_km = P.op("vector", lambda v: v.tensor_max(
                                    kmax2[:], kmax2[:], kmax_t[:]), waits=[t_kt])
                                prev_cp = prev_km
            if upto in ("C1", "C2", "C3"):
                return
            sb_qn2 = P.sbuf("c_qn2full", [16, T], F32)
            negmh_sb = P.sbuf("c_negmh", [16, T], BF16)
            t_ld = P.dma("sync", sb_qn2[:, :], qn2_d[:, :], waits=[prev_cp, prev_ev])
            t_mh = P.op("vector", lambda v: v.tensor_scalar(sb_qn2[:, :], sb_qn2[:, :], kmax2[:],
                                                            s * s, ALU.mult, ALU.mult),
                        waits=[t_ld, prev_km])
            t_sq2 = P.op("scalar", lambda sc: sc.activation(sb_qn2[:, :], sb_qn2[:, :], AF.Sqrt),
                         waits=[t_mh])
            t_ng = P.op("vector", lambda v: v.tensor_single_scalar(negmh_sb[:, :], sb_qn2[:, :],
                                                                   -1.0, ALU.mult), waits=[t_sq2])
            t_dn = P.dma("sync", negmh_d[:, :], negmh_sb[:, :], waits=[t_ng])
        _phaseC()
        P.barrier()
        if upto in ("C", "C1", "C2", "C3"):
            return P.cur("vector")

        def _phaseD():
          with P.scope():
            msk = P.sbuf("d_msk", [128, 4 * 512], F32)
            mio = P.sbuf("d_mio", [128, 512], I32)
            m_toks = []
            for p in range(4):
                tI = P.op("gpsimd", lambda g, p=p: g.iota(mio[:], pattern=[[1, 512]],
                                                          base=-128 * p, channel_multiplier=-1),
                          waits=m_toks[-1:])
                tM = P.op("vector", lambda v, p=p: v.tensor_scalar(
                    msk[:, p * 512:(p + 1) * 512], mio[:], 0.0, -1e30, ALU.is_lt, ALU.mult),
                    waits=[tI])
                m_toks.append(tM)
            sb_khat = P.sbuf("d_khat", [65, T], BF16)
            sb_qhat = P.sbuf("d_qhat", [65, T], BF16)
            sb_vth = P.sbuf("d_vth", [64, T], BF16)
            sb_vhat = P.sbuf("d_vhat", [128, 32 * 65], BF16)
            sb_p = P.sbuf("d_p", [128, 2 * 512], BF16)
            sb_oc = P.sbuf("d_oc", [65, 512], F32)
            ps_s = [P.psum(f"d_pss{i}", [128, 512], F32) for i in range(2)]
            ps_o = P.psum("d_pso", [65, 512], F32)
            prev_h_toks = list(m_toks)
            for h in range(H):
                hb = 64 * h
                t_k = P.dma("sync", sb_khat[0:64, :], qkvfT[C + hb:C + hb + 64, :],
                            waits=prev_h_toks)
                t_ks = P.op("vector", lambda v: v.tensor_single_scalar(
                    sb_khat[0:64, :], sb_khat[0:64, :], s, ALU.mult), waits=[t_k])
                t_k1 = P.op("vector", lambda v: v.memset(sb_khat[64:65, :], 1.0),
                            waits=prev_h_toks)
                t_q = P.dma("sync", sb_qhat[0:64, :], qkvfT[hb:hb + 64, :], waits=prev_h_toks)
                t_qm = P.dma("sync", sb_qhat[64:65, :], negmh_d[h:h + 1, :], waits=prev_h_toks)
                t_vm = P.op("vector", lambda v: v.memset(sb_vhat[:, :], 1.0), waits=prev_h_toks)
                t_v = P.dma("sync", sb_vth[:, :], qkvfT[2 * C + hb:2 * C + hb + 64, :],
                            waits=prev_h_toks)
                vtoks = []
                for jt in range(T // 128):
                    tT = P.op("tensor", lambda t, jt=jt: t.transpose(
                        ps_trb[:, 0:64], sb_vth[:, jt * 128:(jt + 1) * 128], idb[0:64, 0:64]),
                        waits=[t_v, t_vm, P.note(ps_trb)])
                    tC = P.op("vector", lambda v, jt=jt: v.tensor_copy(
                        sb_vhat[:, jt * 65:jt * 65 + 64], ps_trb[:, 0:64]), waits=[tT])
                    P.setn(ps_trb, tC)
                    vtoks.append(tC)
                head_ready = [t_ks, t_k1, t_q, t_qm, vtoks[-1]]
                o_toks = []
                for qi in range(T // 512):
                    i0 = qi * 512
                    njt = 4 * (qi + 1)
                    mm_o_last = None
                    exp_toks = []
                    for jt in range(njt):
                        j0 = jt * 128
                        pb = jt % 2
                        t_s = P.op("tensor", lambda t, jt=jt, i0=i0, pb=pb: t.matmul(
                            ps_s[pb][:, :], sb_khat[:, jt * 128:(jt + 1) * 128],
                            sb_qhat[:, i0:i0 + 512], start=True, stop=True),
                            waits=head_ready + exp_toks[-2:-1] + o_toks)
                        if j0 >= i0:
                            pat = (j0 - i0) // 128
                            t_s = P.op("vector", lambda v, pb=pb, pat=pat: v.tensor_tensor(
                                ps_s[pb][:, :], ps_s[pb][:, :],
                                msk[:, pat * 512:(pat + 1) * 512], ALU.add), waits=[t_s])
                        t_e = P.op("scalar", lambda sc, pb=pb: sc.activation(
                            sb_p[:, pb * 512:(pb + 1) * 512], ps_s[pb][:, :], AF.Exp),
                            waits=[t_s] + ([mm_o_last] if mm_o_last else []))
                        exp_toks.append(t_e)
                        mm_o_last = P.op("tensor", lambda t, jt=jt, pb=pb: t.matmul(
                            ps_o[:, :], sb_vhat[:, jt * 65:(jt + 1) * 65],
                            sb_p[:, pb * 512:(pb + 1) * 512],
                            start=(jt == 0), stop=(jt == njt - 1)), waits=[t_e])
                    t_oc = P.op("vector", lambda v: v.tensor_copy(sb_oc[:, :], ps_o[:, :]),
                                waits=[mm_o_last, P.note(sb_oc)])
                    t_do = P.dma("gpsimd_q", causT[hb:hb + 64, i0:i0 + 512], sb_oc[0:64, :],
                                 waits=[t_oc])
                    t_dd = P.dma("sync", den_d[h:h + 1, i0:i0 + 512], sb_oc[64:65, :],
                                 waits=[t_oc])
                    P.setn(sb_oc, t_dd)
                    o_toks = [t_oc, t_do, t_dd]
                    prev_h_toks = [t_oc, t_do, t_dd]
        _phaseD()
        P.barrier()
        if upto == "D":
            return P.cur("vector")

        def _phaseE():
          with P.scope():
            sb_wc = P.sbuf("e_wc", [128, 256], BF16)
            t_wc0 = P.dma("gpsimd_q", sb_wc[0:64, :], w["W_conv"].rearrange("c d f -> c (d f)"))
            t_wc = P.dma("gpsimd_q", sb_wc[64:128, :], w["W_conv"].rearrange("c d f -> c (d f)"),
                         waits=[])
            sb_f = P.sbuf("e_f", [128, 8 * 512], BF16)
            sb_qt = P.sbuf("e_qt", [128, 8 * 512], BF16)
            sb_up = P.sbuf("e_up", [128, 4 * 8 * 512], BF16)
            sb_prod = P.sbuf("e_prod", [128, 512], BF16)
            sb_fa = P.sbuf("e_fa", [16, 4 * 512], F32)
            sb_nmh = P.sbuf("e_nmh", [16, 512], BF16)
            sb_pfb = P.sbuf("e_pfb", [16, 4 * 512], BF16)
            sb_denf = P.sbuf("e_denf", [16, 512], F32)
            sb_den = P.sbuf("e_den", [16, 512], F32)
            sb_recb = P.sbuf("e_recb", [16, 512], BF16)
            sb_ups = P.sbuf("e_ups", [128, 8 * 512], F32)
            sb_upsb = P.sbuf("e_upsb", [128, 8 * 512], BF16)
            sb_tmp = P.sbuf("e_tmp", [128, 512], F32)
            sb_caus = P.sbuf("e_caus", [128, 512], BF16)
            sb_att = P.sbuf("e_att", [128, 512], BF16)
            ps_a = P.psum("e_psa", [128, 512], F32)
            ps_b = P.psum("e_psb", [128, 512], F32)
            ps_fa = P.psum("e_psfa", [16, 512], F32)
            prev = [t_wc]
            for tc in range(T // 512):
                t0c = tc * 512
                t_f = [P.dma("sync", sb_f[:, cc * 512:(cc + 1) * 512],
                             qkvfT[3 * C + cc * 128:3 * C + (cc + 1) * 128, t0c:t0c + 512],
                             waits=prev) for cc in range(8)]
                t_qt = [P.dma("sync", sb_qt[:, cc * 512:(cc + 1) * 512],
                              qkvfT[cc * 128:(cc + 1) * 128, t0c:t0c + 512],
                              waits=prev) for cc in range(8)]
                t_nm = P.dma("sync", sb_nmh[:, :], negmh_d[:, t0c:t0c + 512], waits=prev)
                up_toks = []
                prev_upc = None
                for f in range(F):
                    for h in range(H):
                        cc = h // 2
                        r0 = (h % 2) * 64
                        t_mm = P.op("tensor", lambda t, f=f, cc=cc, r0=r0: t.matmul(
                            ps_a[0:64, :], sb_wc[r0:r0 + 64, f::4],
                            sb_f[r0:r0 + 64, cc * 512:(cc + 1) * 512],
                            start=True, stop=True), waits=[t_wc0, t_wc] + t_f + [prev_upc])
                        t_cp = P.op("vector", lambda v, f=f, cc=cc, r0=r0: v.tensor_copy(
                            sb_up[r0:r0 + 64, (f * 8 + cc) * 512:(f * 8 + cc) * 512 + 512],
                            ps_a[0:64, :]), waits=[t_mm])
                        prev_upc = t_cp
                        up_toks.append(t_cp)
                fa_toks = []
                prev_prod = None
                for f in range(F):
                    t_facp = None
                    for cc in range(8):
                        mm_last = None
                        for c2 in range(8):
                            mm_last = P.op("tensor", lambda t, f=f, cc=cc, c2=c2: t.matmul(
                                ps_b[:, :],
                                sb_Wq[:, c2 * 4 * C + C + cc * 128:c2 * 4 * C + C + cc * 128 + 128],
                                sb_up[:, (f * 8 + c2) * 512:(f * 8 + c2) * 512 + 512],
                                start=(c2 == 0), stop=(c2 == 7)),
                                waits=up_toks + [prev_prod])
                        t_pr = P.op("vector", lambda v, cc=cc: v.tensor_tensor(
                            sb_prod[:, :], ps_b[:, :], sb_qt[:, cc * 512:(cc + 1) * 512],
                            ALU.mult), waits=[mm_last] + t_qt)
                        prev_prod = t_pr
                        t_fa = P.op("tensor", lambda t, cc=cc: t.matmul(
                            ps_fa[:, :], E16[:, cc * 16:(cc + 1) * 16], sb_prod[:, :],
                            start=(cc == 0), stop=(cc == 7)), waits=[t_pr])
                    t_facp = P.op("vector", lambda v, f=f: v.tensor_copy(
                        sb_fa[:, f * 512:(f + 1) * 512], ps_fa[:, :]), waits=[t_fa])
                    fa_toks.append(t_facp)
                nmh_b = sb_nmh[:, :].rearrange("p (a t) -> p a t", a=1).broadcast_to([16, 4, 512])
                fa_v = sb_fa[:, :].rearrange("p (f t) -> p f t", t=512)
                t_sub = P.op("vector", lambda v: v.tensor_tensor(fa_v, fa_v, nmh_b, ALU.add),
                             waits=fa_toks + [t_nm])
                t_exp = P.op("scalar", lambda sc: sc.activation(sb_fa[:, :], sb_fa[:, :], AF.Exp),
                             waits=[t_sub])
                t_pfb = P.op("vector", lambda v: v.tensor_copy(sb_pfb[:, :], sb_fa[:, :]),
                             waits=[t_exp])
                faT_v = sb_fa[:, :].rearrange("p (f t) -> p t f", t=512)
                t_df = P.op("vector", lambda v: v.tensor_reduce(sb_denf[:, :], faT_v, AX.X,
                                                                ALU.add), waits=[t_exp])
                t_dl = P.dma("sync", sb_den[:, :], den_d[:, t0c:t0c + 512], waits=prev)
                t_dt = P.op("vector", lambda v: v.tensor_add(sb_den[:, :], sb_den[:, :],
                                                             sb_denf[:, :]), waits=[t_df, t_dl])
                t_rc = P.op("vector", lambda v: v.reciprocal(sb_den[:, :], sb_den[:, :]),
                            waits=[t_dt])
                t_rb = P.op("vector", lambda v: v.tensor_copy(sb_recb[:, :], sb_den[:, :]),
                            waits=[t_rc])
                ups_toks = []
                for cc in range(8):
                    for f in range(F):
                        t_rep = P.op("tensor", lambda t, cc=cc, f=f: t.matmul(
                            ps_a[0:128, :], E16T[:, cc * 128:(cc + 1) * 128],
                            sb_pfb[:, f * 512:(f + 1) * 512], start=True, stop=True),
                            waits=[t_pfb] + ups_toks[-1:])
                        if f == 0:
                            t_up = P.op("vector", lambda v, cc=cc, f=f: v.tensor_tensor(
                                sb_ups[:, cc * 512:(cc + 1) * 512],
                                sb_up[:, (f * 8 + cc) * 512:(f * 8 + cc) * 512 + 512],
                                ps_a[0:128, :], ALU.mult), waits=[t_rep])
                        else:
                            t_t1 = P.op("vector", lambda v, cc=cc, f=f: v.tensor_tensor(
                                sb_tmp[:, :],
                                sb_up[:, (f * 8 + cc) * 512:(f * 8 + cc) * 512 + 512],
                                ps_a[0:128, :], ALU.mult), waits=[t_rep])
                            t_up = P.op("vector", lambda v, cc=cc: v.tensor_add(
                                sb_ups[:, cc * 512:(cc + 1) * 512],
                                sb_ups[:, cc * 512:(cc + 1) * 512], sb_tmp[:, :]),
                                waits=[t_t1])
                        ups_toks.append(t_up)
                    t_ub = P.op("vector", lambda v, cc=cc: v.tensor_copy(
                        sb_upsb[:, cc * 512:(cc + 1) * 512],
                        sb_ups[:, cc * 512:(cc + 1) * 512]), waits=ups_toks[-1:])
                    ups_toks.append(t_ub)
                comb_toks = []
                for cc in range(8):
                    mm_last = None
                    for c2 in range(8):
                        mm_last = P.op("tensor", lambda t, cc=cc, c2=c2: t.matmul(
                            ps_b[:, :],
                            sb_Wq[:, c2 * 4 * C + 2 * C + cc * 128:c2 * 4 * C + 2 * C + cc * 128 + 128],
                            sb_upsb[:, c2 * 512:(c2 + 1) * 512],
                            start=(c2 == 0), stop=(c2 == 7)),
                            waits=ups_toks + comb_toks[-1:])
                    t_cl = P.dma("sync", sb_caus[:, :],
                                 causT[cc * 128:(cc + 1) * 128, t0c:t0c + 512],
                                 waits=comb_toks[-1:] + prev)
                    t_ad = P.op("vector", lambda v: v.tensor_tensor(
                        sb_tmp[:, :], ps_b[:, :], sb_caus[:, :], ALU.add),
                        waits=[mm_last, t_cl])
                    t_rep = P.op("tensor", lambda t, cc=cc: t.matmul(
                        ps_a[0:128, :], E16T[:, cc * 128:(cc + 1) * 128], sb_recb[:, :],
                        start=True, stop=True), waits=[t_rb, t_ad])
                    t_mu = P.op("vector", lambda v: v.tensor_tensor(
                        sb_att[:, :], sb_tmp[:, :], ps_a[0:128, :], ALU.mult),
                        waits=[t_rep])
                    t_da = P.dma("gpsimd_q", attT[cc * 128:(cc + 1) * 128, t0c:t0c + 512],
                                 sb_att[:, :], waits=[t_mu])
                    comb_toks.append(t_da)
                prev = [comb_toks[-1], t_pfb, t_rb]
        _phaseE()
        P.barrier()
    if upto == "E":
        return P.cur("vector")

    def _phaseF():
      with P.scope():
        sb_Wp = P.sbuf("f_Wp", [128, 8 * C], BF16)
        wp_toks = [P.dma("sync", sb_Wp[:, cc * C:(cc + 1) * C], WpT[cc * 128:(cc + 1) * 128, :])
                   for cc in range(8)]
        sb_at = P.sbuf("f_at", [128, 8 * 512], BF16)
        sb_apT = P.sbuf("f_apT", [128, 8 * 512], BF16)
        sb_tok = P.sbuf("f_tok", [128, C], F32)
        sb_y = P.sbuf("f_y", [128, C], F32)
        sb_z = P.sbuf("f_z", [128, C], F32)
        sb_ln = P.sbuf("f_ln", [128, C], BF16)
        sb_t2 = P.sbuf("f_t2", [128, 128], BF16)
        ps_mm = P.psum("f_ps", [128, 512], F32)
        prev = []
        prev_ev = None
        for tc in range(T // 512):
            t0c = tc * 512
            t_a = [P.dma("sync", sb_at[:, cc * 512:(cc + 1) * 512],
                         attT[cc * 128:(cc + 1) * 128, t0c:t0c + 512], waits=prev)
                   for cc in range(8)]
            ap_toks = []
            for cc in range(8):
                mm_last = None
                for c2 in range(8):
                    mm_last = P.op("tensor", lambda t, cc=cc, c2=c2: t.matmul(
                        ps_mm[:, :], sb_Wp[:, c2 * C + cc * 128:c2 * C + cc * 128 + 128],
                        sb_at[:, c2 * 512:(c2 + 1) * 512],
                        start=(c2 == 0), stop=(c2 == 7)), waits=wp_toks + t_a + [prev_ev])
                t_ev = P.op("vector", lambda v, cc=cc: v.tensor_copy(
                    sb_apT[:, cc * 512:(cc + 1) * 512], ps_mm[:, :]), waits=[mm_last])
                prev_ev = t_ev
                ap_toks.append(t_ev)
            for sub in range(4):
                tb = t0c + sub * 128
                tr_toks = []
                for cc in range(8):
                    tT = P.op("tensor", lambda t, cc=cc, sub=sub: t.transpose(
                        ps_trb[:, :], sb_apT[:, cc * 512 + sub * 128:cc * 512 + sub * 128 + 128],
                        idb[:]), waits=ap_toks + [P.note(ps_trb)])
                    tC = P.op("vector", lambda v, cc=cc: v.tensor_copy(
                        sb_tok[:, cc * 128:(cc + 1) * 128], ps_trb[:, :]), waits=[tT])
                    P.setn(ps_trb, tC)
                    tr_toks.append(tC)
                t_y = P.dma("sync", sb_y[:], y_d[tb:tb + 128, :], waits=prev)
                t_z = P.op("vector", lambda v: v.tensor_add(sb_z[:], sb_y[:], sb_tok[:]),
                           waits=tr_toks + [t_y])
                t_zs = P.dma("gpsimd_q", z_d[tb:tb + 128, :], sb_z[:], waits=[t_z])
                t_ln = emit_ln(P, sb_z[:], 128, sb_ln[:], lnscr, waits=[t_z])
                h2_toks = []
                for cc in range(8):
                    tT = P.op("tensor", lambda t, cc=cc: t.transpose(
                        ps_trb[:, :], sb_ln[:, cc * 128:(cc + 1) * 128], idb[:]),
                        waits=[t_ln, P.note(ps_trb)])
                    tC = P.op("vector", lambda v, cc=cc: v.tensor_scalar(
                        sb_t2[:, :], ps_trb[:, :], g2c[:, cc:cc + 1], b2c[:, cc:cc + 1],
                        ALU.mult, ALU.add), waits=[tT, P.note(sb_t2)])
                    P.setn(ps_trb, tC)
                    tD = P.dma("sync", h2T[cc * 128:(cc + 1) * 128, tb:tb + 128], sb_t2[:, :],
                               waits=[tC])
                    P.setn(sb_t2, tD)
                    h2_toks.append(tD)
                prev = [h2_toks[-1], t_zs, t_z]
    _phaseF()
    P.barrier()
    if upto == "F":
        return P.cur("vector")

    def _phaseG():
      with P.scope():
        sb_W1 = P.sbuf("g_W1", [128, 8 * 4 * C], BF16)
        sb_W2 = P.sbuf("g_W2", [128, 32 * C], BF16)
        w1_toks = [P.dma("gpsimd_q", sb_W1[:, cc * 4 * C:(cc + 1) * 4 * C],
                         w["W_ff1"][cc * 128:(cc + 1) * 128, :]) for cc in range(8)]
        w2_toks = [P.dma("gpsimd_q", sb_W2[:, kc * C:(kc + 1) * C],
                         w["W_ff2"][kc * 128:(kc + 1) * 128, :]) for kc in range(32)]
        sb_h2 = P.sbuf("g_h2", [128, 8 * 512], BF16)
        sb_g = P.sbuf("g_g", [128, 32 * 512], BF16)
        sb_ffT = P.sbuf("g_ffT", [128, 8 * 512], BF16)
        sb_tok = P.sbuf("g_tok", [128, C], BF16)
        sb_x = P.sbuf("g_x", [128, C], F32)
        sb_z = P.sbuf("g_z", [128, C], F32)
        ps_mm = [P.psum(f"g_ps{i}", [128, 512], F32) for i in range(2)]
        prev = []
        prev_g = {0: None, 1: None}
        prev_ev = {0: None, 1: None}
        prev_st = None
        for tc in range(T // 512):
            t0c = tc * 512
            t_h = [P.dma("sync", sb_h2[:, cc * 512:(cc + 1) * 512],
                         h2T[cc * 128:(cc + 1) * 128, t0c:t0c + 512], waits=prev)
                   for cc in range(8)]
            g_toks = []
            for fc in range(32):
                pb = fc % 2
                mm_last = None
                for cc in range(8):
                    mm_last = P.op("tensor", lambda t, fc=fc, cc=cc, pb=pb: t.matmul(
                        ps_mm[pb][:, :], sb_W1[:, cc * 4 * C + fc * 128:cc * 4 * C + fc * 128 + 128],
                        sb_h2[:, cc * 512:(cc + 1) * 512],
                        start=(cc == 0), stop=(cc == 7)),
                        waits=w1_toks + t_h + [prev_g[pb]])
                t_g = P.op("scalar", lambda sc, fc=fc, pb=pb: sc.activation(
                    sb_g[:, fc * 512:(fc + 1) * 512], ps_mm[pb][:, :], AF.Gelu),
                    waits=[mm_last])
                prev_g[pb] = t_g
                g_toks.append(t_g)
            ff_toks = []
            for cc in range(8):
                pb = cc % 2
                mm_last = None
                for kc in range(32):
                    mm_last = P.op("tensor", lambda t, cc=cc, kc=kc, pb=pb: t.matmul(
                        ps_mm[pb][:, :], sb_W2[:, kc * C + cc * 128:kc * C + cc * 128 + 128],
                        sb_g[:, kc * 512:(kc + 1) * 512],
                        start=(kc == 0), stop=(kc == 31)),
                        waits=w2_toks + g_toks + [prev_ev[pb], prev_g[pb]])
                t_ev = P.op("vector", lambda v, cc=cc, pb=pb: v.tensor_copy(
                    sb_ffT[:, cc * 512:(cc + 1) * 512], ps_mm[pb][:, :]), waits=[mm_last])
                prev_ev[pb] = t_ev
                ff_toks.append(t_ev)
            for sub in range(4):
                tb = t0c + sub * 128
                tr_toks = []
                for cc in range(8):
                    tT = P.op("tensor", lambda t, cc=cc, sub=sub: t.transpose(
                        ps_trb[:, :], sb_ffT[:, cc * 512 + sub * 128:cc * 512 + sub * 128 + 128],
                        idb[:]), waits=ff_toks + [P.note(ps_trb)])
                    tC = P.op("vector", lambda v, cc=cc: v.tensor_copy(
                        sb_tok[:, cc * 128:(cc + 1) * 128], ps_trb[:, :]),
                        waits=[tT, prev_st])
                    P.setn(ps_trb, tC)
                    tr_toks.append(tC)
                t_x = P.dma("sync", sb_x[:], xin[tb:tb + 128, :], waits=[prev_st])
                t_zl = P.dma("sync", sb_z[:], z_d[tb:tb + 128, :], waits=[prev_st])
                t_o1 = P.op("vector", lambda v: v.tensor_add(sb_x[:], sb_x[:], sb_z[:]),
                            waits=tr_toks + [t_x, t_zl])
                t_o2 = P.op("vector", lambda v: v.tensor_add(sb_x[:], sb_x[:], sb_tok[:]),
                            waits=[t_o1])
                prev_st = P.dma("gpsimd_q", xout[tb:tb + 128, :], sb_x[:], waits=[t_o2])
            prev = [prev_st] + ff_toks
      return prev_st
    tok = _phaseG()
    return tok


def build_l2attn_launch(upto="G"):
    nc = bass.Bass("TRN2", target_bir_lowering=False)
    xin = nc.dram_tensor("x", [T, C], F32, kind="ExternalInput").ap()
    names = {"W_qkvf": [4096, 1024], "W_conv": [64, 64, 4], "W_proj": [1024, 1024],
             "W_ff1": [1024, 4096], "W_ff2": [4096, 1024],
             "ln1_g": [1024], "ln1_b": [1024], "ln2_g": [1024], "ln2_b": [1024]}
    w = {k: nc.dram_tensor(k, sh, F32, kind="ExternalInput").ap() for k, sh in names.items()}
    xout = nc.dram_tensor("xo", [T, C], F32, kind="ExternalOutput").ap()
    P = Prog(nc)
    tok = stage_l2attn(P, xin, xout, w, upto=upto)
    P.finalize(final_waits=[tok])
    return nc



def stage_l3attn(P, xin, xout, w, rg=None):
    H, D = 16, 64
    idf, idb, t_id = make_identity(P)
    P.constf(1e-5)
    ps_trb = P.psum("l3a_trb", [128, 128], BF16)
    ps_trf = P.psum("l3a_trf", [128, 128], F32)
    if rg is None:
        rg = [[0, 1, 2, 3, 4, 5, 6, 7]]

    yT_d = P.dram("l3a_yT", [C, T], BF16)
    diagT_d = P.dram("l3a_diagT", [T, H], F32)
    ag_in = P.dram("l3a_agin", [T, H], F32)
    ag_out = P.dram("l3a_agout", [8 * T, H], F32)
    vsum_row_d = P.dram("l3a_vsum", [1, C], F32)

    lnscr = {
        "sq": P.sbuf("l3aln_sq", [128, C], F32),
        "m": P.sbuf("l3aln_m", [128, 1], F32),
        "ss": P.sbuf("l3aln_ss", [128, 1], F32),
        "var": P.sbuf("l3aln_var", [128, 1], F32),
        "rstd": P.sbuf("l3aln_rstd", [128, 1], F32),
        "msq": P.sbuf("l3aln_msq", [128, 1], F32),
    }

    pdiv = P.sbuf("l3a_pdiv", [128, 1], F32)
    pio = P.sbuf("l3a_pio", [128, 1], I32)
    hgrid = P.sbuf("l3a_hg", [128, 16], I32)
    E16 = P.sbuf("l3a_E16", [128, 8 * 16], BF16)
    t0 = P.op("gpsimd", lambda g: g.iota(pio[:], pattern=[[1, 1]], base=0, channel_multiplier=1))
    t1 = P.op("vector", lambda v: v.tensor_single_scalar(pdiv[:], pio[:], 63.5, ALU.is_gt), waits=[t0])
    t2 = P.op("gpsimd", lambda g: g.iota(hgrid[:], pattern=[[1, 16]], base=0, channel_multiplier=0))
    e_tok = None
    for cc in range(8):
        e_tok = P.op("vector", lambda v, cc=cc: v.tensor_scalar(
            E16[:, cc * 16:(cc + 1) * 16], hgrid[:], float(2 * cc), pdiv[:],
            ALU.subtract, ALU.is_equal), waits=[t1, t2])

    sb_w1 = P.sbuf("l3a_w1", [64, 3 * 64], BF16)
    t_wl = [P.dma("gpsimd_q", sb_w1[:, i * 64:(i + 1) * 64], w[k]) for i, k in
            enumerate(["Wq", "Wk", "Wv"])]
    sb_blk = P.sbuf("l3a_blk", [128, 3 * 128], BF16)
    t_bz = P.op("vector", lambda v: v.memset(sb_blk[:, :], 0.0))
    blk_toks = []
    for i in range(3):
        tT = P.op("tensor", lambda t, i=i: t.transpose(
            ps_trb[0:64, 0:64], sb_w1[:, i * 64:(i + 1) * 64], idb[0:64, 0:64]),
            waits=t_wl + [P.note(ps_trb), t_id])
        tc1 = P.op("vector", lambda v, i=i: v.tensor_copy(
            sb_blk[0:64, i * 128:i * 128 + 64], ps_trb[0:64, 0:64]), waits=[tT, t_bz])
        tc2 = P.op("vector", lambda v, i=i: v.tensor_copy(
            sb_blk[64:128, i * 128 + 64:i * 128 + 128], ps_trb[0:64, 0:64]), waits=[tT, t_bz])
        P.setn(ps_trb, tc2)
        blk_toks += [tc1, tc2]

    with P.scope():
        sb_x = P.sbuf("l3a_x", [128, C], F32)
        sb_ln = P.sbuf("l3a_ln", [128, C], BF16)
        sb_yT = P.sbuf("l3a_yT", [128, 8 * 512], BF16)
        sb_kT = P.sbuf("l3a_kT", [128, 512], BF16)
        sb_diag = P.sbuf("l3a_diag", [16, 512], F32)
        sb_prod = P.sbuf("l3a_prod", [128, 512], BF16)
        sb_vacc = P.sbuf("l3a_vacc", [128, 8], F32)
        sb_vred = P.sbuf("l3a_vred", [128, 1], F32)
        sb_diagT = P.sbuf("l3a_dT", [128, 16], F32)
        ps_q = P.psum("l3a_psq", [128, 512], F32)
        ps_k = P.psum("l3a_psk", [128, 512], F32)
        ps_d = P.psum("l3a_psd", [16, 512], F32)
        t_vz = P.op("vector", lambda v: v.memset(sb_vacc[:, :], 0.0))
        prev_ln = []
        prev_dT = None
        prev_dt2 = []
        prev_vred = t_vz
        for tc in range(T // 512):
            t0c = tc * 512
            ln_toks = []
            for sub in range(4):
                tb = t0c + sub * 128
                t_l = P.dma("sync", sb_x[:], xin[tb:tb + 128, :], waits=prev_ln)
                t_ln = emit_ln(P, sb_x[:], 128, sb_ln[:], lnscr, waits=[t_l])
                for cc in range(8):
                    tT = P.op("tensor", lambda t, cc=cc: t.transpose(
                        ps_trb[:, :], sb_ln[:, cc * 128:(cc + 1) * 128], idb[:]),
                        waits=[t_ln, P.note(ps_trb)])
                    tC = P.op("vector", lambda v, cc=cc, sub=sub: v.tensor_copy(
                        sb_yT[:, cc * 512 + sub * 128:cc * 512 + sub * 128 + 128],
                        ps_trb[:, :]), waits=[tT])
                    P.setn(ps_trb, tC)
                    ln_toks.append(tC)
                prev_ln = [ln_toks[-1]]
            d_toks = []
            for cc in range(8):
                t_q = P.op("tensor", lambda t, cc=cc: t.matmul(
                    ps_q[:, :], sb_blk[:, 0:128], sb_yT[:, cc * 512:(cc + 1) * 512],
                    start=True, stop=True), waits=blk_toks + ln_toks + d_toks[-1:] + [e_tok])
                t_k = P.op("tensor", lambda t, cc=cc: t.matmul(
                    ps_k[:, :], sb_blk[:, 128:256], sb_yT[:, cc * 512:(cc + 1) * 512],
                    start=True, stop=True), waits=blk_toks + ln_toks)
                t_kc = P.op("vector", lambda v: v.tensor_copy(sb_kT[:, :], ps_k[:, :]),
                            waits=[t_k, P.note(sb_kT)])
                t_pr = P.op("vector", lambda v: v.tensor_tensor(sb_prod[:, :], ps_q[:, :],
                                                                sb_kT[:, :], ALU.mult),
                            waits=[t_q, t_kc])
                P.setn(sb_kT, t_pr)
                t_dm = P.op("tensor", lambda t, cc=cc: t.matmul(
                    ps_d[:, :], E16[:, cc * 16:(cc + 1) * 16], sb_prod[:, :],
                    start=True, stop=True), waits=[t_pr])
                if cc == 0:
                    t_da = P.op("vector", lambda v: v.tensor_copy(sb_diag[:, :], ps_d[:, :]),
                                waits=[t_dm, prev_dT])
                else:
                    t_da = P.op("vector", lambda v: v.tensor_add(sb_diag[:, :], sb_diag[:, :],
                                                                 ps_d[:, :]), waits=[t_dm])
                d_toks.append(t_da)
                t_v = P.op("tensor", lambda t, cc=cc: t.matmul(
                    ps_k[:, :], sb_blk[:, 256:384], sb_yT[:, cc * 512:(cc + 1) * 512],
                    start=True, stop=True), waits=blk_toks + ln_toks + [t_pr])
                t_vr = P.op("vector", lambda v: v.tensor_reduce(sb_vred[:, :], ps_k[:, :],
                                                                AX.X, ALU.add),
                            waits=[t_v, prev_vred])
                prev_vred = P.op("vector", lambda v, cc=cc: v.tensor_add(
                    sb_vacc[:, cc:cc + 1], sb_vacc[:, cc:cc + 1], sb_vred[:, :]),
                    waits=[t_vr])
                d_toks.append(prev_vred)
            t_ds = P.op("vector", lambda v: v.tensor_single_scalar(
                sb_diag[:, :], sb_diag[:, :], 1.0 / 32.0, ALU.mult), waits=d_toks)
            dt_toks = []
            for sub in range(4):
                tT = P.op("tensor", lambda t, sub=sub: t.transpose(
                    ps_trf[0:128, 0:16], sb_diag[:, sub * 128:(sub + 1) * 128], idf[0:16, 0:16]),
                    waits=[t_ds, P.note(ps_trf)])
                tC = P.op("vector", lambda v: v.tensor_copy(sb_diagT[:, :], ps_trf[0:128, 0:16]),
                          waits=[tT] + (dt_toks[-2:] if dt_toks else list(prev_dt2)))
                P.setn(ps_trf, tC)
                tD = P.dma("sync", diagT_d[t0c + sub * 128:t0c + (sub + 1) * 128, :],
                           sb_diagT[:, :], waits=[tC])
                tD2 = P.dma("gpsimd_q", ag_in[t0c + sub * 128:t0c + (sub + 1) * 128, :],
                            sb_diagT[:, :], waits=[tC])
                dt_toks += [tD, tD2]
            prev_dT = dt_toks[-1]
            prev_dt2 = dt_toks[-2:]
            prev_ln = dt_toks[-2:]
        tT = P.op("tensor", lambda t: t.transpose(ps_trf[0:8, 0:128], sb_vacc[:, :],
                                                  idf[:]), waits=[prev_vred, P.note(ps_trf)])
        tC = P.op("vector", lambda v: v.tensor_copy(sb_diag[0:8, 0:128], ps_trf[0:8, 0:128]),
                  waits=[tT, prev_dT])
        P.setn(ps_trf, tC)
        tV = P.dma("sync", vsum_row_d.rearrange("a (b c) -> (a b) c", b=8), sb_diag[0:8, 0:128],
                   waits=[tC])
    t_ag = P.collective("AllGather", ALU.bypass, rg, [ag_in], [ag_out], waits=[P.cur("vector")])
    P.barrier()

    with P.scope():
        sb_vrep = P.sbuf("l3b_vrep", [128, C], BF16)
        sb_vrow = P.sbuf("l3b_vrow", [1, C], F32)
        ps_r = P.psum("l3b_psr", [128, 512], F32)
        t_vl = P.dma("sync", sb_vrow[:, :], vsum_row_d[:, :])
        ones1 = P.sbuf("l3b_ones", [1, 128], BF16)
        t_o1 = P.op("vector", lambda v: v.memset(ones1[:, :], 1.0))
        sb_vrowb = P.sbuf("l3b_vrowb", [1, C], BF16)
        t_vb = P.op("vector", lambda v: v.tensor_copy(sb_vrowb[:, :], sb_vrow[:, :]),
                    waits=[t_vl])
        rep_toks = []
        for half in range(2):
            t_r = P.op("tensor", lambda t, half=half: t.matmul(
                ps_r[:, :], ones1[:, :], sb_vrowb[0:1, half * 512:(half + 1) * 512],
                start=True, stop=True), waits=[t_o1, t_vb] + rep_toks[-1:])
            t_c = P.op("vector", lambda v, half=half: v.tensor_copy(
                sb_vrep[:, half * 512:(half + 1) * 512], ps_r[:, :]), waits=[t_r])
            rep_toks.append(t_c)
        sb_ag = P.sbuf("l3b_ag", [128, 8 * 16], F32)
        sb_m = P.sbuf("l3b_m", [128, 16], F32)
        sb_den = P.sbuf("l3b_den", [128, 16], F32)
        sb_down = P.sbuf("l3b_down", [128, 16], F32)
        sb_p = P.sbuf("l3b_p", [128, 16], F32)
        sb_x = P.sbuf("l3b_x", [128, C], F32)
        sb_o = P.sbuf("l3b_o", [128, C], F32)
        prev_st = None
        prev_exp = None
        for tt in range(T // 128):
            tb = tt * 128
            t_g = [P.dma("sync", sb_ag[:, r * 16:(r + 1) * 16],
                         ag_out[r * T + tb:r * T + tb + 128, :], waits=[prev_st, t_ag])
                   for r in range(8)]
            agv = sb_ag[:, :].rearrange("p (r h) -> p h r", h=16)
            t_m = P.op("vector", lambda v: v.tensor_reduce(sb_m[:, :], agv, AX.X, ALU.max),
                       waits=t_g + [prev_exp])
            mb = sb_m[:, :].rearrange("p (a h) -> p h a", h=16).broadcast_to([128, 16, 8])
            t_s = P.op("vector", lambda v: v.tensor_tensor(agv, agv, mb, ALU.subtract),
                       waits=[t_m])
            t_e = P.op("scalar", lambda s: s.activation(sb_ag[:, :], sb_ag[:, :], AF.Exp),
                       waits=[t_s])
            t_d = P.op("vector", lambda v: v.tensor_reduce(sb_den[:, :], agv, AX.X, ALU.add),
                       waits=[t_e])
            prev_exp = t_d
            t_ol = P.dma("sync", sb_down[:, :], diagT_d[tb:tb + 128, :], waits=[prev_st])
            t_su = P.op("vector", lambda v: v.tensor_tensor(sb_down[:, :], sb_down[:, :],
                                                            sb_m[:, :], ALU.subtract),
                        waits=[t_ol, t_m])
            t_ex = P.op("scalar", lambda s: s.activation(sb_down[:, :], sb_down[:, :], AF.Exp),
                        waits=[t_su])
            t_rc = P.op("vector", lambda v: v.reciprocal(sb_p[:, :], sb_den[:, :]), waits=[t_d])
            t_pp = P.op("vector", lambda v: v.tensor_tensor(sb_p[:, :], sb_p[:, :],
                                                            sb_down[:, :], ALU.mult),
                        waits=[t_rc, t_ex])
            pext = sb_p[:, :].rearrange("p (h a) -> p h a", h=16).broadcast_to([128, 16, 64])
            vv = sb_vrep[:, :].rearrange("p (h d) -> p h d", h=16)
            ov = sb_o[:, :].rearrange("p (h d) -> p h d", h=16)
            t_x = P.dma("sync", sb_x[:], xin[tb:tb + 128, :], waits=[prev_st])
            t_mu = P.op("vector", lambda v: v.tensor_tensor(ov, pext, vv, ALU.mult),
                        waits=[t_pp] + rep_toks + [prev_st])
            t_ad = P.op("vector", lambda v: v.tensor_add(sb_o[:], sb_o[:], sb_x[:]),
                        waits=[t_mu, t_x])
            prev_st = P.dma("gpsimd_q", xout[tb:tb + 128, :], sb_o[:], waits=[t_ad])
    return prev_st


def build_l3attn_launch():
    nc = bass.Bass("TRN2", target_bir_lowering=False)
    xin = nc.dram_tensor("x", [T, C], F32, kind="ExternalInput").ap()
    w = {k: nc.dram_tensor(k, [64, 64], F32, kind="ExternalInput").ap()
         for k in ["Wq", "Wk", "Wv"]}
    xout = nc.dram_tensor("xo", [T, C], F32, kind="ExternalOutput").ap()
    P = Prog(nc)
    tok = stage_l3attn(P, xin, xout, w)
    P.finalize(final_waits=[tok])
    return nc



def stage_l2ffn(P, xin, xout, w, coef=0.5):
    E, G, HID = 8, 16, 2048
    TILE = 128
    NT = T // 128
    PAD = T + E * TILE
    NPT = PAD // TILE
    idf, idb, t_id = make_identity(P)
    P.constf(1e-5)
    ps_trb = P.psum("l2f_trb", [128, 128], BF16)

    y_d = P.dram("l2f_y", [T, C], BF16)
    slot_d = P.dram("l2f_slot", [T, 1], U32)
    sorted_d = P.dram("l2f_sorted", [PAD, C], BF16)
    meta_d = P.dram("l2f_meta", [PAD, 2], F32)
    moe_d = P.dram("l2f_moe", [T + TILE, C], F32)
    w1b_d = P.dram("l2f_w1b", [E * 128, 8 * HID], BF16)
    w2b_d = P.dram("l2f_w2b", [E * 128, 16 * C], BF16)
    widx_d = P.dram("l2f_widx", [128, NPT], U32)
    metastage = P.dram("l2f_metastage", [T, 2], F32)
    b1col_d = P.dram("l2f_b1col", [E, 128, 16], F32)
    b2rep_d = P.dram("l2f_b2rep", [E, 128, C], F32)

    lnscr = {
        "sq": P.sbuf("l2fln_sq", [128, C], F32),
        "m": P.sbuf("l2fln_m", [128, 1], F32),
        "ss": P.sbuf("l2fln_ss", [128, 1], F32),
        "var": P.sbuf("l2fln_var", [128, 1], F32),
        "rstd": P.sbuf("l2fln_rstd", [128, 1], F32),
        "msq": P.sbuf("l2fln_msq", [128, 1], F32),
    }

    def _phW():
      with P.scope():
        sb_b1r = P.sbuf("l2fw_b1r", [1, HID], F32)
        sb_b1c = P.sbuf("l2fw_b1c", [128, 16], F32)
        sb_b2r = P.sbuf("l2fw_b2r", [1, C], BF16)
        sb_b2rep = P.sbuf("l2fw_b2rep", [128, C], F32)
        ones1 = P.sbuf("l2fw_ones1", [1, 128], BF16)
        ps_c = P.psum("l2fw_psc", [128, 512], F32)
        t_on = P.op("vector", lambda v: v.memset(ones1[:, :], 1.0))
        for e in range(E):
            t_b1 = P.dma("sync", sb_b1r[:, :], w["b1"][e:e + 1, :], waits=[P.note(sb_b1r)])
            for fc in range(16):
                tT = P.op("tensor", lambda t, fc=fc: t.transpose(
                    ps_c[0:128, 0:1], sb_b1r[0:1, fc * 128:(fc + 1) * 128], idf[0:1, 0:1]),
                    waits=[t_b1, t_id, P.note(ps_c)])
                P.setn(sb_b1r, tT)
                tC = P.op("vector", lambda v, fc=fc: v.tensor_copy(
                    sb_b1c[:, fc:fc + 1], ps_c[0:128, 0:1]), waits=[tT, P.note(sb_b1c)])
                P.setn(ps_c, tC)
                P.setn(sb_b1c, tC)
            tD = P.dma("sync", b1col_d[e, :, :], sb_b1c[:, :], waits=[tC])
            P.setn(sb_b1c, tD)
            t_b2 = P.dma("gpsimd_q", sb_b2r[:, :], w["b2"][e:e + 1, :], waits=[P.note(sb_b2r)])
            for half in range(2):
                tR = P.op("tensor", lambda t, half=half: t.matmul(
                    ps_c[:, :], ones1[:, :], sb_b2r[0:1, half * 512:(half + 1) * 512],
                    start=True, stop=True), waits=[t_b2, t_on, P.note(ps_c)])
                P.setn(sb_b2r, tR)
                tC2 = P.op("vector", lambda v, half=half: v.tensor_copy(
                    sb_b2rep[:, half * 512:(half + 1) * 512], ps_c[:, :]),
                    waits=[tR, P.note(sb_b2rep)])
                P.setn(ps_c, tC2)
                P.setn(sb_b2rep, tC2)
            tD2 = P.dma("sync", b2rep_d[e, :, :], sb_b2rep[:, :], waits=[tC2])
            P.setn(sb_b2rep, tD2)
        sb_w = P.sbuf("l2fw", [128, HID], BF16)
        last = []
        for e in range(E):
            for cc in range(8):
                t_l = P.dma("gpsimd_q", sb_w[:, :], w["W1"][e, cc * 128:(cc + 1) * 128, :],
                            waits=last)
                last = [P.dma("sync", w1b_d[e * 128:(e + 1) * 128, cc * HID:(cc + 1) * HID],
                              sb_w[:, :], waits=[t_l])]
            for fc in range(16):
                t_l = P.dma("gpsimd_q", sb_w[:, 0:C], w["W2"][e, fc * 128:(fc + 1) * 128, :],
                            waits=last)
                last = [P.dma("sync", w2b_d[e * 128:(e + 1) * 128, fc * C:(fc + 1) * C],
                              sb_w[:, 0:C], waits=[t_l])]
    _phW()

    def _phR():
      with P.scope():
        sb_wg = P.sbuf("l2f_wg", [8, 16], F32)
        sb_wgn = P.sbuf("l2f_wgn", [8, 16], BF16)
        sb_n = P.sbuf("l2f_n", [8, 1], F32)
        sb_sq8 = P.sbuf("l2f_sq8", [8, 16], F32)
        sb_wgnT = P.sbuf("l2f_wgnT", [16, 8], BF16)
        t_w = P.dma("sync", sb_wg[:, :], w["wg"])
        t_q = P.op("scalar", lambda s: s.activation(sb_sq8[:, :], sb_wg[:, :], AF.Square,
                                                    accum_out=sb_n[:, :]), waits=[t_w])
        t_n = P.op("scalar", lambda s: s.activation(sb_n[:, :], sb_n[:, :], AF.Sqrt), waits=[t_q])
        t_n2 = P.op("vector", lambda v: v.tensor_single_scalar(sb_n[:, :], sb_n[:, :], 1e-4,
                                                               ALU.max), waits=[t_n])
        t_r = P.op("vector", lambda v: v.reciprocal(sb_n[:, :], sb_n[:, :]), waits=[t_n2])
        t_wn = P.op("vector", lambda v: v.tensor_scalar(sb_wgn[:, :], sb_wg[:, :], sb_n[:, :],
                                                        None, ALU.mult), waits=[t_r])
        tT0 = P.op("tensor", lambda t: t.transpose(ps_trb[0:16, 0:8], sb_wgn[:, :],
                                                   idb[0:8, 0:8]),
                   waits=[t_wn, t_id, P.note(ps_trb)])
        tC0 = P.op("vector", lambda v: v.tensor_copy(sb_wgnT[:, :], ps_trb[0:16, 0:8]),
                   waits=[tT0])
        P.setn(ps_trb, tC0)
        sb_wred = P.sbuf("l2f_wred", [128, 8 * G], BF16)
        wred_toks = [P.dma("gpsimd_q", sb_wred[:, cc * G:(cc + 1) * G],
                           w["w_red"][cc * 128:(cc + 1) * 128, :]) for cc in range(8)]
        hg8i = P.sbuf("l2f_hg8i", [128, 8], I32)
        hg8 = P.sbuf("l2f_hg8", [128, 8], F32)
        t_h1 = P.op("gpsimd", lambda g: g.iota(hg8i[:, :], pattern=[[1, 8]], base=0,
                                               channel_multiplier=0))
        t_h2 = P.op("vector", lambda v: v.tensor_copy(hg8[:, :], hg8i[:, :]), waits=[t_h1])

        sb_x = P.sbuf("l2fr_x", [128, C], F32)
        sb_ln = P.sbuf("l2fr_ln", [128, C], BF16)
        sb_lnT = P.sbuf("l2fr_lnT", [128, 8 * 128], BF16)
        sb_xrT = P.sbuf("l2fr_xrT", [16, 128], BF16)
        sb_lg = P.sbuf("l2fr_lg", [128, 8], F32)
        sb_mx = P.sbuf("l2fr_mx", [128, 8], F32)
        sb_sum = P.sbuf("l2fr_sum", [128, 1], F32)
        sb_idx = P.sbuf("l2fr_idx", [128, 8], U32)
        sb_idxf = P.sbuf("l2fr_idxf", [128, 1], F32)
        sb_oh = P.sbuf("l2fr_oh", [128, 8], BF16)
        sb_ohT = P.sbuf("l2fr_ohT", [8, T], F32)
        sb_scan = P.sbuf("l2fr_scan", [8, T], F32)
        sb_meta = P.sbuf("l2fr_meta", [128, 2], F32)
        sb_tokid = P.sbuf("l2fr_tokid", [128, 1], I32)
        ps_xr = P.psum("l2fr_psxr", [16, 128], F32)
        ps_lg = P.psum("l2fr_pslg", [128, 8], F32)
        prev = [tC0] + wred_toks
        oh_toks = []
        prev_meta = None
        for tt in range(NT):
            tb = tt * TILE
            t_l = P.dma("sync", sb_x[:], xin[tb:tb + 128, :], waits=prev)
            t_ln = emit_ln(P, sb_x[:], 128, sb_ln[:], lnscr, waits=[t_l])
            t_ys = P.dma("gpsimd_q", y_d[tb:tb + 128, :], sb_ln[:], waits=[t_ln])
            tr_toks = []
            for cc in range(8):
                tT2 = P.op("tensor", lambda t, cc=cc: t.transpose(
                    ps_trb[:, :], sb_ln[:, cc * 128:(cc + 1) * 128], idb[:]),
                    waits=[t_ln, P.note(ps_trb)])
                tC2 = P.op("vector", lambda v, cc=cc: v.tensor_copy(
                    sb_lnT[:, cc * 128:(cc + 1) * 128], ps_trb[:, :]), waits=[tT2])
                P.setn(ps_trb, tC2)
                tr_toks.append(tC2)
            mm = None
            for cc in range(8):
                mm = P.op("tensor", lambda t, cc=cc: t.matmul(
                    ps_xr[:, :], sb_wred[:, cc * G:(cc + 1) * G],
                    sb_lnT[:, cc * 128:(cc + 1) * 128],
                    start=(cc == 0), stop=(cc == 7)), waits=tr_toks + wred_toks + oh_toks[-1:])
            t_xr = P.op("vector", lambda v: v.tensor_copy(sb_xrT[:, :], ps_xr[:, :]), waits=[mm])
            t_lgm = P.op("tensor", lambda t: t.matmul(ps_lg[:, :], sb_xrT[:, :], sb_wgnT[:, :],
                                                      start=True, stop=True),
                         waits=[t_xr, tC0])
            t_lg = P.op("vector", lambda v: v.tensor_copy(sb_lg[:, :], ps_lg[:, :]),
                        waits=[t_lgm])
            t_m8 = P.op("vector", lambda v: v.max(sb_mx[:, :], sb_lg[:, :]), waits=[t_lg])
            t_ix = P.op("vector", lambda v: v.max_index(sb_idx[:, :], sb_mx[:, :], sb_lg[:, :]),
                        waits=[t_m8])
            t_sb = P.op("vector", lambda v: v.tensor_scalar(sb_lg[:, :], sb_lg[:, :],
                                                            sb_mx[:, 0:1], None, ALU.subtract),
                        waits=[t_ix])
            t_ex = P.op("scalar", lambda s: s.activation(sb_lg[:, :], sb_lg[:, :], AF.Exp,
                                                         accum_out=sb_sum[:, :]), waits=[t_sb])
            t_if = P.op("vector", lambda v: v.tensor_copy(sb_idxf[:, :], sb_idx[:, 0:1]),
                        waits=[t_ix])
            t_oh = P.op("vector", lambda v: v.tensor_scalar(
                sb_oh[:, :], hg8[:, :], 0.0, sb_idxf[:, 0:1], ALU.add, ALU.is_equal),
                waits=[t_if, t_h2])
            tT3 = P.op("tensor", lambda t: t.transpose(ps_trb[0:8, 0:128], sb_oh[:, :],
                                                       idb[:]), waits=[t_oh, P.note(ps_trb)])
            tC3 = P.op("vector", lambda v, tb=tb: v.tensor_copy(
                sb_ohT[:, tb:tb + 128], ps_trb[0:8, 0:128]), waits=[tT3])
            P.setn(ps_trb, tC3)
            oh_toks.append(tC3)
            t_ti = P.op("gpsimd", lambda g, tb=tb: g.iota(sb_tokid[:, :], pattern=[[1, 1]],
                                                          base=tb, channel_multiplier=1),
                        waits=[prev_meta])
            t_mc = P.op("vector", lambda v: v.tensor_copy(sb_meta[:, 0:1], sb_tokid[:, :]),
                        waits=[t_ti, prev_meta])
            t_sc = P.op("vector", lambda v: v.reciprocal(sb_meta[:, 1:2], sb_sum[:, :]),
                        waits=[t_ex, t_mc])
            t_md = P.dma("sync", metastage[tb:tb + 128, :], sb_meta[:, :], waits=[t_sc])
            prev_meta = t_md
            prev = [t_ys, t_md]
        sb_ones = P.sbuf("l2fr_ones", [8, T], F32)
        sb_slotrow = P.sbuf("l2fr_slotrow", [1, T], F32)
        sb_cnt = P.sbuf("l2fr_cnt", [8, 1], F32)
        sb_cpad = P.sbuf("l2fr_cpad", [8, 1], F32)
        sb_tmp1 = P.sbuf("l2fr_t1", [8, 1], F32)
        sb_tmp2 = P.sbuf("l2fr_t2", [8, 1], F32)
        sb_tmpi = P.sbuf("l2fr_ti", [8, 1], I32)
        sb_base = P.sbuf("l2fr_base", [8, 1], F32)
        sb_tg = P.sbuf("l2fr_tg", [8, NPT], F32)
        sb_tgi = P.sbuf("l2fr_tgi", [8, NPT], I32)
        sb_texp = P.sbuf("l2fr_texp", [1, NPT], F32)
        L8 = P.sbuf("l2fr_L8", [8, 8], F32)
        L8i = P.sbuf("l2fr_L8i", [8, 8], I32)
        ones8 = P.sbuf("l2fr_ones8", [8, 1], F32)
        ps_s8 = P.psum("l2fr_ps8", [8, 512], F32)
        ps_b8 = P.psum("l2fr_psb8", [8, 8], F32)
        t_o1 = P.op("vector", lambda v: v.memset(sb_ones[:, :], 1.0))
        t_sc2 = P.op("vector", lambda v: v.tensor_tensor_scan(
            sb_scan[:, :], sb_ones[:, :], sb_ohT[:, :], 0.0, ALU.mult, ALU.add),
            waits=[t_o1] + oh_toks)
        t_ct = P.op("vector", lambda v: v.tensor_copy(sb_cnt[:, :], sb_scan[:, T - 1:T]),
                    waits=[t_sc2])
        t_ex2 = P.op("vector", lambda v: v.tensor_tensor(sb_scan[:, :], sb_scan[:, :],
                                                         sb_ohT[:, :], ALU.subtract),
                     waits=[t_ct])
        t_c1 = P.op("vector", lambda v: v.tensor_scalar(sb_tmp1[:, :], sb_cnt[:, :], 127.0,
                                                        1.0 / 128.0, ALU.add, ALU.mult),
                    waits=[t_ct])
        t_c2 = P.op("vector", lambda v: v.tensor_copy(sb_tmpi[:, :], sb_tmp1[:, :]), waits=[t_c1])
        t_c3 = P.op("vector", lambda v: v.tensor_copy(sb_cpad[:, :], sb_tmpi[:, :]), waits=[t_c2])
        t_c6 = P.op("vector", lambda v: v.tensor_tensor(sb_tmp2[:, :], sb_cpad[:, :],
                                                        sb_tmp1[:, :], ALU.is_gt), waits=[t_c3])
        t_c7 = P.op("vector", lambda v: v.tensor_tensor(sb_cpad[:, :], sb_cpad[:, :],
                                                        sb_tmp2[:, :], ALU.subtract),
                    waits=[t_c6])
        t_c8 = P.op("vector", lambda v: v.tensor_single_scalar(sb_cpad[:, :], sb_cpad[:, :],
                                                               128.0, ALU.mult), waits=[t_c7])
        t_L1 = P.op("gpsimd", lambda g: g.iota(L8i[:, :], pattern=[[1, 8]], base=0,
                                               channel_multiplier=-1))
        t_L2 = P.op("vector", lambda v: v.tensor_single_scalar(L8[:, :], L8i[:, :], 0.5,
                                                               ALU.is_gt), waits=[t_L1])
        t_ob = P.op("vector", lambda v: v.memset(ones8[:, :], 1.0))
        t_bm = P.op("tensor", lambda t: t.matmul(ps_b8[0:8, 0:1], L8[:, :], sb_cpad[:, :],
                                                 start=True, stop=True), waits=[t_L2, t_c8])
        t_ba = P.op("vector", lambda v: v.tensor_copy(sb_base[:, :], ps_b8[0:8, 0:1]),
                    waits=[t_bm])
        t_kg = P.op("gpsimd", lambda g: g.iota(sb_tgi[:, :], pattern=[[1, NPT]], base=0,
                                               channel_multiplier=0))
        t_kf = P.op("vector", lambda v: v.tensor_copy(sb_tg[:, :], sb_tgi[:, :]), waits=[t_kg])
        t_ks = P.op("vector", lambda v: v.tensor_scalar(sb_tg[:, :], sb_tg[:, :], 128.0,
                                                        sb_base[:, :], ALU.mult, ALU.is_ge),
                    waits=[t_kf, t_ba])
        t_km = P.op("tensor", lambda t: t.matmul(ps_s8[0:1, 0:NPT], ones8[:, :], sb_tg[:, :],
                                                 start=True, stop=True), waits=[t_ob, t_ks])
        t_ke = P.op("vector", lambda v: v.tensor_single_scalar(sb_texp[0:1, :],
                                                               ps_s8[0:1, 0:NPT], 1.0,
                                                               ALU.subtract), waits=[t_km])
        sb_texpb = P.sbuf("l2fr_texpb", [1, NPT], BF16)
        sb_wix = P.sbuf("l2fr_wix", [128, NPT], F32)
        sb_wixu = P.sbuf("l2fr_wixu", [128, NPT], U32)
        sb_pcol = P.sbuf("l2fr_pcol", [128, 1], I32)
        sb_pcolf = P.sbuf("l2fr_pcolf", [128, 1], F32)
        ones128 = P.sbuf("l2fr_o128", [1, 128], BF16)
        ps_wix = P.psum("l2fr_pswix", [128, NPT], F32)
        t_tb = P.op("vector", lambda v: v.tensor_copy(sb_texpb[:, :], sb_texp[0:1, :]),
                    waits=[t_ke])
        t_om = P.op("vector", lambda v: v.memset(ones128[:, :], 1.0))
        t_rep = P.op("tensor", lambda t: t.matmul(ps_wix[:, :], ones128[:, :], sb_texpb[:, :],
                                                  start=True, stop=True), waits=[t_tb, t_om])
        t_pc = P.op("gpsimd", lambda g: g.iota(sb_pcol[:, :], pattern=[[1, 1]], base=0,
                                               channel_multiplier=1))
        t_pf = P.op("vector", lambda v: v.tensor_copy(sb_pcolf[:, :], sb_pcol[:, :]),
                    waits=[t_pc])
        t_wx = P.op("vector", lambda v: v.tensor_scalar(sb_wix[:, :], ps_wix[:, :], 128.0,
                                                        sb_pcolf[:, :], ALU.mult, ALU.add),
                    waits=[t_rep, t_pf])
        t_wu = P.op("vector", lambda v: v.tensor_copy(sb_wixu[:, :], sb_wix[:, :]),
                    waits=[t_wx])
        t_kd = P.dma("sync", widx_d[:, :], sb_wixu[:, :], waits=[t_wu])
        t_sm1 = P.op("vector", lambda v: v.tensor_scalar(sb_scan[:, :], sb_scan[:, :],
                                                         sb_base[:, :], None, ALU.add),
                     waits=[t_ex2, t_ba])
        t_sm2 = P.op("vector", lambda v: v.tensor_tensor(sb_scan[:, :], sb_scan[:, :],
                                                         sb_ohT[:, :], ALU.mult), waits=[t_sm1])
        sr_toks = []
        for ch in range(T // 512):
            t_mm = P.op("tensor", lambda t, ch=ch: t.matmul(
                ps_s8[0:1, 0:512], ones8[:, :], sb_scan[:, ch * 512:(ch + 1) * 512],
                start=True, stop=True), waits=[t_sm2, t_ob] + sr_toks[-1:])
            t_cp = P.op("vector", lambda v, ch=ch: v.tensor_copy(
                sb_slotrow[:, ch * 512:(ch + 1) * 512], ps_s8[0:1, 0:512]), waits=[t_mm])
            sr_toks.append(t_cp)
        sb_sl = P.sbuf("l2fr_sl", [128, 1], F32)
        sb_slu = P.sbuf("l2fr_slu", [128, 1], U32)
        ps_sl = P.psum("l2fr_pssl", [128, 1], F32)
        for tt in range(NT):
            tb = tt * TILE
            tT4 = P.op("tensor", lambda t, tb=tb: t.transpose(
                ps_sl[:, :], sb_slotrow[0:1, tb:tb + 128], idf[0:1, 0:1]),
                waits=sr_toks + [P.note(ps_sl)])
            tC4 = P.op("vector", lambda v: v.tensor_copy(sb_sl[:, :], ps_sl[:, :]),
                       waits=[tT4, P.note(sb_slu)])
            P.setn(ps_sl, tC4)
            tU = P.op("vector", lambda v: v.tensor_copy(sb_slu[:, :], sb_sl[:, :]), waits=[tC4])
            tD = P.dma("sync", slot_d[tb:tb + 128, :], sb_slu[:, :], waits=[tU])
            P.setn(sb_slu, tD)
    _phR()
    P.barrier()

    def _phS():
      with P.scope():
        sb_init = P.sbuf("l2fs_init", [128, 2], F32)
        sb_ii = P.sbuf("l2fs_ii", [128, 1], I32)
        t_i1 = P.op("gpsimd", lambda g: g.iota(sb_ii[:, :], pattern=[[1, 1]], base=T,
                                               channel_multiplier=1))
        t_i2 = P.op("vector", lambda v: v.tensor_copy(sb_init[:, 0:1], sb_ii[:, :]),
                    waits=[t_i1])
        t_i3 = P.op("vector", lambda v: v.memset(sb_init[:, 1:2], 0.0), waits=[t_i2])
        sb_zero = P.sbuf("l2fs_zero", [128, C], BF16)
        t_z = P.op("vector", lambda v: v.memset(sb_zero[:, :], 0.0))
        init_toks = []
        for k in range(NPT):
            init_toks.append(P.dma("sync", meta_d[k * TILE:(k + 1) * TILE, :], sb_init[:, :],
                                   waits=[t_i3]))
            init_toks.append(P.dma("gpsimd_q", sorted_d[k * TILE:(k + 1) * TILE, :],
                                   sb_zero[:, :], waits=[t_z]))
        sb_y = P.sbuf("l2fs_y", [128, C], BF16)
        sb_m = P.sbuf("l2fs_m", [128, 2], F32)
        sb_s = P.sbuf("l2fs_s", [128, 1], U32)
        prev = init_toks[-8:]
        for tt in range(NT):
            tb = tt * TILE
            t_y = P.dma("sync", sb_y[:, :], y_d[tb:tb + 128, :], waits=prev)
            t_m = P.dma("sync", sb_m[:, :], metastage[tb:tb + 128, :], waits=prev)
            t_s = P.dma("sync", sb_s[:, :], slot_d[tb:tb + 128, :], waits=prev)
            s1 = P.indirect_dma(sorted_d[:, :], IndirectOffsetOnAxis(sb_s[:, :], 0),
                                sb_y[:, :], None, waits=[t_y, t_s])
            s2 = P.indirect_dma(meta_d[:, :], IndirectOffsetOnAxis(sb_s[:, :], 0),
                                sb_m[:, :], None, waits=[t_m, t_s] + init_toks[-1:])
            prev = [s1, s2]
    _phS()
    P.barrier()

    def _phE():
      with P.scope():
        sb_wix = P.sbuf("l2fe_wix", [128, NPT], U32)
        t_tx = P.dma("sync", sb_wix[:, :], widx_d[:, :])
        b1flat = b1col_d.rearrange("a b c -> (a b) c")
        b2flat = b2rep_d.rearrange("a b c -> (a b) c")
        sb_xs = P.sbuf("l2fe_xs", [128, C], BF16)
        sb_xT = P.sbuf("l2fe_xT", [128, 8 * 128], BF16)
        sb_W1 = P.sbuf("l2fe_W1", [128, 8 * HID], BF16)
        sb_W2 = P.sbuf("l2fe_W2", [128, 16 * C], BF16)
        sb_b1c = P.sbuf("l2fe_b1c", [128, 16], F32)
        sb_b2rep = P.sbuf("l2fe_b2rep", [128, C], F32)
        sb_hT = P.sbuf("l2fe_hT", [128, 16 * 128], BF16)
        sb_meta = P.sbuf("l2fe_meta", [128, 2], F32)
        sb_tid = P.sbuf("l2fe_tid", [128, 1], U32)
        sb_out = P.sbuf("l2fe_out", [128, C], F32)
        ones1 = P.sbuf("l2fe_ones1", [1, 128], BF16)
        ps_h = P.psum("l2fe_psh", [128, 128], F32)
        ps_o = [P.psum(f"l2fe_pso{i}", [128, 512], F32) for i in range(2)]
        ps_rep = P.psum("l2fe_psrep", [128, 512], F32)
        t_on = P.op("vector", lambda v: v.memset(ones1[:, :], 1.0))
        prev = [t_tx, t_on]
        prev_sc = None
        cell = {}

        for k in range(NPT):
            kb = k * TILE
            t_w1 = P.indirect_dma(sb_W1[:, :], None, w1b_d[:, :],
                                  IndirectOffsetOnAxis(sb_wix[:, k:k + 1], 0),
                                  waits=prev + [P.note(sb_W1), t_tx])
            t_w2 = P.indirect_dma(sb_W2[:, :], None, w2b_d[:, :],
                                  IndirectOffsetOnAxis(sb_wix[:, k:k + 1], 0),
                                  waits=prev + [P.note(sb_W2), t_tx])
            t_b1c = P.indirect_dma(sb_b1c[:, :], None, b1flat,
                                   IndirectOffsetOnAxis(sb_wix[:, k:k + 1], 0),
                                   waits=[P.note(sb_b1c), t_tx])
            P.setn(sb_b1c, t_b1c)
            t_b2rep = P.indirect_dma(sb_b2rep[:, :], None, b2flat,
                                     IndirectOffsetOnAxis(sb_wix[:, k:k + 1], 0),
                                     waits=[P.note(sb_b2rep), t_tx])
            P.setn(sb_b2rep, t_b2rep)
            t_x = P.dma("sync", sb_xs[:, :], sorted_d[kb:kb + 128, :], waits=prev)
            t_mt = P.dma("sync", sb_meta[:, :], meta_d[kb:kb + 128, :], waits=prev)
            b1toks = [t_b1c]
            rep_toks = [t_b2rep]
            xt_toks = []
            for cc in range(8):
                tT = P.op("tensor", lambda t, cc=cc: t.transpose(
                    ps_trb[:, :], sb_xs[:, cc * 128:(cc + 1) * 128], idb[:]),
                    waits=[t_x, P.note(ps_trb)])
                tC = P.op("vector", lambda v, cc=cc: v.tensor_copy(
                    sb_xT[:, cc * 128:(cc + 1) * 128], ps_trb[:, :]), waits=[tT])
                P.setn(ps_trb, tC)
                xt_toks.append(tC)
            g_toks = []
            for fc in range(16):
                mm = None
                for cc in range(8):
                    mm = P.op("tensor", lambda t, fc=fc, cc=cc: t.matmul(
                        ps_h[:, :], sb_W1[:, cc * HID + fc * 128:cc * HID + fc * 128 + 128],
                        sb_xT[:, cc * 128:(cc + 1) * 128],
                        start=(cc == 0), stop=(cc == 7)),
                        waits=xt_toks + [t_w1, P.note(ps_h)])
                t_g = P.op("scalar", lambda s, fc=fc: s.activation(
                    sb_hT[:, fc * 128:(fc + 1) * 128], ps_h[:, :], AF.Gelu,
                    bias=sb_b1c[:, fc:fc + 1]), waits=[mm] + b1toks)
                P.setn(ps_h, t_g)
                g_toks.append(t_g)
            o_toks = []
            for ci in range(2):
                mm = None
                for fc in range(16):
                    mm = P.op("tensor", lambda t, ci=ci, fc=fc: t.matmul(
                        ps_o[ci][:, :], sb_hT[:, fc * 128:(fc + 1) * 128],
                        sb_W2[:, fc * C + ci * 512:fc * C + ci * 512 + 512],
                        start=(fc == 0), stop=(fc == 15)),
                        waits=g_toks + [t_w2, P.note(ps_o[ci])])
                t_o1 = P.op("vector", lambda v, ci=ci: v.tensor_tensor(
                    sb_out[:, ci * 512:(ci + 1) * 512], ps_o[ci][:, :],
                    sb_b2rep[:, ci * 512:(ci + 1) * 512], ALU.add),
                    waits=[mm, rep_toks[-1], P.note(sb_out)])
                P.setn(ps_o[ci], t_o1)
                t_o2 = P.op("vector", lambda v, ci=ci: v.tensor_scalar(
                    sb_out[:, ci * 512:(ci + 1) * 512], sb_out[:, ci * 512:(ci + 1) * 512],
                    sb_meta[:, 1:2], None, ALU.mult), waits=[t_o1, t_mt])
                o_toks.append(t_o2)
            t_tu = P.op("vector", lambda v: v.tensor_copy(sb_tid[:, :], sb_meta[:, 0:1]),
                        waits=[t_mt, P.note(sb_tid)])
            t_sc = P.indirect_dma(moe_d[:, :], IndirectOffsetOnAxis(sb_tid[:, :], 0),
                                  sb_out[:, :], None, waits=o_toks + [t_tu])
            P.setn(sb_tid, t_sc)
            P.setn(sb_out, t_sc)
            prev = [t_sc]
            prev_sc = t_sc
        return prev_sc
    tokE = _phE()
    P.barrier()

    def _phF():
      with P.scope():
        sb_x = P.sbuf("l2ff_x", [128, C], F32)
        sb_m = P.sbuf("l2ff_m", [128, C], F32)
        sb_o = P.sbuf("l2ff_o", [128, C], F32)
        prev_st = None
        for tt in range(NT):
            tb = tt * TILE
            t_x = P.dma("sync", sb_x[:], xin[tb:tb + 128, :], waits=[prev_st])
            t_m = P.dma("sync", sb_m[:], moe_d[tb:tb + 128, :], waits=[prev_st])
            t_o = P.op("vector", lambda v: v.scalar_tensor_tensor(
                sb_o[:], sb_m[:], coef, sb_x[:], ALU.mult, ALU.add), waits=[t_x, t_m, prev_st])
            prev_st = P.dma("gpsimd_q", xout[tb:tb + 128, :], sb_o[:], waits=[t_o])
        return prev_st
    return _phF()


def build_l2ffn_launch():
    nc = bass.Bass("TRN2", target_bir_lowering=False)
    xin = nc.dram_tensor("x", [T, C], F32, kind="ExternalInput").ap()
    names = {"w_red": [1024, 16], "wg": [8, 16], "W1": [8, 1024, 2048], "b1": [8, 2048],
             "W2": [8, 2048, 1024], "b2": [8, 1024]}
    w = {k: nc.dram_tensor(k, sh, F32, kind="ExternalInput").ap() for k, sh in names.items()}
    xout = nc.dram_tensor("xo", [T, C], F32, kind="ExternalOutput").ap()
    P = Prog(nc)
    tok = stage_l2ffn(P, xin, xout, w)
    P.finalize(final_waits=[tok])
    return nc



def stage_l1attn(P, xin, xout, w):
    Cd = 128
    NCH = T // 128
    idf, idb, t_id = make_identity(P)
    P.constf(1e-5)
    P.constf(1.0)
    ps_trb = P.psum("l1a_trb", [128, 128], BF16)
    ps_a = P.psum("l1a_psa", [128, 128], F32)
    ps_b = P.psum("l1a_psb", [128, 128], F32)
    ps_proj = P.psum("l1a_psproj", [128, 512], F32)
    ps_row = P.psum("l1a_psrow", [1, 512], F32)

    oT_dummy = None
    lnscr = {
        "sq": P.sbuf("l1aln_sq", [128, C], F32),
        "m": P.sbuf("l1aln_m", [128, 1], F32),
        "ss": P.sbuf("l1aln_ss", [128, 1], F32),
        "var": P.sbuf("l1aln_var", [128, 1], F32),
        "rstd": P.sbuf("l1aln_rstd", [128, 1], F32),
        "msq": P.sbuf("l1aln_msq", [128, 1], F32),
    }
    qT = P.sbuf("l1a_qT", [128, T], BF16)
    kT = P.sbuf("l1a_kT", [128, T], BF16)
    vT = P.sbuf("l1a_vT", [128, T], BF16)
    gT = P.sbuf("l1a_gT", [128, T], BF16)
    oT = P.sbuf("l1a_oT", [128, T], BF16)
    la_row = P.sbuf("l1a_la", [1, T], F32)
    za_row = P.sbuf("l1a_za", [1, T], F32)
    beta_row = P.sbuf("l1a_beta", [1, T], F32)
    sb_W = P.sbuf("l1a_W", [128, 8 * 4 * 128], BF16)
    sb_Wab = P.sbuf("l1a_Wab", [128, 8 * 2], BF16)
    w_toks = []
    for cc in range(8):
        for i, nm in enumerate(["Wq", "Wk", "Wv", "Wg"]):
            w_toks.append(P.dma("gpsimd_q", sb_W[:, cc * 512 + i * 128:cc * 512 + i * 128 + 128],
                                w[nm][cc * 128:(cc + 1) * 128, :]))
        w_toks.append(P.dma("gpsimd_q", sb_Wab[:, cc * 2:cc * 2 + 1], w["Wa"][cc * 128:(cc + 1) * 128, :]))
        w_toks.append(P.dma("gpsimd_q", sb_Wab[:, cc * 2 + 1:cc * 2 + 2], w["Wb"][cc * 128:(cc + 1) * 128, :]))
    bcol = P.sbuf("l1a_bcol", [128, 4], F32)
    bc_toks = [emit_col_load(P, w["bq"], 128, bcol[:, 0:1], idf, ps_a, waits=[t_id])]
    bc_toks.append(emit_col_load(P, w["bk"], 128, bcol[:, 1:2], idf, ps_a))
    bc_toks.append(emit_col_load(P, w["bv"], 128, bcol[:, 2:3], idf, ps_a))
    bc_toks.append(emit_col_load(P, w["bg"], 128, bcol[:, 3:4], idf, ps_a))
    sb_bab = P.sbuf("l1a_bab", [1, 2], F32)
    t_ba = P.dma("sync", sb_bab[0:1, 0:1], w["ba"].rearrange("(a b) -> a b", a=1))
    t_bb = P.dma("sync", sb_bab[0:1, 1:2], w["bb"].rearrange("(a b) -> a b", a=1))

    def _phA():
      with P.scope():
        sb_x = P.sbuf("l1aa_x", [128, C], F32)
        sb_ln = P.sbuf("l1aa_ln", [128, C], BF16)
        sb_yT = P.sbuf("l1aa_yT", [128, 8 * 512], BF16)
        sb_sq = P.sbuf("l1aa_sq", [128, 512], BF16)
        sb_nrm = P.sbuf("l1aa_nrm", [1, 512], F32)
        sb_nrep = P.sbuf("l1aa_nrep", [128, 512], BF16)
        ones1 = P.sbuf("l1aa_ones", [1, 128], BF16)
        onescol = P.sbuf("l1aa_onesc", [128, 1], BF16)
        t_on = P.op("vector", lambda v: v.memset(ones1[:, :], 1.0))
        t_on2 = P.op("vector", lambda v: v.memset(onescol[:, :], 1.0))
        prev = []
        for tc in range(T // 512):
            t0c = tc * 512
            ln_toks = []
            prev_ln = None
            for sub in range(4):
                tb = t0c + sub * 128
                t_l = P.dma("sync", sb_x[:], xin[tb:tb + 128, :], waits=prev + [prev_ln])
                t_ln = emit_ln(P, sb_x[:], 128, sb_ln[:], lnscr, waits=[t_l])
                prev_ln = t_ln
                for cc in range(8):
                    tT = P.op("tensor", lambda t, cc=cc: t.transpose(
                        ps_trb[:, :], sb_ln[:, cc * 128:(cc + 1) * 128], idb[:]),
                        waits=[t_ln, P.note(ps_trb)])
                    tC = P.op("vector", lambda v, cc=cc, sub=sub: v.tensor_copy(
                        sb_yT[:, cc * 512 + sub * 128:cc * 512 + sub * 128 + 128],
                        ps_trb[:, :]), waits=[tT])
                    P.setn(ps_trb, tC)
                    ln_toks.append(tC)
            prev = [ln_toks[-1]]
            for i, (dst, bi) in enumerate([(qT, 0), (kT, 1), (vT, 2), (gT, 3)]):
                mm = None
                for cc in range(8):
                    mm = P.op("tensor", lambda t, cc=cc, i=i: t.matmul(
                    ps_proj[:, :], sb_W[:, cc * 512 + i * 128:cc * 512 + i * 128 + 128],
                    sb_yT[:, cc * 512:(cc + 1) * 512],
                    start=(cc == 0), stop=(cc == 7)),
                    waits=w_toks + ln_toks + [P.note(ps_proj)])
                if i == 3:
                    t_e = P.op("scalar", lambda s, t0c=t0c: s.activation(
                        gT[:, t0c:t0c + 512], ps_proj[:, :], AF.Sigmoid,
                        bias=bcol[:, 3:4]), waits=[mm] + bc_toks)
                    P.setn(ps_proj, t_e)
                elif i == 0 or i == 2:
                    dst2 = qT if i == 0 else vT
                    t_e = P.op("scalar", lambda s, t0c=t0c, dst2=dst2, bi=bi: s.activation(
                        dst2[:, t0c:t0c + 512], ps_proj[:, :], AF.Identity,
                        bias=bcol[:, bi:bi + 1]), waits=[mm] + bc_toks)
                    P.setn(ps_proj, t_e)
                else:
                    t_kb = P.op("scalar", lambda s, t0c=t0c: s.activation(
                        kT[:, t0c:t0c + 512], ps_proj[:, :], AF.Identity,
                        bias=bcol[:, 1:2]), waits=[mm] + bc_toks)
                    P.setn(ps_proj, t_kb)
                    t_sq = P.op("scalar", lambda s, t0c=t0c: s.activation(
                        sb_sq[:, :], kT[:, t0c:t0c + 512], AF.Square), waits=[t_kb, P.note(sb_sq)])
                    t_nm = P.op("tensor", lambda t: t.matmul(ps_row[0:1, 0:512], onescol[:, :],
                                                             sb_sq[:, :], start=True, stop=True),
                                waits=[t_sq, t_on, t_on2, P.note(ps_row)])
                    P.setn(sb_sq, t_nm)
                    t_sr = P.op("scalar", lambda s: s.activation(sb_nrm[:, :], ps_row[0:1, 0:512],
                                                                 AF.Sqrt), waits=[t_nm])
                    P.setn(ps_row, t_sr)
                    t_mx = P.op("vector", lambda v: v.tensor_single_scalar(
                        sb_nrm[:, :], sb_nrm[:, :], 1e-12, ALU.max), waits=[t_sr])
                    t_rc = P.op("vector", lambda v: v.reciprocal(sb_nrm[:, :], sb_nrm[:, :]),
                                waits=[t_mx])
                    t_rcb = P.op("vector", lambda v: v.tensor_copy(
                        sb_sq[0:1, 0:512].bitcast(BF16)[0:1, 0:512] if False else sb_nrm[:, :],
                        sb_nrm[:, :]), waits=[t_rc]) if False else t_rc
                    sb_nrmb = sb_nrm
                    t_nb = P.op("vector", lambda v: v.tensor_copy(
                        sb_sq[0:1, 0:512], sb_nrm[:, :]), waits=[t_rc])
                    t_rep = P.op("tensor", lambda t: t.matmul(
                        ps_proj[:, :], ones1[:, :], sb_sq[0:1, 0:512], start=True, stop=True),
                        waits=[t_nb, P.note(ps_proj)]) 
                    t_e = P.op("vector", lambda v, t0c=t0c: v.tensor_tensor(
                        kT[:, t0c:t0c + 512], kT[:, t0c:t0c + 512], ps_proj[:, :], ALU.mult),
                        waits=[t_rep])
                P.setn(ps_proj, t_e)
            for j in range(2):
                mm = None
                for cc in range(8):
                    mm = P.op("tensor", lambda t, cc=cc, j=j: t.matmul(
                        ps_row[0:1, 0:512], sb_Wab[:, cc * 2 + j:cc * 2 + j + 1],
                        sb_yT[:, cc * 512:(cc + 1) * 512],
                        start=(cc == 0), stop=(cc == 7)),
                        waits=w_toks + ln_toks + [P.note(ps_row)])
                if j == 0:
                    t_z = P.op("vector", lambda v, t0c=t0c: v.tensor_scalar(
                        za_row[0:1, t0c:t0c + 512], ps_row[0:1, 0:512], sb_bab[0:1, 0:1],
                        None, ALU.add), waits=[mm, t_ba])
                    P.setn(ps_row, t_z)
                else:
                    t_z = P.op("vector", lambda v, t0c=t0c: v.tensor_scalar(
                        beta_row[0:1, t0c:t0c + 512], ps_row[0:1, 0:512], sb_bab[0:1, 1:2],
                        None, ALU.add), waits=[mm, t_bb])
                    P.setn(ps_row, t_z)
        t1a = P.sbuf("l1aa_t1a", [1, T], F32)
        t1b = P.sbuf("l1aa_t1b", [1, T], F32)
        t_m0 = P.op("vector", lambda v: v.tensor_single_scalar(t1a[:, :], za_row[:, :],
                                                               0.0, ALU.min),
                    waits=[P.cur("vector")])
        t_ab = P.op("scalar", lambda s: s.activation(la_row[:, :], za_row[:, :], AF.Abs),
                    waits=[t_m0])
        t_ex = P.op("scalar", lambda s: s.activation(la_row[:, :], la_row[:, :], AF.Exp,
                                                     scale=-1.0), waits=[t_ab])
        cone, cone_tok = P.constf(1.0)
        t_ln1 = P.op("scalar", lambda s: s.activation(la_row[:, :], la_row[:, :], AF.Ln,
                                                      bias=cone[0:1]), waits=[t_ex, cone_tok])
        t_la = P.op("vector", lambda v: v.tensor_tensor(la_row[:, :], t1a[:, :],
                                                        la_row[:, :], ALU.subtract),
                    waits=[t_ln1])
        t_o1 = P.op("vector", lambda v: v.memset(t1a[:, :], 1.0), waits=[t_la])
        t_sc = P.op("vector", lambda v: v.tensor_tensor_scan(
            t1b[:, :], t1a[:, :], la_row[:, :], 0.0, ALU.mult, ALU.add), waits=[t_o1])
        t_cp = P.op("vector", lambda v: v.tensor_copy(la_row[:, :], t1b[:, :]), waits=[t_sc])
        t_bt = P.op("scalar", lambda s: s.activation(beta_row[:, :], beta_row[:, :],
                                                     AF.Sigmoid), waits=[t_cp])
        return [t_cp, t_bt]
    pa = _phA()
    P.barrier()

    def _phB():
      with P.scope():
        R = P.sbuf("l1b_R", [128, 128], F32)
        RT = P.sbuf("l1b_RT", [128, 128], F32)
        Rb = P.sbuf("l1b_Rb", [128, 128], BF16)
        RTb = P.sbuf("l1b_RTb", [128, 128], BF16)
        N_sb = P.sbuf("l1b_N", [128, 128], BF16)
        NT_sb = P.sbuf("l1b_NT", [128, 128], BF16)
        rat = P.sbuf("l1b_rat", [128, 128], F32)
        ratT = P.sbuf("l1b_ratT", [128, 128], F32)
        tril_s = P.sbuf("l1b_trils", [128, 128], BF16)
        triu_i = P.sbuf("l1b_triui", [128, 128], BF16)
        mio = P.sbuf("l1b_mio", [128, 128], I32)
        Ktok = P.sbuf("l1b_Ktok", [128, 128], BF16)
        Rr = P.sbuf("l1b_Rr", [128, 128], F32)
        Rrb = P.sbuf("l1b_Rrb", [128, 128], BF16)
        RrTb = P.sbuf("l1b_RrTb", [128, 128], BF16)
        W_sb = P.sbuf("l1b_W", [128, 128], BF16)
        PsiT = P.sbuf("l1b_PsiT", [128, 128], BF16)
        la_col = P.sbuf("l1b_lacol", [128, 1], F32)
        lac_st = P.sbuf("l1b_lacst", [128, 1], F32)
        lac_la = P.sbuf("l1b_lacla", [128, 1], F32)
        A_col = P.sbuf("l1b_Acol", [128, 1], F32)
        b_col = P.sbuf("l1b_bcol", [128, 1], F32)
        ratio_c = P.sbuf("l1b_ratioc", [128, 1], F32)
        la_rep = P.sbuf("l1b_larep", [128, 128], F32)
        tmp = P.sbuf("l1b_tmp", [128, 128], F32)
        tmpb = P.sbuf("l1b_tmpb", [128, 128], BF16)
        Otok = P.sbuf("l1b_Otok", [128, 128], F32)
        ones1 = P.sbuf("l1b_ones", [1, 128], F32)
        onesc = P.sbuf("l1b_onesc", [128, 1], BF16)
        t_on = P.op("vector", lambda v: v.memset(ones1[:, :], 1.0))
        t_oc = P.op("vector", lambda v: v.memset(onesc[:, :], 1.0))
        t_r0 = P.op("vector", lambda v: v.memset(R[:, :], 0.0))
        t_r1 = P.op("vector", lambda v: v.memset(RT[:, :], 0.0))
        t_r2 = P.op("vector", lambda v: v.memset(Rb[:, :], 0.0))
        t_r3 = P.op("vector", lambda v: v.memset(RTb[:, :], 0.0))
        t_i1 = P.op("gpsimd", lambda g: g.iota(mio[:, :], pattern=[[1, 128]], base=0,
                                               channel_multiplier=-1))
        t_tl = P.op("vector", lambda v: v.tensor_single_scalar(tril_s[:, :], mio[:, :], 0.5,
                                                               ALU.is_gt), waits=[t_i1])
        t_tl = P.op("vector", lambda v: v.tensor_single_scalar(tril_s[:, :], mio[:, :], -0.5,
                                                               ALU.is_lt), waits=[t_tl])
        t_tu = P.op("vector", lambda v: v.tensor_single_scalar(triu_i[:, :], mio[:, :], -0.5,
                                                               ALU.is_gt), waits=[t_i1])
        prev = [t_on, t_oc, t_r0, t_r1, t_r2, t_r3, t_tl, t_tu] + pa
        for ch in range(NCH):
            c0 = ch * 128
            tT1 = P.op("tensor", lambda t, c0=c0: t.transpose(
                ps_a[:, 0:1], la_row[0:1, c0:c0 + 128], idf[0:1, 0:1]),
                waits=prev + [P.note(ps_a)])
            tC1 = P.op("vector", lambda v: v.tensor_copy(la_col[:, :], ps_a[:, 0:1]),
                       waits=[tT1, P.note(la_col)])
            P.setn(ps_a, tC1)
            tT2 = P.op("tensor", lambda t, c0=c0: t.transpose(
                ps_a[:, 0:1], beta_row[0:1, c0:c0 + 128], idf[0:1, 0:1]),
                waits=[tC1, P.note(ps_a)])
            tC2 = P.op("vector", lambda v: v.tensor_copy(b_col[:, :], ps_a[:, 0:1]),
                       waits=[tT2, P.note(b_col)])
            P.setn(ps_a, tC2)
            if ch == 0:
                t_st = P.op("vector", lambda v: v.memset(lac_st[:, :], 0.0), waits=[P.note(lac_st)])
            else:
                t_st = P.op("tensor", lambda t, c0=c0: t.matmul(
                    ps_a[:, 0:1], ones1[0:1, :], la_row[0:1, c0 - 1:c0], start=True, stop=True),
                    waits=[tC2, P.note(ps_a)])
                t_st = P.op("vector", lambda v: v.tensor_copy(lac_st[:, :], ps_a[:, 0:1]),
                            waits=[t_st, P.note(lac_st)])
                P.setn(ps_a, t_st)
            t_lt = P.op("tensor", lambda t, c0=c0: t.matmul(
                ps_a[:, 0:1], ones1[0:1, :], la_row[0:1, c0 + 127:c0 + 128], start=True,
                stop=True), waits=[t_st, P.note(ps_a)])
            t_lt = P.op("vector", lambda v: v.tensor_copy(lac_la[:, :], ps_a[:, 0:1]),
                        waits=[t_lt, P.note(lac_la)])
            P.setn(ps_a, t_lt)
            t_ac = P.op("vector", lambda v: v.tensor_scalar(A_col[:, :], la_col[:, :],
                                                            lac_st[:, :], None, ALU.subtract),
                        waits=[tC1, t_st, P.note(A_col)])
            t_ac = P.op("scalar", lambda s: s.activation(A_col[:, :], A_col[:, :], AF.Exp),
                        waits=[t_ac])
            t_rc = P.op("vector", lambda v: v.tensor_scalar(ratio_c[:, :], la_col[:, :], -1.0,
                                                            lac_la[:, :], ALU.mult, ALU.add),
                        waits=[t_lt, tC1, P.note(ratio_c)])
            t_rc = P.op("scalar", lambda s: s.activation(ratio_c[:, :], ratio_c[:, :], AF.Exp),
                        waits=[t_rc])
            t_lr = P.op("tensor", lambda t, c0=c0: t.matmul(
                ps_b[:, :], ones1[:, :], la_row[0:1, c0:c0 + 128], start=True, stop=True),
                waits=[t_rc, P.note(ps_b)])
            t_lr2 = P.op("vector", lambda v: v.tensor_copy(la_rep[:, :], ps_b[:, :]),
                         waits=[t_lr, P.note(la_rep)])
            P.setn(ps_b, t_lr2)
            t_ra = P.op("vector", lambda v: v.tensor_scalar(rat[:, :], la_rep[:, :], -1.0,
                                                            la_col[:, :], ALU.mult, ALU.add),
                        waits=[t_lr2, P.note(rat)])
            t_ra = P.op("vector", lambda v: v.tensor_single_scalar(rat[:, :], rat[:, :], 0.0,
                                                                   ALU.min), waits=[t_ra])
            t_ra = P.op("scalar", lambda s: s.activation(rat[:, :], rat[:, :], AF.Exp),
                        waits=[t_ra])
            t_rt = P.op("vector", lambda v: v.tensor_scalar(ratT[:, :], la_rep[:, :], 1.0,
                                                            la_col[:, :], ALU.bypass, ALU.subtract),
                        waits=[t_ra, P.note(ratT)]) if False else P.op(
                "vector", lambda v: v.scalar_tensor_tensor(ratT[:, :], la_rep[:, :], 1.0,
                                                           la_col[:, :].broadcast_to([128, 128]) if False else la_rep[:, :],
                                                           ALU.mult, ALU.subtract), waits=[t_ra]) if False else None
            t_rt = P.op("vector", lambda v: v.tensor_scalar(ratT[:, :], la_col[:, :], 1.0,
                                                            None, ALU.mult), waits=[P.note(ratT)]) \
                if False else None
            t_rt = P.op("vector", lambda v: v.scalar_tensor_tensor(
                ratT[:, :], la_rep[:, :], -1.0, la_col[:, :].broadcast_to([128, 128]) if False
                else la_rep[:, :], ALU.mult, ALU.add), waits=[t_lr2]) if False else None
            t_rt = P.op("vector", lambda v: v.tensor_scalar(ratT[:, :], la_rep[:, :],
                                                            la_col[:, :], None, ALU.subtract),
                        waits=[P.note(ratT), t_lr2])
            t_rt = P.op("vector", lambda v: v.tensor_single_scalar(ratT[:, :], ratT[:, :], 0.0,
                                                                   ALU.min), waits=[t_rt])
            t_rt = P.op("scalar", lambda s: s.activation(ratT[:, :], ratT[:, :], AF.Exp),
                        waits=[t_rt])
            t_G = P.op("tensor", lambda t, c0=c0: t.matmul(
                ps_a[:, :], kT[:, c0:c0 + 128], kT[:, c0:c0 + 128], start=True, stop=True),
                waits=[t_lt] + prev + [P.note(ps_a)])
            t_p1 = P.op("vector", lambda v: v.tensor_tensor(tmp[:, :], ps_a[:, :], rat[:, :],
                                                            ALU.mult), waits=[t_G, t_ra, P.note(tmp)])
            P.setn(ps_a, t_p1)
            t_p2 = P.op("vector", lambda v: v.tensor_scalar(tmp[:, :], tmp[:, :], b_col[:, :],
                                                            -1.0, ALU.mult, ALU.mult),
                        waits=[t_p1, tC2])
            t_p3 = P.op("vector", lambda v: v.tensor_tensor(N_sb[:, :], tmp[:, :], tril_s[:, :],
                                                            ALU.mult), waits=[t_p2, P.note(N_sb)])
            tTn = P.op("tensor", lambda t: t.transpose(ps_trb[:, :], N_sb[:, :], idb[:]),
                       waits=[t_p3, P.note(ps_trb)])
            tCn = P.op("vector", lambda v: v.tensor_copy(NT_sb[:, :], ps_trb[:, :]),
                       waits=[tTn, P.note(NT_sb)])
            P.setn(ps_trb, tCn)
            tTk = P.op("tensor", lambda t, c0=c0: t.transpose(
                ps_trb[:, :], kT[:, c0:c0 + 128], idb[:]), waits=[tCn, P.note(ps_trb)])
            tCk = P.op("vector", lambda v: v.tensor_copy(Ktok[:, :], ps_trb[:, :]),
                       waits=[tTk, P.note(Ktok)])
            P.setn(ps_trb, tCk)
            t_kr = P.op("tensor", lambda t, c0=c0: t.matmul(
                ps_b[:, :], kT[:, c0:c0 + 128], Rb[:, :], start=True, stop=True),
                waits=[t_lr2, P.note(ps_b), tCk])
            tTv = P.op("tensor", lambda t, c0=c0: t.transpose(
                ps_trb[:, :], vT[:, c0:c0 + 128], idb[:]), waits=[tCk, P.note(ps_trb)])
            tCv = P.op("vector", lambda v: v.tensor_copy(tmpb[:, :], ps_trb[:, :]),
                       waits=[tTv, P.note(tmpb)])
            P.setn(ps_trb, tCv)
            t_rr1 = P.op("vector", lambda v: v.tensor_scalar(Rr[:, :], ps_b[:, :], A_col[:, :],
                                                             -1.0, ALU.mult, ALU.mult),
                         waits=[t_kr, t_ac, P.note(Rr)])
            P.setn(ps_b, t_rr1)
            t_rr2 = P.op("vector", lambda v: v.tensor_tensor(Rr[:, :], Rr[:, :], tmpb[:, :],
                                                             ALU.add), waits=[t_rr1, tCv])
            t_rr3 = P.op("vector", lambda v: v.tensor_scalar(Rr[:, :], Rr[:, :], b_col[:, :],
                                                             None, ALU.mult), waits=[t_rr2])
            t_rrb = P.op("vector", lambda v: v.tensor_copy(Rrb[:, :], Rr[:, :]),
                         waits=[t_rr3, P.note(Rrb)])
            t_apply = t_rrb
            for it in range(7):
                t_mm = P.op("tensor", lambda t: t.matmul(ps_b[:, :], NT_sb[:, :], Rrb[:, :],
                                                         start=True, stop=True),
                            waits=[t_apply, tCn, P.note(ps_b)])
                t_ad = P.op("vector", lambda v: v.tensor_tensor(Rr[:, :], Rr[:, :], ps_b[:, :],
                                                                ALU.add), waits=[t_mm])
                P.setn(ps_b, t_ad)
                t_apply = P.op("vector", lambda v: v.tensor_copy(Rrb[:, :], Rr[:, :]),
                               waits=[t_ad])
                if it < 6:
                    t_sq = P.op("tensor", lambda t: t.matmul(ps_a[:, :], NT_sb[:, :], N_sb[:, :],
                                                             start=True, stop=True),
                                waits=[t_apply, P.note(ps_a)])
                    t_nc = P.op("vector", lambda v: v.tensor_copy(N_sb[:, :], ps_a[:, :]),
                                waits=[t_sq])
                    P.setn(ps_a, t_nc)
                    tTn2 = P.op("tensor", lambda t: t.transpose(ps_trb[:, :], N_sb[:, :], idb[:]),
                                waits=[t_nc, P.note(ps_trb)])
                    tCn = P.op("vector", lambda v: v.tensor_copy(NT_sb[:, :], ps_trb[:, :]),
                               waits=[tTn2, P.note(NT_sb)])
                    P.setn(ps_trb, tCn)
            tTr = P.op("tensor", lambda t: t.transpose(ps_trb[:, :], Rrb[:, :], idb[:]),
                       waits=[t_apply, P.note(ps_trb)])
            tCr = P.op("vector", lambda v: v.tensor_copy(RrTb[:, :], ps_trb[:, :]),
                       waits=[tTr, P.note(RrTb)])
            P.setn(ps_trb, tCr)
            t_mt = P.op("tensor", lambda t, c0=c0: t.matmul(
                ps_a[:, :], RrTb[:, :], qT[:, c0:c0 + 128], start=True, stop=True),
                waits=[tCr, P.note(ps_a)])
            t_ps1 = P.op("vector", lambda v: v.tensor_tensor(tmp[:, :], ps_a[:, :], ratT[:, :],
                                                             ALU.mult), waits=[t_mt, t_rt, P.note(tmp)])
            P.setn(ps_a, t_ps1)
            t_ps2 = P.op("vector", lambda v: v.tensor_tensor(PsiT[:, :], tmp[:, :], triu_i[:, :],
                                                             ALU.mult), waits=[t_ps1, P.note(PsiT)])
            t_q0 = P.op("tensor", lambda t, c0=c0: t.matmul(
                ps_b[:, :], qT[:, c0:c0 + 128], RTb[:, :], start=True, stop=True),
                waits=[t_apply, P.note(ps_b)])
            t_op = P.op("tensor", lambda t: t.matmul(ps_a[:, :], PsiT[:, :], Ktok[:, :],
                                                     start=True, stop=True),
                        waits=[t_ps2, tCk, P.note(ps_a)])
            t_o1 = P.op("vector", lambda v: v.tensor_scalar(Otok[:, :], ps_b[:, :], A_col[:, :],
                                                            None, ALU.mult),
                        waits=[t_q0, P.note(Otok)])
            P.setn(ps_b, t_o1)
            t_o2 = P.op("vector", lambda v: v.tensor_tensor(Otok[:, :], Otok[:, :], ps_a[:, :],
                                                            ALU.add), waits=[t_o1, t_op])
            P.setn(ps_a, t_o2)
            tTg = P.op("tensor", lambda t, c0=c0: t.transpose(
                ps_trb[:, :], gT[:, c0:c0 + 128], idb[:]), waits=[t_o2, P.note(ps_trb)])
            tCg = P.op("vector", lambda v: v.tensor_copy(tmpb[:, :], ps_trb[:, :]),
                       waits=[tTg, P.note(tmpb)])
            P.setn(ps_trb, tCg)
            t_o3 = P.op("vector", lambda v: v.tensor_tensor(Otok[:, :], Otok[:, :], tmpb[:, :],
                                                            ALU.mult), waits=[t_o2, tCg])
            t_ob = P.op("vector", lambda v: v.tensor_copy(tmpb[:, :], Otok[:, :]),
                        waits=[t_o3])
            P.setn(tmpb, t_ob)
            tTo = P.op("tensor", lambda t: t.transpose(ps_trb[:, :], tmpb[:, :], idb[:]),
                       waits=[t_ob, P.note(ps_trb)])
            tCo = P.op("vector", lambda v, c0=c0: v.tensor_copy(oT[:, c0:c0 + 128],
                                                                ps_trb[:, :]),
                       waits=[tTo])
            P.setn(ps_trb, tCo)
            t_w = P.op("vector", lambda v: v.tensor_scalar(W_sb[:, :], Rr[:, :], ratio_c[:, :],
                                                           None, ALU.mult),
                       waits=[t_apply, t_rc, P.note(W_sb)])
            t_u1 = P.op("tensor", lambda t: t.matmul(ps_a[:, :], Ktok[:, :], W_sb[:, :],
                                                     start=True, stop=True),
                        waits=[t_w, t_o3, P.note(ps_a)])
            t_an = P.op("vector", lambda v: v.tensor_scalar(la_rep[:, 0:1], lac_la[:, :],
                                                            lac_st[:, :], None, ALU.subtract),
                        waits=[t_lt, t_st, t_lr2, t_ra, t_rt])
            t_an = P.op("scalar", lambda s: s.activation(la_rep[:, 0:1], la_rep[:, 0:1],
                                                         AF.Exp), waits=[t_an])
            t_ru = P.op("vector", lambda v: v.scalar_tensor_tensor(
                R[:, :], R[:, :], la_rep[:, 0:1], ps_a[:, :], ALU.mult, ALU.add),
                waits=[t_u1, t_an, t_kr])
            P.setn(ps_a, t_ru)
            t_u2 = P.op("tensor", lambda t: t.matmul(ps_b[:, :], W_sb[:, :], Ktok[:, :],
                                                     start=True, stop=True),
                        waits=[t_w, t_o1, P.note(ps_b), t_ru])
            t_rtu = P.op("vector", lambda v: v.scalar_tensor_tensor(
                RT[:, :], RT[:, :], la_rep[:, 0:1], ps_b[:, :], ALU.mult, ALU.add),
                waits=[t_u2, t_an, t_q0])
            P.setn(ps_b, t_rtu)
            t_rb = P.op("vector", lambda v: v.tensor_copy(Rb[:, :], R[:, :]), waits=[t_ru])
            t_rtb = P.op("vector", lambda v: v.tensor_copy(RTb[:, :], RT[:, :]), waits=[t_rtu])
            prev = [tCo, t_rb, t_rtb]
        return prev
    pb = _phB()
    P.barrier()

    def _phC():
      with P.scope():
        sb_Wo = P.sbuf("l1c_Wo", [128, C], BF16)
        t_wo = P.dma("gpsimd_q", sb_Wo[:, :], w["Wo"][:, :])
        boc = P.sbuf("l1c_boc", [128, 8], F32)
        t_bo = emit_col_load(P, w["bo"], C, boc, idf, ps_a, waits=[t_id])
        sb_outT = P.sbuf("l1c_outT", [128, 8 * 512], BF16)
        sb_tok = P.sbuf("l1c_tok", [128, C], BF16)
        sb_x = P.sbuf("l1c_x", [128, C], F32)
        prev_st = None
        for tc in range(T // 512):
            t0c = tc * 512
            ev_toks = []
            for cc in range(8):
                mm = P.op("tensor", lambda t, cc=cc, t0c=t0c: t.matmul(
                    ps_b[:, 0:512] if False else ps_b[:, :],
                    sb_Wo[:, cc * 128:(cc + 1) * 128],
                    oT[:, t0c:t0c + 512][:, 0:128] if False else oT[:, t0c:t0c + 128] if False else oT[:, t0c:t0c + 512],
                    start=True, stop=True), waits=[t_wo] + pb + ev_toks[-1:] + [P.note(ps_b)]) if False else None
            for cc in range(8):
                mm = P.op("tensor", lambda t, cc=cc, t0c=t0c: t.matmul(
                    ps_proj[:, :], sb_Wo[:, cc * 128:(cc + 1) * 128], oT[:, t0c:t0c + 512],
                    start=True, stop=True), waits=[t_wo] + pb + [P.note(ps_proj)])
                t_e = P.op("scalar", lambda s, cc=cc: s.activation(
                    sb_outT[:, cc * 512:(cc + 1) * 512], ps_proj[:, :], AF.Identity,
                    bias=boc[:, cc:cc + 1]), waits=[mm, t_bo, P.note(sb_outT)])
                P.setn(ps_proj, t_e)
                ev_toks.append(t_e)
            for sub in range(4):
                tb = t0c + sub * 128
                tr_toks = []
                for cc in range(8):
                    tT = P.op("tensor", lambda t, cc=cc, sub=sub: t.transpose(
                        ps_trb[:, :], sb_outT[:, cc * 512 + sub * 128:cc * 512 + sub * 128 + 128],
                        idb[:]), waits=ev_toks + [P.note(ps_trb)])
                    tC = P.op("vector", lambda v, cc=cc: v.tensor_copy(
                        sb_tok[:, cc * 128:(cc + 1) * 128], ps_trb[:, :]),
                        waits=[tT, P.note(sb_tok)])
                    P.setn(ps_trb, tC)
                    tr_toks.append(tC)
                t_x = P.dma("sync", sb_x[:], xin[tb:tb + 128, :], waits=[prev_st])
                t_o = P.op("vector", lambda v: v.tensor_tensor(sb_x[:], sb_x[:], sb_tok[:],
                                                               ALU.add),
                           waits=tr_toks + [t_x, prev_st])
                P.setn(sb_tok, t_o)
                prev_st = P.dma("gpsimd_q", xout[tb:tb + 128, :], sb_x[:], waits=[t_o])
        return prev_st
    return _phC()


def build_l1attn_launch():
    nc = bass.Bass("TRN2", target_bir_lowering=False)
    xin = nc.dram_tensor("x", [T, C], F32, kind="ExternalInput").ap()
    names = {"Wq": [1024, 128], "bq": [128], "Wk": [1024, 128], "bk": [128],
             "Wv": [1024, 128], "bv": [128], "Wa": [1024, 1], "ba": [1],
             "Wb": [1024, 1], "bb": [1], "Wg": [1024, 128], "bg": [128],
             "Wo": [128, 1024], "bo": [1024]}
    w = {k: nc.dram_tensor(k, sh, F32, kind="ExternalInput").ap() for k, sh in names.items()}
    xout = nc.dram_tensor("xo", [T, C], F32, kind="ExternalOutput").ap()
    P = Prog(nc)
    tok = stage_l1attn(P, xin, xout, w)
    P.finalize(final_waits=[tok])
    return nc



def _l0ffn_consts(Tn):
    import numpy as _np
    DIM_HEAD, WINDOW = 64, 512
    half = _np.arange(0, DIM_HEAD, 2, dtype=_np.float64)
    inv = 1.0 / 10000.0 ** (half / DIM_HEAD)
    t = _np.arange(Tn, dtype=_np.float64)
    fr = t[:, None] * inv[None, :]
    fr = _np.concatenate([fr, fr], -1)
    base = (half + 0.4 * DIM_HEAD) / (1.4 * DIM_HEAD)
    power = (t - Tn // 2) / float(WINDOW // 2)
    sc = base[None, :] ** power[:, None]
    sc = _np.concatenate([sc, sc], -1)
    cos, sin = _np.cos(fr), _np.sin(fr)
    csq = (cos * sc * 8.0).T.astype(_np.float32)
    ssq = (sin * sc * 8.0).T.astype(_np.float32)
    csk = (cos / sc).T.astype(_np.float32)
    ssk = (sin / sc).T.astype(_np.float32)
    def two(x):
        return _np.concatenate([x, x], 0)
    R64 = _np.zeros((64, 64), _np.float32)
    for d in range(32):
        R64[d, 32 + d] = -1.0
        R64[32 + d, d] = 1.0
    R2T = _np.zeros((128, 128), _np.float32)
    R2T[:64, :64] = R64.T
    R2T[64:, 64:] = R64.T
    ii = _np.arange(512)[None, :]
    jp = _np.arange(128)[:, None]
    msks = []
    for p in range(4):
        msks.append(_np.where(128 * p + jp - ii < 0, -1e30, 0.0).astype(_np.float32))
    for p in range(4):
        msks.append(_np.where(ii - (128 * p + jp) < 0, -1e30, 0.0).astype(_np.float32))
    msk = _np.concatenate(msks, axis=1)
    E2 = _np.zeros((128, 2), _np.float32)
    E2[:64, 0] = 1.0
    E2[64:, 1] = 1.0
    return two(csq), two(ssq), two(csk), two(ssk), R2T, msk, E2


def stage_l0ffn(P, xin, xout, w, coef=0.5):
    import ml_dtypes as _md
    import numpy as _np
    HID2 = 5460
    HID = 2730
    INNER = 512
    idf, idb, t_id = make_identity(P)
    P.constf(1e-5)
    nc = P.nc
    csq, ssq, csk, ssk, R2T, msk_np, E2np = _l0ffn_consts(T)
    c_csq = nc.inline_tensor(_np.asarray(csq.astype(_md.bfloat16)), name=f"l0c_csq_{P.uid()}").ap()
    c_ssq = nc.inline_tensor(_np.asarray(ssq.astype(_md.bfloat16)), name=f"l0c_ssq_{P.uid()}").ap()
    c_csk = nc.inline_tensor(_np.asarray(csk.astype(_md.bfloat16)), name=f"l0c_csk_{P.uid()}").ap()
    c_ssk = nc.inline_tensor(_np.asarray(ssk.astype(_md.bfloat16)), name=f"l0c_ssk_{P.uid()}").ap()
    c_R2T = nc.inline_tensor(_np.asarray(R2T.astype(_md.bfloat16)), name=f"l0c_R2T_{P.uid()}").ap()
    c_msk = nc.inline_tensor(msk_np, name=f"l0c_msk_{P.uid()}").ap()
    c_E2 = nc.inline_tensor(_np.asarray(E2np.astype(_md.bfloat16)), name=f"l0c_E2_{P.uid()}").ap()
    c_E2T = nc.inline_tensor(_np.asarray(E2np.T.copy().astype(_md.bfloat16)), name=f"l0c_E2T_{P.uid()}").ap()

    ps_trb = P.psum("l0f_trb", [128, 128], BF16)
    ps_a = P.psum("l0f_psa", [128, 512], F32)
    ps_b = P.psum("l0f_psb", [128, 512], F32)
    ps_o = P.psum("l0f_pso", [65, 512], F32)
    ps_row = P.psum("l0f_psrow", [2, 512], F32)
    ps_col = P.psum("l0f_pscol", [128, 1], F32)

    z_a = P.dram("l0f_za", [T, C], F32)
    z_b = P.dram("l0f_zb", [T, C], F32)
    hT_d = P.dram("l0f_hT", [C, T], BF16)
    qkvT_d = P.dram("l0f_qkvT", [1536, T], BF16)
    attT_d = P.dram("l0f_attT", [INNER, T], BF16)
    h2T_d = P.dram("l0f_h2T", [C, T], BF16)
    qn2_d = P.dram("l0f_qn2", [8, T], F32)
    kn2_d = P.dram("l0f_kn2", [8, T], F32)

    lnscr = {
        "sq": P.sbuf("l0fln_sq", [128, C], F32),
        "m": P.sbuf("l0fln_m", [128, 1], F32),
        "ss": P.sbuf("l0fln_ss", [128, 1], F32),
        "var": P.sbuf("l0fln_var", [128, 1], F32),
        "rstd": P.sbuf("l0fln_rstd", [128, 1], F32),
        "msq": P.sbuf("l0fln_msq", [128, 1], F32),
    }
    sb_msk = P.sbuf("l0f_msk", [128, 8 * 512], F32)
    sb_E2 = P.sbuf("l0f_E2", [128, 2], BF16)
    sb_E2T = P.sbuf("l0f_E2T", [2, 128], BF16)
    sb_R2T = P.sbuf("l0f_R2T", [128, 128], BF16)
    onesc = P.sbuf("l0f_onesc", [128, 1], BF16)
    ones1 = P.sbuf("l0f_ones1", [1, 128], BF16)
    c_toks = [P.dma("sync", sb_msk[:, :], c_msk[:, :]),
              P.dma("sync", sb_E2[:, :], c_E2[:, :]),
              P.dma("sync", sb_E2T[:, :], c_E2T[:, :]),
              P.dma("sync", sb_R2T[:, :], c_R2T[:, :]),
              P.op("vector", lambda v: v.memset(onesc[:, :], 1.0)),
              P.op("vector", lambda v: v.memset(ones1[:, :], 1.0))]

    def _ph0():
      with P.scope():
        sb_x = P.sbuf("l00_x", [128, C], F32)
        sb_y = P.sbuf("l00_y", [128, C], F32)
        prev = []
        prev_ln = None
        for tt in range(T // 128):
            tb = tt * 128
            t_l = P.dma("sync", sb_x[:], xin[tb:tb + 128, :], waits=prev + [prev_ln])
            t_ln = emit_ln(P, sb_x[:], 128, sb_y[:], lnscr, waits=[t_l] + prev)
            prev_ln = t_ln
            t_s = P.dma("gpsimd_q", z_a[tb:tb + 128, :], sb_y[:], waits=[t_ln])
            prev = [t_s]
    _ph0()
    P.barrier()

    zin, zout = z_a, z_b
    for ly in range(2):
        def _p1(zin=zin):
          with P.scope():
            stg_tr = P.sbuf(f"l0_stg{ly}", [128, 128], BF16)
            sb_sh = P.sbuf("l01_sh", [128, C], BF16)
            sb_shf = P.sbuf("l01_shf", [128, C], F32)
            prev = []
            for tt in range(T // 128):
                tb = tt * 128
                t_1 = P.dma("sync", sb_shf[:, 0:512], zin[tb:tb + 128, 0:512], waits=prev)
                if tt == 0:
                    t_z = P.op("vector", lambda v: v.memset(sb_shf[0:1, 512:1024], 0.0),
                               waits=prev + [t_1])
                    t_2 = P.dma("sync", sb_shf[1:128, 512:1024], zin[0:127, 512:1024],
                                waits=prev)
                    pre = [t_1, t_2, t_z]
                else:
                    t_2 = P.dma("sync", sb_shf[:, 512:1024], zin[tb - 1:tb + 127, 512:1024],
                                waits=prev)
                    pre = [t_1, t_2]
                t_b = P.op("vector", lambda v: v.tensor_copy(sb_sh[:, :], sb_shf[:, :]),
                           waits=pre)
                toks = []
                for cc in range(8):
                    tT = P.op("tensor", lambda t, cc=cc: t.transpose(
                        ps_trb[:, :], sb_sh[:, cc * 128:(cc + 1) * 128], idb[:]),
                        waits=[t_b, t_id, P.note(ps_trb)])
                    tC = P.op("vector", lambda v: v.tensor_copy(
                        lnscr["sq"].bitcast(BF16)[0:128, 0:128] if False else sb_sh[:, 0:128],
                        ps_trb[:, :]), waits=[tT]) if False else None
                    tC = P.op("vector", lambda v, cc=cc: v.tensor_copy(
                        sb_shf.bitcast(BF16)[:, 0:128], ps_trb[:, :]), waits=[tT, P.note(sb_shf)]) \
                        if False else P.op("vector", lambda v: v.tensor_copy(
                            sb_sh[:, 0:128] if False else lnscr["sq"].bitcast(BF16)[0:128, 0:128],
                            ps_trb[:, :]), waits=[tT]) if False else None
                    if cc == 0 and tt == 0 and ly == 0:
                        pass
                    tC = P.op("vector", lambda v: v.tensor_copy(stg_tr[:, :], ps_trb[:, :]),
                              waits=[tT, P.note(stg_tr)])
                    P.setn(ps_trb, tC)
                    tD = P.dma("sync", hT_d[cc * 128:(cc + 1) * 128, tb:tb + 128],
                               stg_tr[:, :], waits=[tC])
                    P.setn(stg_tr, tD)
                    toks.append(tD)
                prev = [toks[-1], t_b]
        _p1()
        P.barrier()

        def _p2():
          with P.scope():
            sb_W = P.sbuf("l02_W", [128, 8 * 1536], BF16)
            w_toks = [P.dma("gpsimd_q", sb_W[:, cc * 1536:(cc + 1) * 1536],
                            w["wqkv"][ly, cc * 128:(cc + 1) * 128, :]) for cc in range(8)]
            qs_c = P.sbuf("l02_qs", [128, 2], F32)
            stg1 = P.sbuf("l02_stg1", [1, 64], F32)
            qs_toks = []
            for i, nm in enumerate(["q_scale", "k_scale"]):
                t_l = P.dma("sync", stg1[:, :], w[nm][ly:ly + 1, :], waits=qs_toks)
                tT = P.op("tensor", lambda t: t.transpose(ps_col[0:64, 0:1],
                                                          stg1[0:1, :], idf[0:1, 0:1]),
                          waits=[t_l, P.note(ps_col)])
                tC1 = P.op("vector", lambda v, i=i: v.tensor_copy(qs_c[0:64, i:i + 1],
                                                                  ps_col[0:64, 0:1]),
                           waits=[tT])
                tC2 = P.op("vector", lambda v, i=i: v.tensor_copy(qs_c[64:128, i:i + 1],
                                                                  ps_col[0:64, 0:1]),
                           waits=[tT])
                P.setn(ps_col, tC2)
                qs_toks = [tC1, tC2]
            sb_h = P.sbuf("l02_h", [128, 8 * 512], BF16)
            sb_raw = P.sbuf("l02_raw", [128, 512], BF16)
            sb_sq = P.sbuf("l02_sq", [128, 512], BF16)
            sb_n = P.sbuf("l02_n", [2, 512], F32)
            sb_nb = P.sbuf("l02_nb", [2, 512], BF16)
            sb_rot = P.sbuf("l02_rot", [128, 512], BF16)
            sb_cs = P.sbuf("l02_cs", [128, 512], BF16)
            sb_out = P.sbuf("l02_out", [128, 512], BF16)
            prev = []
            for tc in range(T // 512):
                t0c = tc * 512
                t_h = [P.dma("sync", sb_h[:, cc * 512:(cc + 1) * 512],
                             hT_d[cc * 128:(cc + 1) * 128, t0c:t0c + 512], waits=prev)
                       for cc in range(8)]
                for ec in range(12):
                    mm = None
                    for cc in range(8):
                        mm = P.op("tensor", lambda t, cc=cc, ec=ec: t.matmul(
                            ps_a[:, :], sb_W[:, cc * 1536 + ec * 128:cc * 1536 + ec * 128 + 128],
                            sb_h[:, cc * 512:(cc + 1) * 512],
                            start=(cc == 0), stop=(cc == 7)),
                            waits=w_toks + t_h + [P.note(ps_a)])
                    if ec >= 8:
                        t_e = P.op("vector", lambda v: v.tensor_copy(sb_out[:, :], ps_a[:, :]),
                                   waits=[mm, P.note(sb_out)])
                        P.setn(ps_a, t_e)
                        t_d = P.dma("gpsimd_q", qkvT_d[ec * 128:(ec + 1) * 128, t0c:t0c + 512],
                                    sb_out[:, :], waits=[t_e])
                        P.setn(sb_out, t_d)
                        prev = [t_d]
                        continue
                    qk = 0 if ec < 4 else 1
                    t_cp = P.op("vector", lambda v: v.tensor_copy(sb_raw[:, :], ps_a[:, :]),
                                waits=[mm, P.note(sb_raw)])
                    P.setn(ps_a, t_cp)
                    t_sq = P.op("scalar", lambda s: s.activation(sb_sq[:, :], sb_raw[:, :],
                                                                 AF.Square),
                                waits=[t_cp, P.note(sb_sq)])
                    t_nm = P.op("tensor", lambda t: t.matmul(ps_row[:, :], sb_E2[:, :],
                                                             sb_sq[:, :], start=True, stop=True),
                                waits=[t_sq] + c_toks + [P.note(ps_row)])
                    P.setn(sb_sq, t_nm)
                    t_sr = P.op("scalar", lambda s: s.activation(sb_n[:, :], ps_row[:, :],
                                                                 AF.Sqrt),
                                waits=[t_nm, P.note(sb_n)])
                    P.setn(ps_row, t_sr)
                    t_mx = P.op("vector", lambda v: v.tensor_single_scalar(
                        sb_n[:, :], sb_n[:, :], 1e-12, ALU.max), waits=[t_sr])
                    t_rc = P.op("vector", lambda v: v.reciprocal(sb_n[:, :], sb_n[:, :]),
                                waits=[t_mx])
                    t_nb = P.op("vector", lambda v: v.tensor_copy(sb_nb[:, :], sb_n[:, :]),
                                waits=[t_rc, P.note(sb_nb)])
                    t_rp = P.op("tensor", lambda t: t.matmul(ps_b[:, :], sb_E2T[:, :],
                                                             sb_nb[:, :], start=True, stop=True),
                                waits=[t_nb, P.note(ps_b)])
                    P.setn(sb_nb, t_rp)
                    t_q1 = P.op("vector", lambda v: v.tensor_tensor(sb_raw[:, :], sb_raw[:, :],
                                                                    ps_b[:, :], ALU.mult),
                                waits=[t_rp, t_cp])
                    P.setn(ps_b, t_q1)
                    t_q2 = P.op("vector", lambda v, qk=qk: v.tensor_scalar(
                        sb_raw[:, :], sb_raw[:, :], qs_c[:, qk:qk + 1], None, ALU.mult),
                        waits=[t_q1] + qs_toks)
                    t_sq2 = P.op("scalar", lambda s: s.activation(sb_sq[:, :], sb_raw[:, :],
                                                                  AF.Square),
                                 waits=[t_q2, P.note(sb_sq)])
                    t_nm2 = P.op("tensor", lambda t: t.matmul(ps_row[:, :], sb_E2[:, :],
                                                              sb_sq[:, :], start=True,
                                                              stop=True),
                                 waits=[t_sq2, P.note(ps_row)])
                    P.setn(sb_sq, t_nm2)
                    t_nc2 = P.op("vector", lambda v: v.tensor_copy(sb_n[:, :], ps_row[:, :]),
                                 waits=[t_nm2, P.note(sb_n)])
                    P.setn(ps_row, t_nc2)
                    P.setn(sb_n, t_nc2)
                    nd = qn2_d if qk == 0 else kn2_d
                    hh2 = (ec % 4) * 2
                    t_nd = P.dma("sync", nd[hh2:hh2 + 2, t0c:t0c + 512], sb_n[:, :],
                                 waits=[t_nc2])
                    P.setn(sb_n, t_nd)
                    t_rt = P.op("tensor", lambda t: t.matmul(ps_b[:, :], sb_R2T[:, :],
                                                             sb_raw[:, :], start=True, stop=True),
                                waits=[t_q2, P.note(ps_b)] + c_toks)
                    cs_ap = c_csq if qk == 0 else c_csk
                    ss_ap = c_ssq if qk == 0 else c_ssk
                    t_c1 = P.dma("sync", sb_cs[:, :], cs_ap[:, t0c:t0c + 512],
                                 waits=[P.note(sb_cs)])
                    t_o1 = P.op("vector", lambda v: v.tensor_tensor(sb_out[:, :], sb_raw[:, :],
                                                                    sb_cs[:, :], ALU.mult),
                                waits=[t_c1, t_q2, P.note(sb_out)])
                    P.setn(sb_cs, t_o1)
                    t_c2 = P.dma("sync", sb_cs[:, :], ss_ap[:, t0c:t0c + 512], waits=[t_o1])
                    t_rc2 = P.op("vector", lambda v: v.tensor_copy(sb_rot[:, :], ps_b[:, :]),
                                 waits=[t_rt, P.note(sb_rot)])
                    P.setn(ps_b, t_rc2)
                    t_o2 = P.op("vector", lambda v: v.tensor_tensor(sb_rot[:, :], sb_rot[:, :],
                                                                    sb_cs[:, :], ALU.mult),
                                waits=[t_c2, t_rc2])
                    P.setn(sb_cs, t_o2)
                    t_o3 = P.op("vector", lambda v: v.tensor_tensor(sb_out[:, :], sb_out[:, :],
                                                                    sb_rot[:, :], ALU.add),
                                waits=[t_o2, t_o1])
                    P.setn(sb_rot, t_o3)
                    t_d = P.dma("gpsimd_q", qkvT_d[ec * 128:(ec + 1) * 128, t0c:t0c + 512],
                                sb_out[:, :], waits=[t_o3])
                    P.setn(sb_out, t_d)
                    prev = [t_d]
        _p2()
        P.barrier()

        def _p3():
          with P.scope():
            H8 = 8
            sb_khat = P.sbuf("l03_khat", [65, T], BF16)
            sb_qhat = P.sbuf("l03_qhat", [65, T], BF16)
            sb_vth = P.sbuf("l03_vth", [64, T], BF16)
            sb_vhat = P.sbuf("l03_vhat", [128, 32 * 65], BF16)
            sb_p = P.sbuf("l03_p", [128, 2 * 512], BF16)
            sb_oc = P.sbuf("l03_oc", [65, 512], F32)
            sb_ocb = P.sbuf("l03_ocb", [65, 512], BF16)
            sb_sq = P.sbuf("l03_sq", [65, 512], BF16)
            sb_nn = P.sbuf("l03_nn", [1, 512], F32)
            sb_nnb = P.sbuf("l03_nnb", [1, 512], BF16)
            sb_km = P.sbuf("l03_km", [1, 1], F32)
            sb_kmt = P.sbuf("l03_kmt", [1, 1], F32)
            onesc65 = P.sbuf("l03_o65", [65, 1], BF16)
            t_o65 = P.op("vector", lambda v: v.memset(onesc65[:, :], 1.0))
            prev_h = [t_o65]
            for h in range(H8):
                hb = 64 * h
                t_k = P.dma("sync", sb_khat[0:64, :], qkvT_d[512 + hb:512 + hb + 64, :],
                            waits=prev_h)
                t_k1 = P.op("vector", lambda v: v.memset(sb_khat[64:65, :], 1.0), waits=prev_h)
                t_q = P.dma("sync", sb_qhat[0:64, :], qkvT_d[hb:hb + 64, :], waits=prev_h)
                t_v = P.dma("sync", sb_vth[:, :], qkvT_d[1024 + hb:1024 + hb + 64, :],
                            waits=prev_h)
                t_vm = P.op("vector", lambda v: v.memset(sb_vhat[:, :], 1.0), waits=prev_h)
                sb_qn = P.sbuf(f"l03_qn{h}", [1, T], F32) if False else sb_nn
                sb_knrow = P.sbuf(f"l03_kn{h}", [1, T], F32) if False else None
                t_knl = P.dma("sync", sb_oc[0:1, :].bitcast(F32)[0:1, 0:512] if False
                              else sb_nn[:, :], kn2_d[h:h + 1, 0:512], waits=prev_h) \
                    if False else None
                km_toks = []
                t_kz = P.op("vector", lambda v: v.memset(sb_km[:, :], 0.0), waits=prev_h)
                for tc2 in range(T // 512):
                    t_knl = P.dma("sync", sb_nn[:, :], kn2_d[h:h + 1, tc2 * 512:(tc2 + 1) * 512],
                                  waits=prev_h + [P.note(sb_nn)] + km_toks[-1:])
                    t_mx = P.op("vector", lambda v: v.tensor_reduce(
                        sb_kmt[:, :], sb_nn[:, :], AX.X, ALU.max), waits=[t_knl])
                    P.setn(sb_nn, t_mx)
                    t_km2 = P.op("vector", lambda v: v.tensor_max(sb_km[:, :], sb_km[:, :],
                                                                  sb_kmt[:, :]),
                                 waits=[t_mx, t_kz])
                    km_toks.append(t_km2)
                t_ksq = P.op("scalar", lambda s: s.activation(sb_km[:, :], sb_km[:, :],
                                                              AF.Sqrt), waits=km_toks)
                nm_toks = []
                for tc2 in range(T // 512):
                    t_qnl = P.dma("sync", sb_nn[:, :], qn2_d[h:h + 1, tc2 * 512:(tc2 + 1) * 512],
                                  waits=[P.note(sb_nn), t_ksq] + nm_toks[-1:])
                    t_sr = P.op("scalar", lambda s: s.activation(sb_nn[:, :], sb_nn[:, :],
                                                                 AF.Sqrt), waits=[t_qnl])
                    t_ng = P.op("vector", lambda v, tc2=tc2: v.tensor_scalar(
                        sb_qhat[64:65, tc2 * 512:(tc2 + 1) * 512], sb_nn[:, :], sb_km[0:1, 0:1],
                        -8.0, ALU.mult, ALU.mult), waits=[t_sr])
                    P.setn(sb_nn, t_ng)
                    nm_toks.append(t_ng)
                vtoks = []
                for jt in range(T // 128):
                    tT = P.op("tensor", lambda t, jt=jt: t.transpose(
                        ps_trb[:, 0:64], sb_vth[:, jt * 128:(jt + 1) * 128], idb[0:64, 0:64]),
                        waits=[t_v, t_vm, P.note(ps_trb)])
                    tC = P.op("vector", lambda v, jt=jt: v.tensor_copy(
                        sb_vhat[:, jt * 65:jt * 65 + 64], ps_trb[:, 0:64]), waits=[tT])
                    P.setn(ps_trb, tC)
                    vtoks.append(tC)
                head_ready = [t_k1] + nm_toks + vtoks[-1:]
                o_toks = []
                for wdw in range(T // 512):
                    i0 = wdw * 512
                    jts = list(range((wdw - 1) * 4, wdw * 4 + 4)) if wdw > 0 else list(range(4))
                    mm_o_last = None
                    exp_toks = []
                    for idx_j, jt in enumerate(jts):
                        pb = idx_j % 2
                        mpat = idx_j if wdw > 0 else 4 + idx_j
                        t_s = P.op("tensor", lambda t, jt=jt, i0=i0, pb=pb: t.matmul(
                            ps_a[:, :] if pb == 0 else ps_b[:, :],
                            sb_khat[:, jt * 128:(jt + 1) * 128],
                            sb_qhat[:, i0:i0 + 512], start=True, stop=True),
                            waits=head_ready + exp_toks[-2:-1] + o_toks)
                        bank = (lambda pb=pb: ps_a if pb == 0 else ps_b)()
                        t_m = P.op("vector", lambda v, bank=bank, mpat=mpat: v.tensor_tensor(
                            bank[:, :], bank[:, :], sb_msk[:, mpat * 512:(mpat + 1) * 512],
                            ALU.add), waits=[t_s] + c_toks)
                        t_e = P.op("scalar", lambda sc, bank=bank, pb=pb: sc.activation(
                            sb_p[:, pb * 512:(pb + 1) * 512], bank[:, :], AF.Exp),
                            waits=[t_m] + ([mm_o_last] if mm_o_last else []))
                        exp_toks.append(t_e)
                        mm_o_last = P.op("tensor", lambda t, jt=jt, pb=pb, idx_j=idx_j,
                                         njt=len(jts): t.matmul(
                            ps_o[:, :], sb_vhat[:, jt * 65:(jt + 1) * 65],
                            sb_p[:, pb * 512:(pb + 1) * 512],
                            start=(idx_j == 0), stop=(idx_j == njt - 1)), waits=[t_e])
                    t_oc = P.op("vector", lambda v: v.tensor_copy(sb_oc[:, :], ps_o[:, :]),
                                waits=[mm_o_last, P.note(sb_oc)])
                    t_rc = P.op("vector", lambda v: v.reciprocal(sb_nn[:, :], sb_oc[64:65, :]),
                                waits=[t_oc])
                    t_nb = P.op("vector", lambda v: v.tensor_copy(sb_nnb[:, :],
                                                                  sb_nn[:, :]),
                                waits=[t_rc, P.note(sb_nnb)])
                    t_rp = P.op("tensor", lambda t: t.matmul(
                        ps_a[0:64, :], ones1[0:1, 0:64], sb_nnb[:, :], start=True, stop=True),
                        waits=[t_nb, P.note(ps_a)] + c_toks)
                    P.setn(sb_nnb, t_rp)
                    t_dv = P.op("vector", lambda v: v.tensor_tensor(
                        sb_ocb[0:64, :], sb_oc[0:64, :], ps_a[0:64, :], ALU.mult),
                        waits=[t_rp, t_oc])
                    P.setn(ps_a, t_dv)
                    P.setn(sb_oc, t_dv)
                    t_d = P.dma("gpsimd_q", attT_d[hb:hb + 64, i0:i0 + 512], sb_ocb[0:64, :],
                                waits=[t_dv])
                    P.setn(sb_ocb, t_d)
                    o_toks = [t_dv, t_d]
                    prev_h = [t_dv, t_d]
        _p3()
        P.barrier()

        def _p4():
          with P.scope():
            sb_Wo = P.sbuf("l04_Wo", [128, 4 * C], BF16)
            wo_toks = [P.dma("gpsimd_q", sb_Wo[:, dc * C:(dc + 1) * C],
                             w["wo"][ly, dc * 128:(dc + 1) * 128, :]) for dc in range(4)]
            sb_at = P.sbuf("l04_at", [128, 4 * 512], BF16)
            sb_outT = P.sbuf("l04_outT", [128, 8 * 512], BF16)
            sb_tok = P.sbuf("l04_tok", [128, C], BF16)
            sb_z = P.sbuf("l04_z", [128, C], F32)
            prev = []
            prev_ev = None
            for tc in range(T // 512):
                t0c = tc * 512
                t_a = [P.dma("sync", sb_at[:, dc * 512:(dc + 1) * 512],
                             attT_d[dc * 128:(dc + 1) * 128, t0c:t0c + 512], waits=prev)
                       for dc in range(4)]
                ev = []
                for cc in range(8):
                    mm = None
                    for dc in range(4):
                        mm = P.op("tensor", lambda t, cc=cc, dc=dc: t.matmul(
                            ps_a[:, :], sb_Wo[:, dc * C + cc * 128:dc * C + cc * 128 + 128],
                            sb_at[:, dc * 512:(dc + 1) * 512],
                            start=(dc == 0), stop=(dc == 3)),
                            waits=wo_toks + t_a + [prev_ev])
                    t_e = P.op("vector", lambda v, cc=cc: v.tensor_copy(
                        sb_outT[:, cc * 512:(cc + 1) * 512], ps_a[:, :]), waits=[mm])
                    prev_ev = t_e
                    ev.append(t_e)
                for sub in range(4):
                    tb = t0c + sub * 128
                    trt = []
                    for cc in range(8):
                        tT = P.op("tensor", lambda t, cc=cc, sub=sub: t.transpose(
                            ps_trb[:, :],
                            sb_outT[:, cc * 512 + sub * 128:cc * 512 + sub * 128 + 128],
                            idb[:]), waits=ev + [P.note(ps_trb)])
                        tC = P.op("vector", lambda v, cc=cc: v.tensor_copy(
                            sb_tok[:, cc * 128:(cc + 1) * 128], ps_trb[:, :]),
                            waits=[tT, P.note(sb_tok)])
                        P.setn(ps_trb, tC)
                        trt.append(tC)
                    t_z = P.dma("sync", sb_z[:], zin[tb:tb + 128, :], waits=prev)
                    t_ad = P.op("vector", lambda v: v.tensor_tensor(sb_z[:], sb_z[:],
                                                                    sb_tok[:], ALU.add),
                                waits=trt + [t_z])
                    P.setn(sb_tok, t_ad)
                    t_st = P.dma("gpsimd_q", zout[tb:tb + 128, :], sb_z[:], waits=[t_ad])
                    prev = [t_st]
        _p4()
        P.barrier()

        def _p5():
          with P.scope():
            g1c = P.sbuf("l05_g1c", [128, 8], F32)
            b1c = P.sbuf("l05_b1c", [128, 8], F32)
            t_g = emit_col_load(P, w["ln_g"][ly, :], C, g1c, idf, ps_col, waits=[t_id])
            t_b = emit_col_load(P, w["ln_b"][ly, :], C, b1c, idf, ps_col)
            sb_shf = P.sbuf("l05_shf", [128, C], F32)
            sb_ln = P.sbuf("l05_ln", [128, C], BF16)
            sb_t = P.sbuf("l05_t", [128, 128], BF16)
            prev = [t_g, t_b]
            prev_ln = None
            for tt in range(T // 128):
                tb = tt * 128
                t_1 = P.dma("sync", sb_shf[:, 0:512], zout[tb:tb + 128, 0:512],
                            waits=prev + [prev_ln])
                if tt == 0:
                    t_zz = P.op("vector", lambda v: v.memset(sb_shf[0:1, 512:1024], 0.0),
                                waits=prev + [prev_ln, t_1])
                    t_2 = P.dma("sync", sb_shf[1:128, 512:1024], zout[0:127, 512:1024],
                                waits=prev)
                    pre = [t_1, t_2, t_zz]
                else:
                    t_2 = P.dma("sync", sb_shf[:, 512:1024], zout[tb - 1:tb + 127, 512:1024],
                                waits=prev)
                    pre = [t_1, t_2]
                t_ln = emit_ln(P, sb_shf[:], 128, sb_ln[:], lnscr, waits=pre)
                prev_ln = t_ln
                toks = []
                for cc in range(8):
                    tT = P.op("tensor", lambda t, cc=cc: t.transpose(
                        ps_trb[:, :], sb_ln[:, cc * 128:(cc + 1) * 128], idb[:]),
                        waits=[t_ln, P.note(ps_trb)])
                    tC = P.op("vector", lambda v, cc=cc: v.tensor_scalar(
                        sb_t[:, :], ps_trb[:, :], g1c[:, cc:cc + 1], b1c[:, cc:cc + 1],
                        ALU.mult, ALU.add), waits=[tT, P.note(sb_t)])
                    P.setn(ps_trb, tC)
                    tD = P.dma("sync", h2T_d[cc * 128:(cc + 1) * 128, tb:tb + 128],
                               sb_t[:, :], waits=[tC])
                    P.setn(sb_t, tD)
                    toks.append(tD)
                prev = [toks[-1]]
        _p5()
        P.barrier()

        def _p6():
          with P.scope():
            NF = (HID2 + 511) // 512
            sb_W1 = P.sbuf("l06_W1", [128, 8 * HID2], BF16)
            w1_toks = [P.dma("gpsimd_q", sb_W1[:, cc * HID2:(cc + 1) * HID2],
                             w["w1"][ly, cc * 128:(cc + 1) * 128, :]) for cc in range(8)]
            KC = (HID + 127) // 128
            sb_W2 = P.sbuf("l06_W2", [128, KC * C], BF16)
            w2_toks = []
            for kc in range(KC):
                r0 = kc * 128
                rw = min(128, HID - r0)
                w2_toks.append(P.dma("gpsimd_q", sb_W2[0:rw, kc * C:(kc + 1) * C],
                                     w["w2"][ly, r0:r0 + rw, :]))
            sb_b1rep = P.sbuf("l06_b1rep", [128, HID2], BF16)
            sb_b2rep = P.sbuf("l06_b2rep", [128, C], F32)
            with P.scope():
                sb_b1r = P.sbuf("l06_b1r", [1, HID2], BF16)
                sb_b2r = P.sbuf("l06_b2r", [1, C], BF16)
                t_b1 = P.dma("gpsimd_q", sb_b1r[:, :], w["b1"][ly:ly + 1, :])
                t_b2 = P.dma("gpsimd_q", sb_b2r[:, :], w["b2"][ly:ly + 1, :])
                rep_toks = []
                for fc in range(NF):
                    f0 = fc * 512
                    fw = min(512, HID2 - f0)
                    tR = P.op("tensor", lambda t, f0=f0, fw=fw: t.matmul(
                        ps_a[:, 0:fw], ones1[0:1, :], sb_b1r[0:1, f0:f0 + fw],
                        start=True, stop=True), waits=[t_b1, P.note(ps_a)] + c_toks)
                    tC = P.op("vector", lambda v, f0=f0, fw=fw: v.tensor_copy(
                        sb_b1rep[:, f0:f0 + fw], ps_a[:, 0:fw]), waits=[tR])
                    P.setn(ps_a, tC)
                    rep_toks.append(tC)
                for half in range(2):
                    tR = P.op("tensor", lambda t, half=half: t.matmul(
                        ps_a[:, :], ones1[0:1, :], sb_b2r[0:1, half * 512:(half + 1) * 512],
                        start=True, stop=True), waits=[t_b2, P.note(ps_a)] + c_toks)
                    tC = P.op("vector", lambda v, half=half: v.tensor_copy(
                        sb_b2rep[:, half * 512:(half + 1) * 512], ps_a[:, :]), waits=[tR])
                    P.setn(ps_a, tC)
                    rep_toks.append(tC)
            sb_h2 = P.sbuf("l06_h2", [128, 8 * 128], BF16)
            sb_h1 = P.sbuf("l06_h1", [128, HID2], BF16)
            sb_gg = P.sbuf("l06_gg", [128, HID], BF16)
            sb_hT = P.sbuf("l06_hT", [128, KC * 128], BF16)
            sb_z = P.sbuf("l06_z", [128, C], F32)
            sb_o = P.sbuf("l06_o", [128, C], F32)
            prev = rep_toks
            prev_st = None
            for tt in range(T // 128):
                tb = tt * 128
                t_h = [P.dma("sync", sb_h2[:, cc * 128:(cc + 1) * 128],
                             h2T_d[cc * 128:(cc + 1) * 128, tb:tb + 128], waits=prev)
                       for cc in range(8)]
                g_toks = []
                for fc in range(NF):
                    f0 = fc * 512
                    fw = min(512, HID2 - f0)
                    mm = None
                    for cc in range(8):
                        mm = P.op("tensor", lambda t, cc=cc, f0=f0, fw=fw: t.matmul(
                            ps_b[:, 0:fw], sb_h2[:, cc * 128:(cc + 1) * 128],
                            sb_W1[:, cc * HID2 + f0:cc * HID2 + f0 + fw],
                            start=(cc == 0), stop=(cc == 7)),
                            waits=w1_toks + t_h + g_toks[-1:] + [P.note(ps_b)])
                    t_e = P.op("vector", lambda v, f0=f0, fw=fw: v.tensor_tensor(
                        sb_h1[:, f0:f0 + fw], ps_b[:, 0:fw], sb_b1rep[:, f0:f0 + fw],
                        ALU.add), waits=[mm, P.note(sb_h1) if fc == 0 else None])
                    P.setn(ps_b, t_e)
                    g_toks.append(t_e)
                P.setn(sb_h1, g_toks[-1])
                t_gl = P.op("scalar", lambda s: s.activation(sb_gg[:, :], sb_h1[:, HID:HID2],
                                                             AF.Gelu), waits=g_toks + [P.note(sb_gg)])
                t_mu = P.op("vector", lambda v: v.tensor_tensor(sb_gg[:, :], sb_gg[:, :],
                                                                sb_h1[:, 0:HID], ALU.mult),
                            waits=[t_gl])
                t_gb = t_mu
                P.setn(sb_gg, t_gb)
                tr_toks = []
                for kc in range(KC):
                    r0 = kc * 128
                    rw = min(128, HID - r0)
                    tT = P.op("tensor", lambda t, r0=r0, rw=rw: t.transpose(
                        ps_trb[0:rw, :], sb_gg[:, r0:r0 + rw], idb[:]),
                        waits=[t_gb, P.note(ps_trb)])
                    tC = P.op("vector", lambda v, kc=kc, rw=rw: v.tensor_copy(
                        sb_hT[0:rw, kc * 128:(kc + 1) * 128], ps_trb[0:rw, :]),
                        waits=[tT, P.note(sb_hT) if kc == 0 else None])
                    P.setn(ps_trb, tC)
                    tr_toks.append(tC)
                P.setn(sb_hT, tr_toks[-1])
                o_toks = []
                for ci in range(2):
                    mm = None
                    for kc in range(KC):
                        rw = min(128, HID - kc * 128)
                        mm = P.op("tensor", lambda t, kc=kc, ci=ci, rw=rw: t.matmul(
                            ps_a[:, :], sb_hT[0:rw, kc * 128:(kc + 1) * 128],
                            sb_W2[0:rw, kc * C + ci * 512:kc * C + ci * 512 + 512],
                            start=(kc == 0), stop=(kc == KC - 1)),
                            waits=tr_toks + w2_toks + o_toks[-1:] + [P.note(ps_a)])
                    t_oo = P.op("vector", lambda v, ci=ci: v.tensor_tensor(
                        sb_o[:, ci * 512:(ci + 1) * 512], ps_a[:, :],
                        sb_b2rep[:, ci * 512:(ci + 1) * 512], ALU.add),
                        waits=[mm, P.note(sb_o) if ci == 0 else None])
                    P.setn(ps_a, t_oo)
                    o_toks.append(t_oo)
                P.setn(sb_o, o_toks[-1])
                t_z = P.dma("sync", sb_z[:], zout[tb:tb + 128, :], waits=[prev_st])
                t_ad = P.op("vector", lambda v: v.tensor_tensor(sb_z[:], sb_z[:], sb_o[:],
                                                                ALU.add),
                            waits=o_toks + [t_z, prev_st])
                P.setn(sb_o, t_ad)
                prev_st = P.dma("gpsimd_q", zin[tb:tb + 128, :], sb_z[:], waits=[t_ad])
                prev = [prev_st]
        _p6()
        P.barrier()
    def _phF():
      with P.scope():
        sb_x = P.sbuf("l0F_x", [128, C], F32)
        sb_z = P.sbuf("l0F_z", [128, C], F32)
        prev_st = None
        for tt in range(T // 128):
            tb = tt * 128
            t_x = P.dma("sync", sb_x[:], xin[tb:tb + 128, :], waits=[prev_st])
            t_z = P.dma("sync", sb_z[:], z_a[tb:tb + 128, :], waits=[prev_st])
            t_o = P.op("vector", lambda v: v.scalar_tensor_tensor(
                sb_x[:], sb_z[:], coef, sb_x[:], ALU.mult, ALU.add),
                waits=[t_x, t_z, prev_st])
            prev_st = P.dma("gpsimd_q", xout[tb:tb + 128, :], sb_x[:], waits=[t_o])
        return prev_st
    return _phF()


def build_l0ffn_launch():
    nc = bass.Bass("TRN2", target_bir_lowering=False)
    xin = nc.dram_tensor("x", [T, C], F32, kind="ExternalInput").ap()
    names = {"wqkv": [2, 1024, 1536], "q_scale": [2, 64], "k_scale": [2, 64],
             "wo": [2, 512, 1024], "ln_g": [2, 1024], "ln_b": [2, 1024],
             "w1": [2, 1024, 5460], "b1": [2, 5460], "w2": [2, 2730, 1024],
             "b2": [2, 1024]}
    w = {k: nc.dram_tensor(k, sh, F32, kind="ExternalInput").ap() for k, sh in names.items()}
    xout = nc.dram_tensor("xo", [T, C], F32, kind="ExternalOutput").ap()
    P = Prog(nc)
    tok = stage_l0ffn(P, xin, xout, w)
    P.finalize(final_waits=[tok])
    return nc



L3CAP = 384


def emit_gelu_tanh(P, out_ap, x_sb, s1, s2, waits=()):
    t1 = P.op("vector", lambda v: v.tensor_tensor(s1, x_sb, x_sb, ALU.mult), waits=list(waits))
    t2 = P.op("vector", lambda v: v.tensor_tensor(s1, s1, x_sb, ALU.mult), waits=[t1])
    t3 = P.op("vector", lambda v: v.tensor_single_scalar(s1, s1, 0.044715, ALU.mult), waits=[t2])
    t4 = P.op("vector", lambda v: v.tensor_tensor(s1, s1, x_sb, ALU.add), waits=[t3])
    t5 = P.op("scalar", lambda s: s.activation(s1, s1, AF.Tanh, scale=0.7978845608028654),
              waits=[t4])
    t6 = P.op("vector", lambda v: v.tensor_tensor(s2, x_sb, s1, ALU.mult), waits=[t5])
    t7 = P.op("vector", lambda v: v.tensor_tensor(s2, s2, x_sb, ALU.add), waits=[t6])
    t8 = P.op("vector", lambda v: v.tensor_scalar(out_ap, s2, 0.5, None, ALU.mult), waits=[t7])
    return t8


def stage_l3ffn(P, xin, xout, w, rg=None, coef=0.5):
    E3, K6, F3, SF = 64, 6, 512, 2048
    CAP = L3CAP
    NT = T // 128
    NCH = T // 512
    XSR = E3 * CAP + 128
    if rg is None:
        rg = [[0, 1, 2, 3, 4, 5, 6, 7]]
    idf, idb, t_id = make_identity(P)
    P.constf(1e-5)
    ps_trb = P.psum("l3f_trb", [128, 128], BF16)

    xln_d = P.dram("l3f_xln", [T, C], BF16)
    xlnT_d = P.dram("l3f_xlnT", [C, T], BF16)
    probs_d = P.dram("l3f_probs", [T, K6], F32)
    topif_d = P.dram("l3f_topif", [T, K6], F32)
    idxrows_d = P.dram("l3f_idxrows", [K6, T], F32)
    rk_d = P.dram("l3f_rk", [K6, T], F32)
    bs_d = P.dram("l3f_bs", [K6, T], F32)
    pairw_d = P.dram("l3f_pairw", [T, K6], F32)
    sslot_d = P.dram("l3f_sslot", [T, K6], U32)
    gslot_d = P.dram("l3f_gslot", [T, K6], U32)
    xs_d = P.dram("l3f_xs", [XSR, C], BF16)
    y_d = P.dram("l3f_y", [E3 * CAP + 1, C], BF16)
    agin_d = P.dram("l3f_agin", [1, E3], F32)
    agout_d = P.dram("l3f_agout", [8, E3], F32)

    lnscr = {
        "sq": P.sbuf("l3fln_sq", [128, C], F32),
        "m": P.sbuf("l3fln_m", [128, 1], F32),
        "ss": P.sbuf("l3fln_ss", [128, 1], F32),
        "var": P.sbuf("l3fln_var", [128, 1], F32),
        "rstd": P.sbuf("l3fln_rstd", [128, 1], F32),
        "msq": P.sbuf("l3fln_msq", [128, 1], F32),
    }

    def _phA():
      with P.scope():
        sb_rwT = P.sbuf("l3fa_rwT", [128, 8 * E3], BF16)
        rw_toks = [P.dma("gpsimd_q", sb_rwT[:, cc * E3:(cc + 1) * E3],
                         w["rwT"][cc * 128:(cc + 1) * 128, :]) for cc in range(8)]
        sb_ebr = P.sbuf("l3fa_ebr", [1, E3], F32)
        sb_on1 = P.sbuf("l3fa_on1", [1, 128], F32)
        sb_ebrep = P.sbuf("l3fa_ebrep", [128, E3], F32)
        ps_eb = P.psum("l3fa_pseb", [128, E3], F32)
        t_eb = P.dma("sync", sb_ebr[:, :], w["ebias"])
        t_o1 = P.op("vector", lambda v: v.memset(sb_on1[:, :], 1.0))
        t_em = P.op("tensor", lambda t: t.matmul(ps_eb[:, :], sb_on1[:, :], sb_ebr[:, :],
                                                 start=True, stop=True), waits=[t_eb, t_o1])
        t_ec = P.op("vector", lambda v: v.tensor_copy(sb_ebrep[:, :], ps_eb[:, :]), waits=[t_em])
        sb_ii = P.sbuf("l3fa_ii", [128, E3], I32)
        sb_iot = P.sbuf("l3fa_iot", [128, E3], F32)
        t_i1 = P.op("gpsimd", lambda g: g.iota(sb_ii[:, :], pattern=[[1, E3]], base=0,
                                               channel_multiplier=0))
        t_i2 = P.op("vector", lambda v: v.tensor_copy(sb_iot[:, :], sb_ii[:, :]), waits=[t_i1])

        sb_x = P.sbuf("l3fa_x", [128, C], F32)
        sb_ln = P.sbuf("l3fa_ln", [128, C], BF16)
        sb_lnT = P.sbuf("l3fa_lnT", [128, 8 * 512], BF16)
        sb_sc = P.sbuf("l3fa_sc", [64, 512], F32)
        sb_scT = P.sbuf("l3fa_scT", [128, E3], F32)
        sb_st = P.sbuf("l3fa_st", [128, E3], F32)
        sb_m8 = P.sbuf("l3fa_m8", [128, 8], F32)
        sb_ix8 = P.sbuf("l3fa_ix8", [128, 8], U32)
        sb_ixf = P.sbuf("l3fa_ixf", [128, K6], F32)
        sb_oh = P.sbuf("l3fa_oh", [128, E3], F32)
        sb_pr = P.sbuf("l3fa_pr", [128, K6], F32)
        sb_sum = P.sbuf("l3fa_sum", [128, 1], F32)
        sb_ir6 = P.sbuf("l3fa_ir6", [K6, 128], F32)
        ps_lg = P.psum("l3fa_pslg", [64, 512], F32)
        ps_sct = P.psum("l3fa_psct", [128, E3], F32)
        ps_ir = P.psum("l3fa_psir", [K6, 128], F32)
        prev = [t_ec, t_i2] + rw_toks
        prev_tr = []
        for ch in range(NCH):
            t0c = ch * 512
            tr_toks = []
            ln_toks = []
            for sub in range(4):
                tb = t0c + sub * 128
                t_l = P.dma("sync", sb_x[:], xin[tb:tb + 128, :], waits=prev + ln_toks[-1:])
                t_ln = emit_ln(P, sb_x[:], 128, sb_ln[:], lnscr,
                               waits=[t_l] + prev_tr[-1:] + ln_toks[-1:])
                ln_toks.append(P.dma("gpsimd_q", xln_d[tb:tb + 128, :], sb_ln[:],
                                     waits=[t_ln]))
                sub_tr = []
                for cc in range(8):
                    tT = P.op("tensor", lambda t, cc=cc: t.transpose(
                        ps_trb[:, :], sb_ln[:, cc * 128:(cc + 1) * 128], idb[:]),
                        waits=[t_ln, t_id, P.note(ps_trb)])
                    tC = P.op("vector", lambda v, cc=cc, sub=sub: v.tensor_copy(
                        sb_lnT[:, cc * 512 + sub * 128:cc * 512 + sub * 128 + 128],
                        ps_trb[:, :]), waits=[tT])
                    P.setn(ps_trb, tC)
                    tr_toks.append(tC)
                    sub_tr.append(tC)
                prev_tr = [sub_tr[-1]]
            xt_toks = [P.dma("scalar_q", xlnT_d[cc * 128:(cc + 1) * 128, t0c:t0c + 512],
                             sb_lnT[:, cc * 512:(cc + 1) * 512], waits=tr_toks)
                       for cc in range(8)]
            mm = None
            for cc in range(8):
                mm = P.op("tensor", lambda t, cc=cc: t.matmul(
                    ps_lg[:, :], sb_rwT[:, cc * E3:(cc + 1) * E3],
                    sb_lnT[:, cc * 512:(cc + 1) * 512],
                    start=(cc == 0), stop=(cc == 7)),
                    waits=tr_toks + rw_toks + [P.note(ps_lg)])
            t_sg = P.op("scalar", lambda s: s.activation(sb_sc[:, :], ps_lg[:, :], AF.Sigmoid),
                        waits=[mm, P.note(sb_sc)])
            P.setn(ps_lg, t_sg)
            sel_toks = []
            for sub in range(4):
                tb = t0c + sub * 128
                tT = P.op("tensor", lambda t, sub=sub: t.transpose(
                    ps_sct[:, :], sb_sc[:, sub * 128:(sub + 1) * 128], idf[0:64, 0:64]),
                    waits=[t_sg, t_id, P.note(ps_sct)])
                tC = P.op("vector", lambda v: v.tensor_copy(sb_scT[:, :], ps_sct[:, :]),
                          waits=[tT, P.note(sb_scT)])
                P.setn(ps_sct, tC)
                t_st = P.op("vector", lambda v: v.tensor_tensor(sb_st[:, :], sb_scT[:, :],
                                                                sb_ebrep[:, :], ALU.add),
                            waits=[tC, t_ec])
                t_m8 = P.op("vector", lambda v: v.max(sb_m8[:, :], sb_st[:, :]), waits=[t_st])
                t_ix = P.op("vector", lambda v: v.max_index(sb_ix8[:, :], sb_m8[:, :],
                                                            sb_st[:, :]), waits=[t_m8])
                t_if = P.op("vector", lambda v: v.tensor_copy(sb_ixf[:, :], sb_ix8[:, 0:K6]),
                            waits=[t_ix, P.note(sb_ixf)])
                t_sm = None
                for k in range(K6):
                    t_oh = P.op("vector", lambda v, k=k: v.tensor_scalar(
                        sb_oh[:, :], sb_iot[:, :], sb_ixf[:, k:k + 1], None, ALU.is_equal),
                        waits=[t_if, t_i2, P.note(sb_oh)])
                    t_pd = P.op("vector", lambda v: v.tensor_tensor(sb_oh[:, :], sb_oh[:, :],
                                                                    sb_scT[:, :], ALU.mult),
                                waits=[t_oh])
                    t_rd = P.op("vector", lambda v, k=k: v.tensor_reduce(
                        sb_pr[:, k:k + 1], sb_oh[:, :], AX.X, ALU.add),
                        waits=[t_pd, P.note(sb_pr)])
                    P.setn(sb_oh, t_rd)
                    t_sm = t_rd
                t_s1 = P.op("vector", lambda v: v.tensor_reduce(sb_sum[:, :], sb_pr[:, :],
                                                                AX.X, ALU.add), waits=[t_sm])
                t_s2 = P.op("vector", lambda v: v.tensor_single_scalar(
                    sb_sum[:, :], sb_sum[:, :], 1e-20, ALU.add), waits=[t_s1])
                t_s3 = P.op("vector", lambda v: v.reciprocal(sb_sum[:, :], sb_sum[:, :]),
                            waits=[t_s2])
                t_s4 = P.op("vector", lambda v: v.tensor_scalar(sb_pr[:, :], sb_pr[:, :],
                                                                sb_sum[:, 0:1], None, ALU.mult),
                            waits=[t_s3])
                tD1 = P.dma("sync", probs_d[tb:tb + 128, :], sb_pr[:, :], waits=[t_s4])
                P.setn(sb_pr, tD1)
                tD2 = P.dma("sync", topif_d[tb:tb + 128, :], sb_ixf[:, :], waits=[t_if])
                tT2 = P.op("tensor", lambda t: t.transpose(ps_ir[:, :], sb_ixf[:, :],
                                                           idf[:]),
                           waits=[t_if, tD2, P.note(ps_ir)])
                P.setn(sb_ixf, tT2)
                tC2 = P.op("vector", lambda v: v.tensor_copy(sb_ir6[:, :],
                                                             ps_ir[:, :]),
                           waits=[tT2, P.note(sb_ir6)])
                P.setn(ps_ir, tC2)
                tD3 = P.dma("gpsimd_q", idxrows_d[:, tb:tb + 128], sb_ir6[:, :],
                            waits=[tC2])
                P.setn(sb_ir6, tD3)
                sel_toks += [tD1, tD2, tD3]
            prev = sel_toks + xt_toks
    _phA()
    P.barrier()

    def _phB():
      with P.scope():
        sb_M = P.sbuf("l3fb_M", [E3, T], F32)
        sb_scan = P.sbuf("l3fb_scan", [E3, T], F32)
        sb_ones = P.sbuf("l3fb_ones", [E3, T], F32)
        sb_Mk = P.sbuf("l3fb_Mk", [E3, 512], F32)
        sb_row0 = P.sbuf("l3fb_row0", [1, 512], F32)
        sb_i64 = P.sbuf("l3fb_i64i", [E3, 1], I32)
        sb_ic = P.sbuf("l3fb_ic", [E3, 1], F32)
        ones64 = P.sbuf("l3fb_o64", [E3, 1], F32)
        on1 = P.sbuf("l3fb_on1", [1, E3], F32)
        sb_cnt = P.sbuf("l3fb_cnt", [E3, 1], F32)
        sb_call = P.sbuf("l3fb_call", [8, E3], F32)
        sb_sel = P.sbuf("l3fb_sel", [8, 1], F32)
        sb_base = P.sbuf("l3fb_base", [E3, 1], F32)
        sb_brow = P.sbuf("l3fb_brow", [1, E3], F32)
        sb_racc = P.sbuf("l3fb_racc", [E3, 512], F32)
        sb_cr = P.sbuf("l3fb_cr", [1, E3], F32)
        sb_rr = P.sbuf("l3fb_rr", [1, 512], F32)
        sb_br2 = P.sbuf("l3fb_br2", [1, 512], F32)
        ps_rep = P.psum("l3fb_psrep", [E3, 512], F32)
        ps_r1 = P.psum("l3fb_psr1", [1, 512], F32)
        ps_b1 = P.psum("l3fb_psb1", [1, 512], F32)
        ps_tc = P.psum("l3fb_pstc", [1, E3], F32)
        ps_bc = P.psum("l3fb_psbc", [E3, 1], F32)
        t_z = P.op("vector", lambda v: v.memset(sb_M[:, :], 0.0))
        t_on = P.op("vector", lambda v: v.memset(sb_ones[:, :], 1.0))
        t_o64 = P.op("vector", lambda v: v.memset(ones64[:, :], 1.0))
        t_on1 = P.op("vector", lambda v: v.memset(on1[:, :], 1.0))
        t_i1 = P.op("gpsimd", lambda g: g.iota(sb_i64[:, :], pattern=[[1, 1]], base=0,
                                               channel_multiplier=1))
        t_i2 = P.op("vector", lambda v: v.tensor_copy(sb_ic[:, :], sb_i64[:, :]), waits=[t_i1])
        t_acc = t_z
        for ch in range(NCH):
            for k in range(K6):
                t_r = P.dma("sync", sb_row0[:, :], idxrows_d[k:k + 1, ch * 512:(ch + 1) * 512],
                            waits=[P.note(sb_row0)])
                t_mm = P.op("tensor", lambda t: t.matmul(ps_rep[:, :], on1[:, 0:E3],
                                                         sb_row0[:, :], start=True, stop=True),
                            waits=[t_r, t_on1, P.note(ps_rep)])
                P.setn(sb_row0, t_mm)
                t_eq = P.op("vector", lambda v: v.tensor_scalar(
                    sb_Mk[:, :], ps_rep[:, :], sb_ic[:, 0:1], None, ALU.is_equal),
                    waits=[t_mm, t_i2, P.note(sb_Mk)])
                P.setn(ps_rep, t_eq)
                t_acc = P.op("vector", lambda v, ch=ch: v.tensor_tensor(
                    sb_M[:, ch * 512:(ch + 1) * 512], sb_M[:, ch * 512:(ch + 1) * 512],
                    sb_Mk[:, :], ALU.add), waits=[t_eq, t_acc])
                P.setn(sb_Mk, t_acc)
        t_sc = P.op("vector", lambda v: v.tensor_tensor_scan(
            sb_scan[:, :], sb_ones[:, :], sb_M[:, :], 0.0, ALU.mult, ALU.add),
            waits=[t_on, t_acc])
        t_ct = P.op("vector", lambda v: v.tensor_copy(sb_cnt[:, :], sb_scan[:, T - 1:T]),
                    waits=[t_sc])
        t_ex = P.op("vector", lambda v: v.tensor_tensor(sb_scan[:, :], sb_scan[:, :],
                                                        sb_M[:, :], ALU.subtract),
                    waits=[t_ct])
        tT = P.op("tensor", lambda t: t.transpose(ps_tc[:, :], sb_cnt[:, :], idf[0:64, 0:64]),
                  waits=[t_ct, t_id])
        t_cc = P.op("vector", lambda v: v.tensor_copy(sb_cr[:, :], ps_tc[0:1, 0:E3]),
                    waits=[tT])
        tDc = P.dma("sync", agin_d[:, :], sb_cr[:, :], waits=[t_cc])
        t_ag = P.collective("AllGather", ALU.bypass, rg, [agin_d], [agout_d], waits=[tDc])
        t_ca = P.dma("sync", sb_call[:, :], agout_d[:, :], waits=[t_ag])
        t_se = P.dma("sync", sb_sel[:, :], w["sel"])
        t_bm = P.op("tensor", lambda t: t.matmul(ps_tc[:, :], sb_sel[:, :], sb_call[:, :],
                                                 start=True, stop=True),
                    waits=[t_ca, t_se, tDc])
        t_bc = P.op("vector", lambda v: v.tensor_copy(sb_brow[:, :], ps_tc[0:1, 0:E3]),
                    waits=[t_bm])
        P.setn(ps_tc, t_bc)
        tT2 = P.op("tensor", lambda t: t.transpose(ps_bc[:, :], sb_brow[:, :], idf[0:1, 0:1]),
                   waits=[t_bc])
        t_b2 = P.op("vector", lambda v: v.tensor_copy(sb_base[:, :], ps_bc[:, :]), waits=[tT2])
        rkbs_toks = []
        for ch in range(NCH):
            t_rc = P.op("vector", lambda v, ch=ch: v.tensor_copy(
                sb_racc[:, :], sb_scan[:, ch * 512:(ch + 1) * 512]),
                waits=[t_ex, P.note(sb_racc)])
            for k in range(K6):
                t_r = P.dma("sync", sb_row0[:, :], idxrows_d[k:k + 1, ch * 512:(ch + 1) * 512],
                            waits=[P.note(sb_row0)])
                t_mm = P.op("tensor", lambda t: t.matmul(ps_rep[:, :], on1[:, 0:E3],
                                                         sb_row0[:, :], start=True, stop=True),
                            waits=[t_r, P.note(ps_rep)])
                P.setn(sb_row0, t_mm)
                t_eq = P.op("vector", lambda v: v.tensor_scalar(
                    sb_Mk[:, :], ps_rep[:, :], sb_ic[:, 0:1], None, ALU.is_equal),
                    waits=[t_mm, P.note(sb_Mk)])
                P.setn(ps_rep, t_eq)
                t_pd = P.op("vector", lambda v: v.tensor_tensor(sb_Mk[:, :], sb_Mk[:, :],
                                                                sb_racc[:, :], ALU.mult),
                            waits=[t_eq, t_rc])
                t_rm = P.op("tensor", lambda t: t.matmul(ps_r1[:, :], ones64[:, :],
                                                         sb_Mk[:, :], start=True, stop=True),
                            waits=[t_pd, t_o64, P.note(ps_r1)])
                t_rcp = P.op("vector", lambda v: v.tensor_copy(sb_rr[:, :], ps_r1[0:1, 0:512]),
                             waits=[t_rm, P.note(sb_rr)])
                P.setn(ps_r1, t_rcp)
                tD1 = P.dma("gpsimd_q", rk_d[k:k + 1, ch * 512:(ch + 1) * 512],
                            sb_rr[:, :], waits=[t_rcp])
                P.setn(sb_rr, tD1)
                t_eq2 = P.op("vector", lambda v: v.tensor_scalar(
                    sb_Mk[:, :], ps_rep[:, :], sb_ic[:, 0:1], None, ALU.is_equal),
                    waits=[t_rm, P.note(sb_Mk)])
                P.setn(ps_rep, t_eq2)
                t_bm2 = P.op("tensor", lambda t: t.matmul(ps_b1[:, :], sb_base[:, :],
                                                          sb_Mk[:, :], start=True, stop=True),
                             waits=[t_eq2, t_b2, P.note(ps_b1)])
                t_bcp = P.op("vector", lambda v: v.tensor_copy(sb_br2[:, :], ps_b1[0:1, 0:512]),
                             waits=[t_bm2, P.note(sb_br2)])
                P.setn(ps_b1, t_bcp)
                tD2 = P.dma("gpsimd_q", bs_d[k:k + 1, ch * 512:(ch + 1) * 512],
                            sb_br2[:, :], waits=[t_bcp])
                P.setn(sb_br2, tD2)
                t_ra = P.op("vector", lambda v: v.tensor_tensor(sb_racc[:, :], sb_racc[:, :],
                                                                sb_Mk[:, :], ALU.add),
                            waits=[t_bm2])
                P.setn(sb_Mk, t_ra)
                P.setn(sb_racc, t_ra)
                rkbs_toks += [tD1, tD2]
        sb_r6 = P.sbuf("l3fb_r6", [K6, 128], F32)
        sb_rk = P.sbuf("l3fb_rkt", [128, K6], F32)
        sb_bs = P.sbuf("l3fb_bst", [128, K6], F32)
        sb_ei = P.sbuf("l3fb_ei", [128, K6], F32)
        sb_prt = P.sbuf("l3fb_prt", [128, K6], F32)
        sb_pos = P.sbuf("l3fb_pos", [128, K6], F32)
        sb_val = P.sbuf("l3fb_val", [128, K6], F32)
        sb_u = P.sbuf("l3fb_u", [128, K6], F32)
        sb_sl = P.sbuf("l3fb_sl", [128, K6], F32)
        sb_tmp = P.sbuf("l3fb_tmp", [128, K6], F32)
        sb_ssu = P.sbuf("l3fb_ssu", [128, K6], U32)
        sb_gsu = P.sbuf("l3fb_gsu", [128, K6], U32)
        sb_tri = P.sbuf("l3fb_tri", [128, 1], I32)
        sb_trc = P.sbuf("l3fb_trc", [128, 1], F32)
        ps_t6 = P.psum("l3fb_pst6", [128, K6], F32)
        t_t1 = P.op("gpsimd", lambda g: g.iota(sb_tri[:, :], pattern=[[1, 1]], base=E3 * CAP,
                                               channel_multiplier=1))
        t_t2 = P.op("vector", lambda v: v.tensor_copy(sb_trc[:, :], sb_tri[:, :]), waits=[t_t1])
        for tt in range(NT):
            tb = tt * 128
            t_l1 = P.dma("sync", sb_r6[:, :], rk_d[:, tb:tb + 128],
                         waits=[P.note(sb_r6)] + rkbs_toks[-3:])
            tT3 = P.op("tensor", lambda t: t.transpose(ps_t6[:, :], sb_r6[:, :],
                                                       idf[0:K6, 0:K6]),
                       waits=[t_l1, P.note(ps_t6)])
            tC3 = P.op("vector", lambda v: v.tensor_copy(sb_rk[:, :], ps_t6[:, :]),
                       waits=[tT3, P.note(sb_rk)])
            P.setn(ps_t6, tC3)
            t_l2 = P.dma("sync", sb_r6[:, :], bs_d[:, tb:tb + 128],
                         waits=[tT3] + rkbs_toks[-3:])
            P.setn(sb_r6, t_l2)
            tT4 = P.op("tensor", lambda t: t.transpose(ps_t6[:, :], sb_r6[:, :],
                                                       idf[0:K6, 0:K6]),
                       waits=[t_l2, tC3])
            tC4 = P.op("vector", lambda v: v.tensor_copy(sb_bs[:, :], ps_t6[:, :]),
                       waits=[tT4, P.note(sb_bs)])
            P.setn(ps_t6, tC4)
            P.setn(sb_r6, tT4)
            t_l3 = P.dma("sync", sb_ei[:, :], topif_d[tb:tb + 128, :], waits=[P.note(sb_ei)])
            t_l4 = P.dma("sync", sb_prt[:, :], probs_d[tb:tb + 128, :], waits=[P.note(sb_prt)])
            t_p = P.op("vector", lambda v: v.tensor_tensor(sb_pos[:, :], sb_rk[:, :],
                                                           sb_bs[:, :], ALU.add),
                       waits=[tC3, tC4, P.note(sb_pos)])
            P.setn(sb_rk, t_p)
            t_v = P.op("vector", lambda v: v.tensor_single_scalar(
                sb_val[:, :], sb_pos[:, :], float(CAP), ALU.is_lt),
                waits=[t_p, P.note(sb_val)])
            P.setn(sb_pos, t_v)
            t_pw = P.op("vector", lambda v: v.tensor_tensor(sb_prt[:, :], sb_prt[:, :],
                                                            sb_val[:, :], ALU.mult),
                        waits=[t_v, t_l4])
            tD4 = P.dma("gpsimd_q", pairw_d[tb:tb + 128, :], sb_prt[:, :], waits=[t_pw])
            P.setn(sb_prt, tD4)
            t_sl = P.op("vector", lambda v: v.scalar_tensor_tensor(
                sb_sl[:, :], sb_ei[:, :], float(CAP), sb_rk[:, :], ALU.mult, ALU.add),
                waits=[t_l3, tC3, P.note(sb_sl)])
            P.setn(sb_ei, t_sl)
            t_u = P.op("vector", lambda v: v.tensor_scalar(sb_u[:, :], sb_val[:, :], -1.0, 1.0,
                                                           ALU.mult, ALU.add),
                       waits=[t_v, P.note(sb_u)])
            t_s1 = P.op("vector", lambda v: v.tensor_tensor(sb_sl[:, :], sb_sl[:, :],
                                                            sb_val[:, :], ALU.mult),
                        waits=[t_sl, t_v])
            P.setn(sb_val, t_s1)
            t_s2 = P.op("vector", lambda v: v.tensor_scalar(sb_tmp[:, :], sb_u[:, :],
                                                            sb_trc[:, 0:1], None, ALU.mult),
                        waits=[t_u, t_t2, P.note(sb_tmp)])
            t_s3 = P.op("vector", lambda v: v.tensor_tensor(sb_tmp[:, :], sb_tmp[:, :],
                                                            sb_sl[:, :], ALU.add),
                        waits=[t_s2, t_s1])
            t_s4 = P.op("vector", lambda v: v.tensor_copy(sb_ssu[:, :], sb_tmp[:, :]),
                        waits=[t_s3, P.note(sb_ssu)])
            tD5 = P.dma("gpsimd_q", sslot_d[tb:tb + 128, :], sb_ssu[:, :], waits=[t_s4])
            P.setn(sb_ssu, tD5)
            t_g1 = P.op("vector", lambda v: v.tensor_scalar(sb_tmp[:, :], sb_u[:, :],
                                                            float(E3 * CAP), None, ALU.mult),
                        waits=[t_s4, t_u])
            P.setn(sb_tmp, t_g1)
            t_g2 = P.op("vector", lambda v: v.tensor_tensor(sb_tmp[:, :], sb_tmp[:, :],
                                                            sb_sl[:, :], ALU.add),
                        waits=[t_g1])
            P.setn(sb_sl, t_g2)
            P.setn(sb_u, t_g2)
            t_g3 = P.op("vector", lambda v: v.tensor_copy(sb_gsu[:, :], sb_tmp[:, :]),
                        waits=[t_g2, P.note(sb_gsu)])
            P.setn(sb_tmp, t_g3)
            tD6 = P.dma("gpsimd_q", gslot_d[tb:tb + 128, :], sb_gsu[:, :], waits=[t_g3])
            P.setn(sb_gsu, tD6)
    _phB()
    P.barrier()

    def _phC():
      with P.scope():
        sb_z = P.sbuf("l3fc_z", [128, C], BF16)
        t_z = P.op("vector", lambda v: v.memset(sb_z[:, :], 0.0))
        z_toks = []
        for r in range(XSR // 128):
            z_toks.append(P.dma("sync" if r % 2 == 0 else "gpsimd_q",
                                xs_d[r * 128:(r + 1) * 128, :], sb_z[:, :], waits=[t_z]))
        t_zy = P.dma("scalar_q", y_d[E3 * CAP:E3 * CAP + 1, :], sb_z[0:1, :], waits=[t_z])
        sb_xt = P.sbuf("l3fc_xt", [128, C], BF16)
        sb_ss = P.sbuf("l3fc_ss", [128, K6], U32)
        prev = z_toks[-6:] + [t_zy]
        for tt in range(NT):
            tb = tt * 128
            t_x = P.dma("sync", sb_xt[:, :], xln_d[tb:tb + 128, :], waits=prev)
            t_s = P.dma("sync", sb_ss[:, :], sslot_d[tb:tb + 128, :], waits=prev)
            for k in range(K6):
                t_sc = P.indirect_dma(xs_d[:, :], IndirectOffsetOnAxis(sb_ss[:, k:k + 1], 0),
                                      sb_xt[:, :], None, waits=[t_x, t_s] + prev)
                prev = [t_sc]
    _phC()
    P.barrier()

    def _phD():
      with P.scope():
        sb_W1 = P.sbuf("l3fd_W1", [128, 8 * F3], BF16)
        sb_W2 = P.sbuf("l3fd_W2", [128, 4 * C], BF16)
        sb_xs = P.sbuf("l3fd_xs", [128, C], BF16)
        sb_xT = P.sbuf("l3fd_xT", [128, 8 * CAP], BF16)
        sb_hf = P.sbuf("l3fd_hf", [128, 4 * CAP], F32)
        sb_s1 = P.sbuf("l3fd_s1", [128, 4 * CAP], F32)
        sb_s2 = P.sbuf("l3fd_s2", [128, 4 * CAP], F32)
        sb_h1 = P.sbuf("l3fd_h1", [128, 4 * CAP], BF16)
        sb_y = P.sbuf("l3fd_y", [128, C], BF16)
        ps_h = P.psum("l3fd_psh", [128, CAP], F32)
        ps_y = [P.psum(f"l3fd_psy{i}", [128, 512], F32) for i in range(2)]
        NT3 = CAP // 128
        prev = []
        for e in range(E3):
            w1_toks = [P.dma("sync", sb_W1[:, cc * F3:(cc + 1) * F3],
                             w["w1b"][e, cc * 128:(cc + 1) * 128, :],
                             waits=[P.note(sb_W1)] + prev) for cc in range(8)]
            P.setn(sb_W1, w1_toks[-1])
            w2_toks = [P.dma("gpsimd_q", sb_W2[:, fc * C:(fc + 1) * C],
                             w["w2b"][e, fc * 128:(fc + 1) * 128, :],
                             waits=[P.note(sb_W2)] + prev) for fc in range(4)]
            P.setn(sb_W2, w2_toks[-1])
            xt_toks = []
            for ts3 in range(NT3):
                rb = e * CAP + ts3 * 128
                t_x = P.dma("scalar_q", sb_xs[:, :], xs_d[rb:rb + 128, :],
                            waits=[P.note(sb_xs)] + prev)
                for cc in range(8):
                    tT = P.op("tensor", lambda t, cc=cc: t.transpose(
                        ps_trb[:, :], sb_xs[:, cc * 128:(cc + 1) * 128], idb[:]),
                        waits=[t_x, t_id, P.note(ps_trb)])
                    tC = P.op("vector", lambda v, cc=cc, ts3=ts3: v.tensor_copy(
                        sb_xT[:, cc * CAP + ts3 * 128:cc * CAP + ts3 * 128 + 128],
                        ps_trb[:, :]), waits=[tT, P.note(sb_xT)])
                    P.setn(ps_trb, tC)
                    xt_toks.append(tC)
                P.setn(sb_xs, xt_toks[-1])
            P.setn(sb_xT, xt_toks[-1])
            g_toks = []
            for fc in range(4):
                mm = None
                for cc in range(8):
                    mm = P.op("tensor", lambda t, fc=fc, cc=cc: t.matmul(
                        ps_h[:, :], sb_W1[:, cc * F3 + fc * 128:cc * F3 + fc * 128 + 128],
                        sb_xT[:, cc * CAP:(cc + 1) * CAP],
                        start=(cc == 0), stop=(cc == 7)),
                        waits=xt_toks + w1_toks + [P.note(ps_h)])
                t_cp = P.op("vector", lambda v, fc=fc: v.tensor_copy(
                    sb_hf[:, fc * CAP:(fc + 1) * CAP], ps_h[:, :]),
                    waits=[mm, P.note(sb_hf)])
                P.setn(ps_h, t_cp)
                g_toks.append(t_cp)
            t_g = emit_gelu_tanh(P, sb_h1[:, :], sb_hf[:, :], sb_s1[:, :], sb_s2[:, :],
                                 waits=g_toks + [P.note(sb_h1), P.note(sb_s1), P.note(sb_s2)])
            P.setn(sb_hf, t_g)
            P.setn(sb_s1, t_g)
            P.setn(sb_s2, t_g)
            P.setn(sb_h1, t_g)
            for ts3 in range(NT3):
                rb = e * CAP + ts3 * 128
                o_toks = []
                for hh in range(2):
                    mm = None
                    for fc in range(4):
                        mm = P.op("tensor", lambda t, hh=hh, fc=fc, ts3=ts3: t.matmul(
                            ps_y[hh][:, :], sb_h1[:, fc * CAP + ts3 * 128:fc * CAP + ts3 * 128 + 128],
                            sb_W2[:, fc * C + hh * 512:fc * C + hh * 512 + 512],
                            start=(fc == 0), stop=(fc == 3)),
                            waits=[t_g, P.note(ps_y[hh])] + w2_toks)
                    t_c = P.op("vector", lambda v, hh=hh: v.tensor_copy(
                        sb_y[:, hh * 512:(hh + 1) * 512], ps_y[hh][:, :]),
                        waits=[mm, P.note(sb_y)])
                    P.setn(ps_y[hh], t_c)
                    o_toks.append(t_c)
                P.setn(sb_y, o_toks[-1])
                tD = P.dma("sync", y_d[rb:rb + 128, :], sb_y[:, :], waits=o_toks)
                P.setn(sb_y, tD)
                prev = [tD]
    _phD()
    P.barrier()

    def _phE():
      with P.scope():
        sb_sw1 = P.sbuf("l3fe_sw1", [128, 16 * 128], BF16)
        sb_sw2 = P.sbuf("l3fe_sw2", [128, 16 * C], BF16)
        sw2_toks = [P.dma("sync", sb_sw2[:, fc * C:(fc + 1) * C],
                          w["sw2T"][fc * 128:(fc + 1) * 128, :]) for fc in range(16)]
        sb_xT = P.sbuf("l3fe_xT", [128, 8 * 512], BF16)
        sb_hf = P.sbuf("l3fe_hf", [128, 512], F32)
        sb_s1 = P.sbuf("l3fe_s1", [128, 512], F32)
        sb_s2 = P.sbuf("l3fe_s2", [128, 512], F32)
        sb_h1 = P.sbuf("l3fe_h1", [128, 16 * 512], BF16)
        sb_gs = P.sbuf("l3fe_gs", [128, K6], U32)
        sb_pw = P.sbuf("l3fe_pw", [128, K6], F32)
        sb_yk = P.sbuf("l3fe_yk", [128, C], BF16)
        sb_tmp = P.sbuf("l3fe_tmp", [128, C], F32)
        sb_acc = P.sbuf("l3fe_acc", [128, C], F32)
        sb_x = P.sbuf("l3fe_x", [128, C], F32)
        ps_h = P.psum("l3fe_psh", [128, 512], F32)
        ps_o = [P.psum(f"l3fe_pso{i}", [128, 512], F32) for i in range(2)]
        prev = []
        for ch in range(NCH):
            t0c = ch * 512
            xt_toks = [P.dma("sync", sb_xT[:, cc * 512:(cc + 1) * 512],
                             xlnT_d[cc * 128:(cc + 1) * 128, t0c:t0c + 512],
                             waits=[P.note(sb_xT)] + prev) for cc in range(8)]
            P.setn(sb_xT, xt_toks[-1])
            g_toks = []
            for fc in range(16):
                mm = None
                for cc in range(8):
                    sw1_t = P.dma("gpsimd_q", sb_sw1[:, fc * 128:(fc + 1) * 128],
                                  w["sw1T"][cc * 128:(cc + 1) * 128,
                                            fc * 128:(fc + 1) * 128],
                                  waits=[P.note(sb_sw1)])
                    mm = P.op("tensor", lambda t, fc=fc, cc=cc: t.matmul(
                        ps_h[:, :], sb_sw1[:, fc * 128:(fc + 1) * 128],
                        sb_xT[:, cc * 512:(cc + 1) * 512],
                        start=(cc == 0), stop=(cc == 7)),
                        waits=xt_toks + [sw1_t, P.note(ps_h)])
                    P.setn(sb_sw1, mm)
                t_cp = P.op("vector", lambda v: v.tensor_copy(sb_hf[:, :], ps_h[:, :]),
                            waits=[mm, P.note(sb_hf)])
                P.setn(ps_h, t_cp)
                t_g = emit_gelu_tanh(P, sb_h1[:, fc * 512:(fc + 1) * 512], sb_hf[:, :],
                                     sb_s1[:, :], sb_s2[:, :],
                                     waits=[t_cp, P.note(sb_h1), P.note(sb_s1),
                                            P.note(sb_s2)])
                P.setn(sb_hf, t_g)
                P.setn(sb_s1, t_g)
                P.setn(sb_s2, t_g)
                g_toks.append(t_g)
            P.setn(sb_h1, g_toks[-1])
            for sub in range(4):
                tb = t0c + sub * 128
                o_toks = []
                for hh in range(2):
                    mm = None
                    for fc in range(16):
                        mm = P.op("tensor", lambda t, hh=hh, fc=fc, sub=sub: t.matmul(
                            ps_o[hh][:, :],
                            sb_h1[:, fc * 512 + sub * 128:fc * 512 + sub * 128 + 128],
                            sb_sw2[:, fc * C + hh * 512:fc * C + hh * 512 + 512],
                            start=(fc == 0), stop=(fc == 15)),
                            waits=g_toks + sw2_toks + [P.note(ps_o[hh])])
                    t_c = P.op("vector", lambda v, hh=hh: v.tensor_copy(
                        sb_acc[:, hh * 512:(hh + 1) * 512], ps_o[hh][:, :]),
                        waits=[mm, P.note(sb_acc)])
                    P.setn(ps_o[hh], t_c)
                    o_toks.append(t_c)
                t_gl = P.dma("sync", sb_gs[:, :], gslot_d[tb:tb + 128, :],
                             waits=[P.note(sb_gs)])
                t_pl = P.dma("sync", sb_pw[:, :], pairw_d[tb:tb + 128, :],
                             waits=[P.note(sb_pw)])
                t_a = o_toks[-1]
                for k in range(K6):
                    t_y = P.indirect_dma(sb_yk[:, :], None, y_d[:, :],
                                         IndirectOffsetOnAxis(sb_gs[:, k:k + 1], 0),
                                         waits=[t_gl, P.note(sb_yk)])
                    t_m = P.op("vector", lambda v, k=k: v.tensor_scalar(
                        sb_tmp[:, :], sb_yk[:, :], sb_pw[:, k:k + 1], None, ALU.mult),
                        waits=[t_y, t_pl, P.note(sb_tmp)])
                    P.setn(sb_yk, t_m)
                    t_a = P.op("vector", lambda v: v.tensor_tensor(
                        sb_acc[:, :], sb_acc[:, :], sb_tmp[:, :], ALU.add),
                        waits=[t_m, t_a])
                    P.setn(sb_tmp, t_a)
                P.setn(sb_gs, t_a)
                P.setn(sb_pw, t_a)
                P.setn(sb_acc, t_a)
                t_x = P.dma("gpsimd_q", sb_x[:, :], xin[tb:tb + 128, :],
                            waits=[P.note(sb_x)])
                t_o = P.op("vector", lambda v: v.scalar_tensor_tensor(
                    sb_x[:, :], sb_acc[:, :], coef, sb_x[:, :], ALU.mult, ALU.add),
                    waits=[t_x, t_a])
                P.setn(sb_acc, t_o)
                tD = P.dma("sync", xout[tb:tb + 128, :], sb_x[:, :], waits=[t_o])
                P.setn(sb_x, tD)
                prev = [tD]
        return prev[0]
    return _phE()


def build_l3ffn_launch(rg=None):
    nc = bass.Bass("TRN2", target_bir_lowering=False)
    xin = nc.dram_tensor("x", [T, C], F32, kind="ExternalInput").ap()
    E3, F3, SF = 64, 512, 2048
    w = {
        "rwT": nc.dram_tensor("rwT", [C, E3], BF16, kind="ExternalInput").ap(),
        "ebias": nc.dram_tensor("ebias", [1, E3], F32, kind="ExternalInput").ap(),
        "sel": nc.dram_tensor("sel", [8, 1], F32, kind="ExternalInput").ap(),
        "w1b": nc.dram_tensor("w1b", [E3, C, F3], BF16, kind="ExternalInput").ap(),
        "w2b": nc.dram_tensor("w2b", [E3, F3, C], BF16, kind="ExternalInput").ap(),
        "sw1T": nc.dram_tensor("sw1T", [C, SF], BF16, kind="ExternalInput").ap(),
        "sw2T": nc.dram_tensor("sw2T", [SF, C], BF16, kind="ExternalInput").ap(),
    }
    xout = nc.dram_tensor("xo", [T, C], F32, kind="ExternalOutput").ap()
    P = Prog(nc)
    tok = stage_l3ffn(P, xin, xout, w, rg=rg)
    P.finalize(final_waits=[tok])
    return nc



def stage_l0attn(P, xin, xout, w):
    S33 = 33
    NCH = T // 512
    NT = T // 128
    idf, idb, t_id = make_identity(P)
    P.constf(1e-5)

    xln_d = P.dram("l0a_xln", [T, C], F32)
    xlnT_d = P.dram("l0a_xlnT", [C, T], BF16)
    idx_d = P.dram("l0a_idx", [T, S33], U32)
    wgt_d = P.dram("l0a_wgt", [8, S33 * T], F32)
    att_d = P.dram("l0a_att", [T, C], F32)

    lnscr = {
        "sq": P.sbuf("l0aln_sq", [128, C], F32),
        "m": P.sbuf("l0aln_m", [128, 1], F32),
        "ss": P.sbuf("l0aln_ss", [128, 1], F32),
        "var": P.sbuf("l0aln_var", [128, 1], F32),
        "rstd": P.sbuf("l0aln_rstd", [128, 1], F32),
        "msq": P.sbuf("l0aln_msq", [128, 1], F32),
    }

    def _ph0():
      with P.scope():
        ps_trb = P.psum("l0a0_trb", [128, 128], BF16)
        sb_x = P.sbuf("l0a0_x", [128, C], F32)
        sb_xf = P.sbuf("l0a0_xf", [128, C], F32)
        sb_ln = P.sbuf("l0a0_ln", [128, C], BF16)
        sb_lnT = P.sbuf("l0a0_lnT", [128, 8 * 128], BF16)
        prev = []
        prev_tr = []
        for tt in range(NT):
            tb = tt * 128
            t_l = P.dma("sync", sb_x[:], xin[tb:tb + 128, :], waits=prev)
            t_ln = emit_ln(P, sb_x[:], 128, sb_ln[:], lnscr, waits=[t_l] + prev_tr[-1:])
            t_f = P.op("scalar", lambda s: s.activation(sb_xf[:], sb_ln[:], AF.Copy),
                       waits=[t_ln, P.note(sb_xf)])
            tD0 = P.dma("gpsimd_q", xln_d[tb:tb + 128, :], sb_xf[:], waits=[t_f])
            P.setn(sb_xf, tD0)
            tr_toks = []
            for cc in range(8):
                tT = P.op("tensor", lambda t, cc=cc: t.transpose(
                    ps_trb[:, :], sb_ln[:, cc * 128:(cc + 1) * 128], idb[:]),
                    waits=[t_ln, t_id, P.note(ps_trb)])
                tC = P.op("vector", lambda v, cc=cc: v.tensor_copy(
                    sb_lnT[:, cc * 128:(cc + 1) * 128], ps_trb[:, :]),
                    waits=[tT, P.note(sb_lnT)])
                P.setn(ps_trb, tC)
                tr_toks.append(tC)
            prev_tr = [tr_toks[-1]]
            prev = [P.dma("sync", xlnT_d[cc * 128:(cc + 1) * 128, tb:tb + 128],
                          sb_lnT[:, cc * 128:(cc + 1) * 128], waits=tr_toks)
                    for cc in range(8)]
            P.setn(sb_lnT, prev[-1])
    _ph0()
    P.barrier()

    def _ph1():
      with P.scope():
        sb_wv = P.sbuf("l0a1_wv", [128, 8 * 24], BF16)
        sb_qw = P.sbuf("l0a1_qw", [128, 8 * 128], BF16)
        wv_toks = [P.dma("sync", sb_wv[:, cc * 24:(cc + 1) * 24],
                         w["wave_w"][cc * 128:(cc + 1) * 128, :]) for cc in range(8)]
        qw_toks = [P.dma("sync", sb_qw[:, cc * 128:(cc + 1) * 128],
                         w["query_w"][cc * 128:(cc + 1) * 128, :]) for cc in range(8)]
        sb_wb = P.sbuf("l0a1_wb", [24, 1], F32)
        sb_qb = P.sbuf("l0a1_qb", [128, 1], F32)
        sb_posw = P.sbuf("l0a1_posw", [2, S33], F32)
        sb_kws = P.sbuf("l0a1_kws", [128, S33], F32)
        sb_e16c = P.sbuf("l0a1_e16c", [128, 8], F32)
        sb_e16r = P.sbuf("l0a1_e16r", [8, 128], F32)
        sb_wsel = P.sbuf("l0a1_wsel", [24, 3], F32)
        cst_toks = [P.dma("gpsimd_q", sb_wb[:, :], w["wave_bcol"]),
                    P.dma("gpsimd_q", sb_qb[:, :], w["query_bcol"]),
                    P.dma("gpsimd_q", sb_posw[:, :], w["posw"]),
                    P.dma("gpsimd_q", sb_kws[:, :], w["kwscol"]),
                    P.dma("gpsimd_q", sb_e16c[:, :], w["e16col"]),
                    P.dma("gpsimd_q", sb_e16r[:, :], w["e16row"]),
                    P.dma("gpsimd_q", sb_wsel[:, :], w["wsel3"])]
        ones24 = P.sbuf("l0a1_o24", [24, 2], F32)
        i24 = P.sbuf("l0a1_i24", [24, 1], I32)
        i24f = P.sbuf("l0a1_i24f", [24, 1], F32)
        t_z1 = P.op("gpsimd", lambda g: g.iota(i24[:, :], pattern=[[1, 1]], base=0,
                                               channel_multiplier=1))
        t_z2 = P.op("vector", lambda v: v.tensor_copy(i24f[:, :], i24[:, :]), waits=[t_z1])
        t_z3 = P.op("vector", lambda v: v.tensor_single_scalar(ones24[:, 0:1], i24f[:, :],
                                                               8.0, ALU.is_lt), waits=[t_z2])
        t_z4 = P.op("vector", lambda v: v.tensor_single_scalar(ones24[:, 1:2], i24f[:, :],
                                                               8.0, ALU.is_ge), waits=[t_z3])
        t_z5 = P.op("vector", lambda v: v.tensor_single_scalar(i24f[:, :], i24f[:, :],
                                                               16.0, ALU.is_lt), waits=[t_z4])
        t_o24 = P.op("vector", lambda v: v.tensor_tensor(ones24[:, 1:2], ones24[:, 1:2],
                                                         i24f[:, :], ALU.mult),
                     waits=[t_z5])
        on18 = P.sbuf("l0a1_on18", [1, 8], F32)
        t_338 = P.op("vector", lambda v: v.memset(on18[:, :], 1.0))
        sb_nbr0 = P.sbuf("l0a1_nbr0", [1, 512], F32)

        sb_xT = P.sbuf("l0a1_xT", [128, 8 * 512], BF16)
        sb_wave = P.sbuf("l0a1_wave", [24, 512], F32)
        sb_wv0 = P.sbuf("l0a1_wv0", [24, 512], F32)
        sb_sg = P.sbuf("l0a1_sg", [24, 512], F32)
        sb_th = P.sbuf("l0a1_th", [24, 512], F32)
        sb_dec0 = P.sbuf("l0a1_dec0", [8, 512], F32)
        sb_Q = P.sbuf("l0a1_Q", [128, 512], F32)
        sb_fr = P.sbuf("l0a1_fr", [128, 512], F32)
        sb_fd = P.sbuf("l0a1_fd", [8, 512], F32)
        sb_rd = P.sbuf("l0a1_rd", [8, 512], F32)
        sb_r2 = P.sbuf("l0a1_r2", [2, 512], F32)
        sb_t1 = P.sbuf("l0a1_t1", [1, 512], F32)
        sb_it = P.sbuf("l0a1_it", [2, 512], I32)
        sb_itf = P.sbuf("l0a1_itf", [2, 512], F32)
        sb_m1 = P.sbuf("l0a1_m1", [S33, 512], F32)
        sb_m2 = P.sbuf("l0a1_m2", [S33, 512], F32)
        sb_nb = P.sbuf("l0a1_nb", [S33, 512], F32)
        sb_pc = P.sbuf("l0a1_pc", [S33, 512], F32)
        sb_pi = P.sbuf("l0a1_pi", [S33, 512], I32)
        sb_pf = P.sbuf("l0a1_pf", [S33, 512], F32)
        sb_fi = P.sbuf("l0a1_fi", [S33, 512], I32)
        sb_idxT = P.sbuf("l0a1_idxT", [128, 4 * S33], F32)
        sb_idxU = P.sbuf("l0a1_idxU", [128, 4 * S33], U32)
        sb_keys = P.sbuf("l0a1_keys", [128, 512], F32)
        sb_prod = P.sbuf("l0a1_prod", [128, 512], F32)
        sb_Lg = P.sbuf("l0a1_Lg", [8, S33 * 512], F32)
        sb_EN = P.sbuf("l0a1_EN", [8, S33 * 512], F32)
        sb_m8 = P.sbuf("l0a1_m8", [8, 512], F32)
        sb_Z = P.sbuf("l0a1_Z", [8, 512], F32)
        sb_den = P.sbuf("l0a1_den", [8, 512], F32)
        ps_wv = P.psum("l0a1_pswv", [24, 512], F32)
        ps_q = P.psum("l0a1_psq", [128, 512], F32)
        ps_av = P.psum("l0a1_psav", [1, 512], F32)
        ps_pos = P.psum("l0a1_pspos", [S33, 512], F32)
        ps_tr = P.psum("l0a1_pstr", [128, S33], F32)
        ps_fr = P.psum("l0a1_psfr", [128, 512], F32)
        ps_lg8 = P.psum("l0a1_pslg8", [8, 512], F32)
        prev = cst_toks + wv_toks + qw_toks + [t_o24, t_338]
        for ch in range(NCH):
            t0c = ch * 512
            xt_toks = [P.dma("sync", sb_xT[:, cc * 512:(cc + 1) * 512],
                             xlnT_d[cc * 128:(cc + 1) * 128, t0c:t0c + 512],
                             waits=[P.note(sb_xT)] + prev) for cc in range(8)]
            P.setn(sb_xT, xt_toks[-1])
            mm = None
            for cc in range(8):
                mm = P.op("tensor", lambda t, cc=cc: t.matmul(
                    ps_wv[:, :], sb_wv[:, cc * 24:(cc + 1) * 24],
                    sb_xT[:, cc * 512:(cc + 1) * 512],
                    start=(cc == 0), stop=(cc == 7)),
                    waits=xt_toks + wv_toks + [P.note(ps_wv)])
            t_wa = P.op("scalar", lambda s: s.activation(sb_wv0[:, :], ps_wv[:, :], AF.Silu,
                                                         bias=sb_wb[0:24, 0:1]),
                        waits=[mm, cst_toks[0], P.note(sb_wv0)])
            P.setn(ps_wv, t_wa)
            t_f1 = P.op("scalar", lambda s: s.activation(sb_sg[:, :], sb_wv0[:, :],
                                                         AF.Sigmoid),
                        waits=[t_wa, P.note(sb_sg)])
            t_p1 = P.op("scalar", lambda s: s.activation(sb_th[:, :], sb_wv0[:, :],
                                                         AF.Tanh), waits=[t_wa, P.note(sb_th)])
            P.setn(sb_wv0, t_p1)
            t_f2 = P.op("vector", lambda v: v.tensor_scalar(sb_sg[:, :], sb_sg[:, :],
                                                            sb_wsel[:, 0:1], None, ALU.mult),
                        waits=[t_f1, cst_toks[6]])
            t_p2 = P.op("vector", lambda v: v.tensor_scalar(sb_th[:, :], sb_th[:, :],
                                                            sb_wsel[:, 1:2], None, ALU.mult),
                        waits=[t_p1, cst_toks[6]])
            t_w1 = P.op("vector", lambda v: v.tensor_tensor(sb_wave[:, :], sb_sg[:, :],
                                                            sb_th[:, :], ALU.add),
                        waits=[t_f2, t_p2, P.note(sb_wave)])
            P.setn(sb_sg, t_w1)
            P.setn(sb_th, t_w1)
            t_w2 = P.op("vector", lambda v: v.tensor_scalar(sb_wave[:, :], sb_wave[:, :],
                                                            sb_wsel[:, 2:3], None, ALU.add),
                        waits=[t_w1])
            t_f2 = t_w2
            t_p2 = t_w2
            t_d3 = t_w2
            t_dd = P.dma("scalar_q", sb_dec0[:, :],
                         lambda e: sb_wave[16:24, :], waits=[t_w2, P.note(sb_dec0)])
            t_dm = P.op("vector", lambda v: v.tensor_single_scalar(
                sb_dec0[:, :], sb_dec0[:, :], 0.1, ALU.max), waits=[t_dd])
            t_rd = P.op("vector", lambda v: v.reciprocal(sb_rd[:, :], sb_dec0[:, :]),
                        waits=[t_dm, P.note(sb_rd)])
            P.setn(sb_dec0, t_rd)
            t_fa = P.op("tensor", lambda t: t.matmul(ps_av[:, :], ones24[:, 0:1],
                                                     sb_wave[:, :], start=True, stop=True),
                        waits=[t_f2, t_p2, t_d3, t_o24, P.note(ps_av)])
            t_fc = P.op("vector", lambda v: v.tensor_single_scalar(
                sb_r2[0:1, :], ps_av[0:1, :], 0.125, ALU.mult),
                waits=[t_fa, P.note(sb_r2)])
            P.setn(ps_av, t_fc)
            t_pa = P.op("tensor", lambda t: t.matmul(ps_av[:, :], ones24[:, 1:2],
                                                     sb_wave[:, :], start=True, stop=True),
                        waits=[t_p2, t_fc])
            t_pc = P.op("vector", lambda v: v.tensor_single_scalar(
                sb_t1[0:1, :], ps_av[0:1, :], 0.125, ALU.mult),
                waits=[t_pa, P.note(sb_t1)])
            P.setn(ps_av, t_pc)
            t_it = P.op("gpsimd", lambda g, t0c=t0c: g.iota(sb_it[0:1, :], pattern=[[1, 512]],
                                                            base=t0c, channel_multiplier=0),
                        waits=[P.note(sb_it)])
            t_if = P.op("vector", lambda v: v.tensor_copy(sb_itf[0:1, :], sb_it[0:1, :]),
                        waits=[t_it, P.note(sb_itf)])
            t_ia = P.op("vector", lambda v: v.tensor_tensor(sb_t1[0:1, :], sb_t1[0:1, :],
                                                            sb_itf[0:1, :], ALU.add),
                        waits=[t_if, t_pc])
            P.setn(sb_itf, t_ia)
            P.setn(sb_it, t_ia)
            t_r2 = P.dma("gpsimd_q", sb_r2[1:2, :], lambda e: sb_t1[0:1, :],
                         waits=[t_ia, t_fc])
            P.setn(sb_t1, t_r2)
            t_pm = P.op("tensor", lambda t: t.matmul(ps_pos[:, :], sb_posw[:, :], sb_r2[:, :],
                                                     start=True, stop=True),
                        waits=[t_r2, cst_toks[2], P.note(ps_pos)])
            P.setn(sb_r2, t_pm)
            t_v1 = P.op("vector", lambda v: v.tensor_single_scalar(sb_m1[:, :], ps_pos[:, :],
                                                                   0.0, ALU.is_ge),
                        waits=[t_pm, P.note(sb_m1)])
            t_v2 = P.op("vector", lambda v: v.tensor_single_scalar(sb_m2[:, :], ps_pos[:, :],
                                                                   float(T), ALU.is_lt),
                        waits=[t_pm, P.note(sb_m2)])
            t_vm = P.op("vector", lambda v: v.tensor_tensor(sb_m1[:, :], sb_m1[:, :],
                                                            sb_m2[:, :], ALU.mult),
                        waits=[t_v1, t_v2])
            t_nb = P.op("vector", lambda v: v.tensor_scalar(sb_nb[:, :], sb_m1[:, :],
                                                            1.0, 1e30, ALU.subtract, ALU.mult),
                        waits=[t_vm, P.note(sb_nb)])
            P.setn(sb_m1, t_nb)
            t_c1 = P.op("vector", lambda v: v.tensor_scalar(sb_pc[:, :], ps_pos[:, :],
                                                            float(T - 1), 0.0, ALU.min,
                                                            ALU.max),
                        waits=[t_pm, P.note(sb_pc)])
            P.setn(ps_pos, t_c1)
            t_c2 = P.op("vector", lambda v: v.tensor_copy(sb_pi[:, :], sb_pc[:, :]),
                        waits=[t_c1, P.note(sb_pi)])
            t_c3 = P.op("vector", lambda v: v.tensor_copy(sb_pf[:, :], sb_pi[:, :]),
                        waits=[t_c2, P.note(sb_pf)])
            t_c4 = P.op("vector", lambda v: v.tensor_tensor(sb_pf[:, :], sb_pf[:, :],
                                                            sb_pc[:, :], ALU.is_gt),
                        waits=[t_c3])
            P.setn(sb_pc, t_c4)
            t_c5 = P.op("vector", lambda v: v.tensor_copy(sb_fi[:, :], sb_pf[:, :]),
                        waits=[t_c4, P.note(sb_fi)])
            P.setn(sb_pf, t_c5)
            t_c6 = P.op("vector", lambda v: v.tensor_tensor(sb_pi[:, :], sb_pi[:, :],
                                                            sb_fi[:, :], ALU.subtract),
                        waits=[t_c5])
            P.setn(sb_fi, t_c6)
            t_c7 = P.op("vector", lambda v: v.tensor_copy(sb_pf[:, :], sb_pi[:, :]),
                        waits=[t_c6, t_c5])
            P.setn(sb_pi, t_c7)
            it_toks = []
            for sub in range(4):
                tT = P.op("tensor", lambda t, sub=sub: t.transpose(
                    ps_tr[:, 0:S33], sb_pf[:, sub * 128:(sub + 1) * 128], idf[0:S33, 0:S33]),
                    waits=[t_c7, t_id, P.note(ps_tr)])
                tC = P.op("vector", lambda v, sub=sub: v.tensor_copy(
                    sb_idxT[:, sub * S33:(sub + 1) * S33], ps_tr[:, 0:S33]),
                    waits=[tT, P.note(sb_idxT)])
                P.setn(ps_tr, tC)
                it_toks.append(tC)
            P.setn(sb_pf, it_toks[-1])
            t_iu = P.op("vector", lambda v: v.tensor_copy(sb_idxU[:, :], sb_idxT[:, :]),
                        waits=it_toks + [P.note(sb_idxU)])
            P.setn(sb_idxT, t_iu)
            idx_dma = [P.dma("sync", idx_d[t0c + sub * 128:t0c + (sub + 1) * 128, :],
                             sb_idxU[:, sub * S33:(sub + 1) * S33], waits=[t_iu])
                       for sub in range(4)]
            P.setn(sb_idxU, idx_dma[-1])
            mm = None
            for cc in range(8):
                mm = P.op("tensor", lambda t, cc=cc: t.matmul(
                    ps_q[:, :], sb_qw[:, cc * 128:(cc + 1) * 128],
                    sb_xT[:, cc * 512:(cc + 1) * 512],
                    start=(cc == 0), stop=(cc == 7)),
                    waits=xt_toks + qw_toks + [P.note(ps_q)])
            t_qa = P.op("scalar", lambda s: s.activation(sb_Q[:, :], ps_q[:, :], AF.Silu,
                                                         bias=sb_qb[:, 0:1]),
                        waits=[mm, cst_toks[1], P.note(sb_Q)])
            P.setn(ps_q, t_qa)
            t_fr = P.op("tensor", lambda t: t.matmul(ps_fr[:, :], sb_e16r[:, :],
                                                     sb_wave[0:8, :], start=True, stop=True),
                        waits=[t_f2, cst_toks[5], P.note(ps_fr)])
            t_frc = P.op("vector", lambda v: v.tensor_copy(sb_fr[:, :], ps_fr[:, :]),
                         waits=[t_fr, P.note(sb_fr)])
            P.setn(ps_fr, t_frc)
            t_fd = P.op("vector", lambda v: v.tensor_tensor(sb_fd[:, :], sb_wave[0:8, :],
                                                            sb_rd[:, :], ALU.mult),
                        waits=[t_f2, t_rd, P.note(sb_fd)])
            lg_toks = []
            for s5 in range(S33):
                t_k = P.op("scalar", lambda s, s5=s5: s.activation(
                    sb_keys[:, :], sb_fr[:, :], AF.Silu, scale=sb_kws[:, s5:s5 + 1]),
                    waits=[t_frc, cst_toks[3], P.note(sb_keys)])
                t_pr = P.op("vector", lambda v: v.tensor_tensor(sb_prod[:, :], sb_Q[:, :],
                                                                sb_keys[:, :], ALU.mult),
                            waits=[t_k, t_qa, P.note(sb_prod)])
                P.setn(sb_keys, t_pr)
                t_lm = P.op("tensor", lambda t: t.matmul(ps_lg8[:, :], sb_e16c[:, :],
                                                         sb_prod[:, :], start=True, stop=False),
                            waits=[t_pr, cst_toks[4], P.note(ps_lg8)])
                P.setn(sb_prod, t_lm)
                t_nr = P.dma("scalar_q", sb_nbr0[:, :], sb_nb[s5:s5 + 1, :],
                             waits=[t_nb, P.note(sb_nbr0)])
                t_nm = P.op("tensor", lambda t: t.matmul(
                    ps_lg8[:, :], on18[:, :], sb_nbr0[:, :],
                    start=False, stop=True), waits=[t_nr, t_338, t_lm])
                P.setn(sb_nbr0, t_nm)
                t_ad = P.op("vector", lambda v, s5=s5: v.tensor_copy(
                    sb_Lg[:, s5 * 512:(s5 + 1) * 512], ps_lg8[:, :]),
                    waits=[t_nm, P.note(sb_Lg)])
                P.setn(ps_lg8, t_ad)
                lg_toks.append(t_ad)
            LgV = sb_Lg[:, :].rearrange("p (s l) -> p l s", s=S33)
            t_m = P.op("vector", lambda v: v.tensor_reduce(sb_m8[:, :], LgV, AX.X, ALU.max),
                       waits=lg_toks + [P.note(sb_m8)])
            mb = sb_m8[:, :].rearrange("p (a l) -> p l a", a=1).broadcast_to([8, 512, S33])
            t_s = P.op("vector", lambda v: v.tensor_tensor(LgV, LgV, mb, ALU.subtract),
                       waits=[t_m])
            t_e = P.op("scalar", lambda s: s.activation(sb_Lg[:, :], sb_Lg[:, :], AF.Exp),
                       waits=[t_s])
            t_z = P.op("vector", lambda v: v.tensor_reduce(sb_Z[:, :], LgV, AX.X, ALU.add),
                       waits=[t_e, P.note(sb_Z)])
            en_toks = []
            for s5 in range(S33):
                en_toks.append(P.op("vector", lambda v, s5=s5: v.tensor_single_scalar(
                    sb_EN[:, s5 * 512:(s5 + 1) * 512], sb_fd[:, :],
                    -abs(float(s5 - 16)), ALU.mult), waits=[t_fd, P.note(sb_EN)]))
            t_ee = P.op("scalar", lambda s: s.activation(sb_EN[:, :], sb_EN[:, :], AF.Exp),
                        waits=en_toks)
            P.setn(sb_EN, t_ee)
            t_me = P.op("vector", lambda v: v.tensor_tensor(sb_Lg[:, :], sb_Lg[:, :],
                                                            sb_EN[:, :], ALU.mult),
                        waits=[t_ee, t_z])
            t_dn = P.op("vector", lambda v: v.tensor_reduce(sb_den[:, :], LgV, AX.X, ALU.add),
                        waits=[t_me, P.note(sb_den)])
            t_d4 = P.op("vector", lambda v: v.tensor_scalar(sb_Z[:, :], sb_Z[:, :], 1e-8,
                                                            None, ALU.mult), waits=[t_dn, t_z])
            t_d5 = P.op("vector", lambda v: v.tensor_tensor(sb_den[:, :], sb_den[:, :],
                                                            sb_Z[:, :], ALU.add), waits=[t_d4])
            P.setn(sb_Z, t_d5)
            t_d6 = P.op("vector", lambda v: v.reciprocal(sb_den[:, :], sb_den[:, :]),
                        waits=[t_d5])
            db = sb_den[:, :].rearrange("p (a l) -> p l a", a=1).broadcast_to([8, 512, S33])
            t_w = P.op("vector", lambda v: v.tensor_tensor(LgV, LgV, db, ALU.mult),
                       waits=[t_d6])
            P.setn(sb_den, t_w)
            P.setn(sb_Lg, t_w)
            wg_dma = P.dma("sync", wgt_d[:, :].rearrange("p (s t) -> p s t", s=S33)
                           [:, :, t0c:t0c + 512],
                           sb_Lg[:, :].rearrange("p (s l) -> p s l", s=S33), waits=[t_w])
            prev = [wg_dma] + idx_dma
    _ph1()
    P.barrier()

    def _ph2():
      with P.scope():
        sb_e8 = P.sbuf("l0a2_e8", [8, C], F32)
        e8_tok = P.dma("sync", sb_e8[:, :], w["e8row"])
        sb_Wh = P.sbuf("l0a2_Wh", [8, S33 * 128], F32)
        sb_ix = P.sbuf("l0a2_ix", [128, S33], U32)
        sb_G = [P.sbuf(f"l0a2_G{i}", [128, C], F32) for i in range(2)]
        sb_tmp = P.sbuf("l0a2_tmp", [128, C], F32)
        sb_acc = P.sbuf("l0a2_acc", [128, C], F32)
        ps_we = P.psum("l0a2_pswe", [128, C], F32)
        prev = [e8_tok]
        for tt in range(NT):
            tb = tt * 128
            t_wl = P.dma("sync", sb_Wh[:, :].rearrange("p (s t) -> p s t", s=S33),
                         wgt_d[:, :].rearrange("p (s t) -> p s t", s=S33)[:, :, tb:tb + 128],
                         waits=[P.note(sb_Wh)] + prev)
            t_il = P.dma("sync", sb_ix[:, :], idx_d[tb:tb + 128, :],
                         waits=[P.note(sb_ix)] + prev)
            t_za = P.op("vector", lambda v: v.memset(sb_acc[:, :], 0.0),
                        waits=[P.note(sb_acc)])
            g_toks = {}

            def issue_gather(s5, extra):
                g = s5 % 2
                g_toks[s5] = P.indirect_dma(sb_G[g][:, :], None, xln_d[:, :],
                                            IndirectOffsetOnAxis(sb_ix[:, s5:s5 + 1], 0),
                                            waits=[t_il, P.note(sb_G[g])] + extra)
            issue_gather(0, prev)
            issue_gather(1, prev)
            t_a = t_za
            for s5 in range(S33):
                g = s5 % 2
                mm1 = P.op("tensor", lambda t, s5=s5: t.matmul(
                    ps_we[:, 0:512], sb_Wh[:, s5 * 128:(s5 + 1) * 128], sb_e8[:, 0:512],
                    start=True, stop=True), waits=[t_wl, e8_tok, P.note(ps_we)])
                mm2 = P.op("tensor", lambda t, s5=s5: t.matmul(
                    ps_we[:, 512:1024], sb_Wh[:, s5 * 128:(s5 + 1) * 128], sb_e8[:, 512:1024],
                    start=True, stop=True), waits=[t_wl, e8_tok])
                t_mu = P.op("vector", lambda v, g=g: v.tensor_tensor(
                    sb_tmp[:, :], sb_G[g][:, :], ps_we[:, :], ALU.mult),
                    waits=[mm1, mm2, g_toks[s5], P.note(sb_tmp)])
                P.setn(ps_we, t_mu)
                P.setn(sb_G[g], t_mu)
                if s5 + 2 < S33:
                    issue_gather(s5 + 2, [t_mu])
                t_a = P.op("gpsimd", lambda g2: g2.tensor_tensor(
                    sb_acc[:, :], sb_acc[:, :], sb_tmp[:, :], ALU.add), waits=[t_mu, t_a])
                P.setn(sb_tmp, t_a)
            P.setn(sb_acc, t_a)
            P.setn(sb_Wh, t_a)
            P.setn(sb_ix, t_a)
            tD = P.dma("gpsimd_q", att_d[tb:tb + 128, :], sb_acc[:, :], waits=[t_a])
            P.setn(sb_acc, tD)
            prev = [tD]
    _ph2()
    P.barrier()

    def _ph3():
      with P.scope():
        ps_trb = P.psum("l0a3_trb", [128, 128], BF16)
        sb_s1w = P.sbuf("l0a3_s1w", [128, 8 * 256], BF16)
        sb_s2w = P.sbuf("l0a3_s2w", [128, 2 * C], BF16)
        sb_ow = P.sbuf("l0a3_ow", [128, 8 * C], BF16)
        s1_toks = [P.dma("sync", sb_s1w[:, cc * 256:(cc + 1) * 256],
                         w["se1_w"][cc * 128:(cc + 1) * 128, :]) for cc in range(8)]
        s2_toks = [P.dma("sync", sb_s2w[:, f2 * C:(f2 + 1) * C],
                         w["se2_w"][f2 * 128:(f2 + 1) * 128, :]) for f2 in range(2)]
        ow_toks = [P.dma("sync", sb_ow[:, cc * C:(cc + 1) * C],
                         w["out_w"][cc * 128:(cc + 1) * 128, :]) for cc in range(8)]
        sb_s1b = P.sbuf("l0a3_s1b", [128, 2], F32)
        sb_s2b = P.sbuf("l0a3_s2b", [128, 8], F32)
        b_toks = [P.dma("gpsimd_q", sb_s1b[:, :], w["se1bcol"]),
                  P.dma("gpsimd_q", sb_s2b[:, :], w["se2bcol"])]
        sb_att = P.sbuf("l0a3_att", [128, C], F32)
        sb_attb = P.sbuf("l0a3_attb", [128, C], BF16)
        sb_aT = P.sbuf("l0a3_aT", [128, 8 * 512], BF16)
        sb_seh = P.sbuf("l0a3_seh", [128, 2 * 512], BF16)
        sb_se = P.sbuf("l0a3_se", [128, 512], BF16)
        sb_g = P.sbuf("l0a3_g", [128, 8 * 512], BF16)
        sb_rf = P.sbuf("l0a3_rf", [128, 8 * 512], BF16)
        sb_res = P.sbuf("l0a3_res", [128, C], BF16)
        sb_x = P.sbuf("l0a3_x", [128, C], F32)
        ps_se = P.psum("l0a3_psse", [128, 512], F32)
        ps_o = P.psum("l0a3_pso", [128, 512], F32)
        prev = s1_toks + s2_toks + ow_toks + b_toks
        for ch in range(NCH):
            t0c = ch * 512
            tr_toks = []
            cv_toks = []
            for sub in range(4):
                tb = t0c + sub * 128
                t_al = P.dma("sync", sb_att[:, :], att_d[tb:tb + 128, :],
                             waits=prev + cv_toks[-1:] + [P.note(sb_att)])
                t_cv = P.op("vector", lambda v: v.tensor_copy(sb_attb[:, :], sb_att[:, :]),
                            waits=[t_al, P.note(sb_attb)])
                P.setn(sb_att, t_cv)
                cv_toks.append(t_cv)
                for cc in range(8):
                    tT = P.op("tensor", lambda t, cc=cc: t.transpose(
                        ps_trb[:, :], sb_attb[:, cc * 128:(cc + 1) * 128], idb[:]),
                        waits=[t_cv, t_id, P.note(ps_trb)])
                    tC = P.op("vector", lambda v, cc=cc, sub=sub: v.tensor_copy(
                        sb_aT[:, cc * 512 + sub * 128:cc * 512 + sub * 128 + 128],
                        ps_trb[:, :]), waits=[tT, P.note(sb_aT)])
                    P.setn(ps_trb, tC)
                    tr_toks.append(tC)
                P.setn(sb_attb, tr_toks[-1])
            P.setn(sb_aT, tr_toks[-1])
            se_toks = []
            for st2 in range(2):
                mm = None
                for cc in range(8):
                    mm = P.op("tensor", lambda t, st2=st2, cc=cc: t.matmul(
                        ps_se[:, :], sb_s1w[:, cc * 256 + st2 * 128:cc * 256 + st2 * 128 + 128],
                        sb_aT[:, cc * 512:(cc + 1) * 512],
                        start=(cc == 0), stop=(cc == 7)),
                        waits=tr_toks + s1_toks + [P.note(ps_se)])
                t_h = P.op("scalar", lambda s, st2=st2: s.activation(
                    sb_seh[:, st2 * 512:(st2 + 1) * 512], ps_se[:, :], AF.Silu,
                    bias=sb_s1b[:, st2:st2 + 1]), waits=[mm] + b_toks + [P.note(sb_seh)])
                P.setn(ps_se, t_h)
                se_toks.append(t_h)
            P.setn(sb_seh, se_toks[-1])
            g_toks = []
            for oc in range(8):
                mm = None
                for f2 in range(2):
                    mm = P.op("tensor", lambda t, oc=oc, f2=f2: t.matmul(
                        ps_o[:, :], sb_s2w[:, f2 * C + oc * 128:f2 * C + oc * 128 + 128],
                        sb_seh[:, f2 * 512:(f2 + 1) * 512],
                        start=(f2 == 0), stop=(f2 == 1)),
                        waits=se_toks + s2_toks + [P.note(ps_o)])
                t_sg = P.op("scalar", lambda s, oc=oc: s.activation(
                    sb_se[:, :], ps_o[:, :], AF.Sigmoid, bias=sb_s2b[:, oc:oc + 1]),
                    waits=[mm, P.note(sb_se)])
                P.setn(ps_o, t_sg)
                t_gt = P.op("vector", lambda v, oc=oc: v.tensor_tensor(
                    sb_g[:, oc * 512:(oc + 1) * 512], sb_aT[:, oc * 512:(oc + 1) * 512],
                    sb_se[:, :], ALU.mult), waits=[t_sg, P.note(sb_g)])
                P.setn(sb_se, t_gt)
                g_toks.append(t_gt)
            P.setn(sb_g, g_toks[-1])
            rf_toks = []
            for oc2 in range(8):
                mm = None
                for cc in range(8):
                    mm = P.op("tensor", lambda t, oc2=oc2, cc=cc: t.matmul(
                        ps_o[:, :], sb_ow[:, cc * C + oc2 * 128:cc * C + oc2 * 128 + 128],
                        sb_g[:, cc * 512:(cc + 1) * 512],
                        start=(cc == 0), stop=(cc == 7)),
                        waits=g_toks + ow_toks + [P.note(ps_o)])
                t_si = P.op("scalar", lambda s, oc2=oc2: s.activation(
                    sb_rf[:, oc2 * 512:(oc2 + 1) * 512], ps_o[:, :], AF.Silu),
                    waits=[mm, P.note(sb_rf)])
                P.setn(ps_o, t_si)
                rf_toks.append(t_si)
            P.setn(sb_rf, rf_toks[-1])
            res_prev = []
            for sub in range(4):
                tb = t0c + sub * 128
                st_toks = []
                for oc2 in range(8):
                    tT = P.op("tensor", lambda t, oc2=oc2, sub=sub: t.transpose(
                        ps_trb[:, :], sb_rf[:, oc2 * 512 + sub * 128:oc2 * 512 + sub * 128 + 128],
                        idb[:]), waits=rf_toks + [P.note(ps_trb)])
                    tC = P.op("vector", lambda v, oc2=oc2: v.tensor_copy(
                        sb_res[:, oc2 * 128:(oc2 + 1) * 128], ps_trb[:, :]),
                        waits=[tT, P.note(sb_res)])
                    P.setn(ps_trb, tC)
                    st_toks.append(tC)
                t_x = P.dma("sync", sb_x[:, :], xin[tb:tb + 128, :],
                            waits=res_prev + [P.note(sb_x)])
                t_o = P.op("vector", lambda v: v.tensor_tensor(sb_x[:, :], sb_x[:, :],
                                                               sb_res[:, :], ALU.add),
                           waits=[t_x] + st_toks)
                P.setn(sb_res, t_o)
                tD = P.dma("gpsimd_q", xout[tb:tb + 128, :], sb_x[:, :], waits=[t_o])
                P.setn(sb_x, tD)
                res_prev = [tD]
            prev = res_prev
        return prev[0]
    return _ph3()


def build_l0attn_launch():
    nc = bass.Bass("TRN2", target_bir_lowering=False)
    xin = nc.dram_tensor("x", [T, C], F32, kind="ExternalInput").ap()
    S33 = 33
    w = {
        "wave_w": nc.dram_tensor("wave_w", [C, 24], BF16, kind="ExternalInput").ap(),
        "wave_bcol": nc.dram_tensor("wave_bcol", [24, 1], F32, kind="ExternalInput").ap(),
        "query_w": nc.dram_tensor("query_w", [C, 128], BF16, kind="ExternalInput").ap(),
        "query_bcol": nc.dram_tensor("query_bcol", [128, 1], F32, kind="ExternalInput").ap(),
        "posw": nc.dram_tensor("posw", [2, S33], F32, kind="ExternalInput").ap(),
        "kwscol": nc.dram_tensor("kwscol", [128, S33], F32, kind="ExternalInput").ap(),
        "e16col": nc.dram_tensor("e16col", [128, 8], F32, kind="ExternalInput").ap(),
        "e16row": nc.dram_tensor("e16row", [8, 128], F32, kind="ExternalInput").ap(),
        "e8row": nc.dram_tensor("e8row", [8, C], F32, kind="ExternalInput").ap(),
        "wsel3": nc.dram_tensor("wsel3", [24, 3], F32, kind="ExternalInput").ap(),
        "se1_w": nc.dram_tensor("se1_w", [C, 256], BF16, kind="ExternalInput").ap(),
        "se1bcol": nc.dram_tensor("se1bcol", [128, 2], F32, kind="ExternalInput").ap(),
        "se2_w": nc.dram_tensor("se2_w", [256, C], BF16, kind="ExternalInput").ap(),
        "se2bcol": nc.dram_tensor("se2bcol", [128, 8], F32, kind="ExternalInput").ap(),
        "out_w": nc.dram_tensor("out_w", [C, C], BF16, kind="ExternalInput").ap(),
    }
    xout = nc.dram_tensor("xo", [T, C], F32, kind="ExternalOutput").ap()
    P = Prog(nc)
    tok = stage_l0attn(P, xin, xout, w)
    P.finalize(final_waits=[tok])
    return nc



def _l0attn_consts():
    import ml_dtypes
    S = 33
    offs = np.arange(-16, 17, dtype=np.float32)
    posw = np.stack([offs, np.ones(S, np.float32)])
    e16col = np.zeros((128, 8), np.float32)
    e16row = np.zeros((8, 128), np.float32)
    for h in range(8):
        e16col[h * 16:(h + 1) * 16, h] = 0.25
        e16row[h, h * 16:(h + 1) * 16] = 1.0
    e8row = np.zeros((8, 1024), np.float32)
    for h in range(8):
        e8row[h, h * 128:(h + 1) * 128] = 1.0
    wsel3 = np.zeros((24, 3), np.float32)
    wsel3[0:8, 0] = 15.0; wsel3[16:24, 0] = 9.5
    wsel3[8:16, 1] = 16.0
    wsel3[0:8, 2] = 1.0; wsel3[16:24, 2] = 0.5
    return posw, e16col, e16row, e8row, wsel3


def kernel_multi(**inp):
    import ml_dtypes
    bf = lambda a: np.ascontiguousarray(a).astype(ml_dtypes.bfloat16)
    x = np.ascontiguousarray(inp["L0_attn_x"], dtype=np.float32)
    cores = list(range(NCORES))

    m_l0f = {k: np.ascontiguousarray(inp["L0_ffn_" + k]) for k in
             ["wqkv", "q_scale", "k_scale", "wo", "ln_g", "ln_b", "w1", "b1", "w2", "b2"]}
    posw, e16col, e16row, e8row, wsel3 = _l0attn_consts()
    key_w = inp["L0_attn_key_w"]
    kwscol = np.zeros((128, 33), np.float32)
    offs = np.arange(-16, 17, dtype=np.float32)
    for p in range(128):
        kwscol[p] = key_w[p % 16] * np.abs(offs)
    m_l0a = {
        "wave_w": bf(inp["L0_attn_wave_w"]),
        "wave_bcol": np.ascontiguousarray(inp["L0_attn_wave_b"].reshape(24, 1),
                                          dtype=np.float32),
        "query_w": bf(inp["L0_attn_query_w"]),
        "query_bcol": np.ascontiguousarray(inp["L0_attn_query_b"].reshape(128, 1),
                                           dtype=np.float32),
        "posw": posw, "kwscol": kwscol, "e16col": e16col, "e16row": e16row,
        "e8row": e8row, "wsel3": wsel3,
        "se1_w": bf(inp["L0_attn_se1_w"]),
        "se1bcol": np.ascontiguousarray(inp["L0_attn_se1_b"].reshape(2, 128).T,
                                        dtype=np.float32),
        "se2_w": bf(inp["L0_attn_se2_w"]),
        "se2bcol": np.ascontiguousarray(inp["L0_attn_se2_b"].reshape(8, 128).T,
                                        dtype=np.float32),
        "out_w": bf(inp["L0_attn_out_w"]),
    }
    m_l1f = {k: np.ascontiguousarray(inp["L1_ffn_" + k]) for k in
             ["w_gate", "w_up", "w_down"]}
    m_l1a = {k: np.ascontiguousarray(inp["L1_attn_" + k]) for k in
             ["Wq", "bq", "Wk", "bk", "Wv", "bv", "Wa", "ba", "Wb", "bb",
              "Wg", "bg", "Wo", "bo"]}
    m_l2f = {k: np.ascontiguousarray(inp["L2_ffn_" + k]) for k in
             ["w_red", "wg", "W1", "b1", "W2", "b2"]}
    m_l2a = {k: np.ascontiguousarray(inp["L2_attn_" + k]) for k in
             ["W_qkvf", "W_conv", "W_proj", "W_ff1", "W_ff2",
              "ln1_g", "ln1_b", "ln2_g", "ln2_b"]}
    m_l3f = {
        "rwT": bf(inp["L3_ffn_router_weight"].T),
        "ebias": np.ascontiguousarray(inp["L3_ffn_expert_bias"].reshape(1, 64),
                                      dtype=np.float32),
        "w1b": bf(inp["L3_ffn_w1"]),
        "w2b": bf(inp["L3_ffn_w2"]),
        "sw1T": bf(inp["L3_ffn_sw1"].T),
        "sw2T": bf(inp["L3_ffn_sw2"].T),
    }
    sels = []
    for c in cores:
        s = np.zeros((8, 1), np.float32); s[:c] = 1.0
        sels.append(s)
    m_l3a = {k: np.ascontiguousarray(inp["L3_attn_" + k]) for k in ["Wq", "Wk", "Wv"]}

    ncs = {
        "l0f": build_l0ffn_launch(),
        "l0a": build_l0attn_launch(),
        "l1f": build_l1ffn_launch(),
        "l1a": build_l1attn_launch(),
        "l2f": build_l2ffn_launch(),
        "l2a": build_l2attn_launch(),
        "l3f": build_l3ffn_launch(),
        "l3a": build_l3attn_launch(),
    }
    seq = [("l0f", m_l0f), ("l0a", m_l0a), ("l0f", m_l0f),
           ("l1f", m_l1f), ("l1a", m_l1a), ("l1f", m_l1f),
           ("l2f", m_l2f), ("l2a", m_l2a), ("l2f", m_l2f),
           ("l3f", m_l3f), ("l3a", m_l3a), ("l3f", m_l3f)]
    cur = x
    for (name, wm) in seq:
        in_maps = []
        for c in cores:
            d = dict(wm)
            d["x"] = np.ascontiguousarray(cur[c])
            if name == "l3f":
                d["sel"] = sels[c]
            in_maps.append(d)
        res = run_bass_kernel_spmd(ncs[name], in_maps, core_ids=cores)
        cur = np.stack([res.results[c]["xo"] for c in cores])
    return cur



def build_fused_launch():
    nc = bass.Bass("TRN2", target_bir_lowering=False)
    xin = nc.dram_tensor("x", [T, C], F32, kind="ExternalInput").ap()
    xout = nc.dram_tensor("xo", [T, C], F32, kind="ExternalOutput").ap()

    def mk(prefix, names):
        return {k: nc.dram_tensor(prefix + k, sh, dt, kind="ExternalInput").ap()
                for k, (sh, dt) in names.items()}

    w_l0f = mk("l0f_", {"wqkv": ([2, 1024, 1536], F32), "q_scale": ([2, 64], F32),
                        "k_scale": ([2, 64], F32), "wo": ([2, 512, 1024], F32),
                        "ln_g": ([2, 1024], F32), "ln_b": ([2, 1024], F32),
                        "w1": ([2, 1024, 5460], F32), "b1": ([2, 5460], F32),
                        "w2": ([2, 2730, 1024], F32), "b2": ([2, 1024], F32)})
    w_l0a = mk("l0a_", {"wave_w": ([C, 24], BF16), "wave_bcol": ([24, 1], F32),
                        "query_w": ([C, 128], BF16), "query_bcol": ([128, 1], F32),
                        "posw": ([2, 33], F32), "kwscol": ([128, 33], F32),
                        "e16col": ([128, 8], F32), "e16row": ([8, 128], F32),
                        "e8row": ([8, C], F32), "wsel3": ([24, 3], F32),
                        "se1_w": ([C, 256], BF16), "se1bcol": ([128, 2], F32),
                        "se2_w": ([256, C], BF16), "se2bcol": ([128, 8], F32),
                        "out_w": ([C, C], BF16)})
    w_l1f = mk("l1f_", {"w_gate": ([2816, 1024], F32), "w_up": ([2816, 1024], F32),
                        "w_down": ([1024, 2816], F32)})
    w_l1a = mk("l1a_", {"Wq": ([1024, 128], F32), "bq": ([128], F32),
                        "Wk": ([1024, 128], F32), "bk": ([128], F32),
                        "Wv": ([1024, 128], F32), "bv": ([128], F32),
                        "Wa": ([1024, 1], F32), "ba": ([1], F32),
                        "Wb": ([1024, 1], F32), "bb": ([1], F32),
                        "Wg": ([1024, 128], F32), "bg": ([128], F32),
                        "Wo": ([128, 1024], F32), "bo": ([1024], F32)})
    w_l2f = mk("l2f_", {"w_red": ([1024, 16], F32), "wg": ([8, 16], F32),
                        "W1": ([8, 1024, 2048], F32), "b1": ([8, 2048], F32),
                        "W2": ([8, 2048, 1024], F32), "b2": ([8, 1024], F32)})
    w_l2a = mk("l2a_", {"W_qkvf": ([4096, 1024], F32), "W_conv": ([64, 64, 4], F32),
                        "W_proj": ([1024, 1024], F32), "W_ff1": ([1024, 4096], F32),
                        "W_ff2": ([4096, 1024], F32), "ln1_g": ([1024], F32),
                        "ln1_b": ([1024], F32), "ln2_g": ([1024], F32),
                        "ln2_b": ([1024], F32)})
    w_l3f = mk("l3f_", {"rwT": ([C, 64], BF16), "ebias": ([1, 64], F32),
                        "sel": ([8, 1], F32), "w1b": ([64, C, 512], BF16),
                        "w2b": ([64, 512, C], BF16), "sw1T": ([C, 2048], BF16),
                        "sw2T": ([2048, C], BF16)})
    w_l3a = mk("l3a_", {"Wq": ([64, 64], F32), "Wk": ([64, 64], F32),
                        "Wv": ([64, 64], F32)})

    P = Prog(nc)
    P.constf(1e-5)
    P.constf(-0.25)
    P.constf(-0.75)
    P.constf(0.0)
    P.constf(1.0)
    xi = [xin] + [P.dram(f"xchain{i}", [T, C], F32) for i in range(1, 12)] + [xout]
    stages = [
        (stage_l0ffn, w_l0f), (stage_l0attn, w_l0a), (stage_l0ffn, w_l0f),
        (stage_l1ffn, w_l1f), (stage_l1attn, w_l1a), (stage_l1ffn, w_l1f),
        (stage_l2ffn, w_l2f), (stage_l2attn, w_l2a), (stage_l2ffn, w_l2f),
        (stage_l3ffn, w_l3f), (stage_l3attn, w_l3a), (stage_l3ffn, w_l3f),
    ]
    tok = None
    for i, (fn, wm) in enumerate(stages):
        with P.scope():
            tok = fn(P, xi[i], xi[i + 1], wm)
    P.finalize(final_waits=[tok])
    return nc


def kernel(**inp):
    import ml_dtypes
    bf = lambda a: np.ascontiguousarray(a).astype(ml_dtypes.bfloat16)
    x = np.ascontiguousarray(inp["L0_attn_x"], dtype=np.float32)
    cores = list(range(NCORES))
    posw, e16col, e16row, e8row, wsel3 = _l0attn_consts()
    key_w = inp["L0_attn_key_w"]
    kwscol = np.zeros((128, 33), np.float32)
    offs = np.arange(-16, 17, dtype=np.float32)
    for p in range(128):
        kwscol[p] = key_w[p % 16] * np.abs(offs)
    base = {}
    for k in ["wqkv", "q_scale", "k_scale", "wo", "ln_g", "ln_b", "w1", "b1", "w2", "b2"]:
        base["l0f_" + k] = np.ascontiguousarray(inp["L0_ffn_" + k])
    base.update({
        "l0a_wave_w": bf(inp["L0_attn_wave_w"]),
        "l0a_wave_bcol": np.ascontiguousarray(inp["L0_attn_wave_b"].reshape(24, 1),
                                              dtype=np.float32),
        "l0a_query_w": bf(inp["L0_attn_query_w"]),
        "l0a_query_bcol": np.ascontiguousarray(inp["L0_attn_query_b"].reshape(128, 1),
                                               dtype=np.float32),
        "l0a_posw": posw, "l0a_kwscol": kwscol, "l0a_e16col": e16col,
        "l0a_e16row": e16row, "l0a_e8row": e8row, "l0a_wsel3": wsel3,
        "l0a_se1_w": bf(inp["L0_attn_se1_w"]),
        "l0a_se1bcol": np.ascontiguousarray(inp["L0_attn_se1_b"].reshape(2, 128).T,
                                            dtype=np.float32),
        "l0a_se2_w": bf(inp["L0_attn_se2_w"]),
        "l0a_se2bcol": np.ascontiguousarray(inp["L0_attn_se2_b"].reshape(8, 128).T,
                                            dtype=np.float32),
        "l0a_out_w": bf(inp["L0_attn_out_w"]),
    })
    for k in ["w_gate", "w_up", "w_down"]:
        base["l1f_" + k] = np.ascontiguousarray(inp["L1_ffn_" + k])
    for k in ["Wq", "bq", "Wk", "bk", "Wv", "bv", "Wa", "ba", "Wb", "bb",
              "Wg", "bg", "Wo", "bo"]:
        base["l1a_" + k] = np.ascontiguousarray(inp["L1_attn_" + k])
    for k in ["w_red", "wg", "W1", "b1", "W2", "b2"]:
        base["l2f_" + k] = np.ascontiguousarray(inp["L2_ffn_" + k])
    for k in ["W_qkvf", "W_conv", "W_proj", "W_ff1", "W_ff2",
              "ln1_g", "ln1_b", "ln2_g", "ln2_b"]:
        base["l2a_" + k] = np.ascontiguousarray(inp["L2_attn_" + k])
    base.update({
        "l3f_rwT": bf(inp["L3_ffn_router_weight"].T),
        "l3f_ebias": np.ascontiguousarray(inp["L3_ffn_expert_bias"].reshape(1, 64),
                                          dtype=np.float32),
        "l3f_w1b": bf(inp["L3_ffn_w1"]),
        "l3f_w2b": bf(inp["L3_ffn_w2"]),
        "l3f_sw1T": bf(inp["L3_ffn_sw1"].T),
        "l3f_sw2T": bf(inp["L3_ffn_sw2"].T),
    })
    for k in ["Wq", "Wk", "Wv"]:
        base["l3a_" + k] = np.ascontiguousarray(inp["L3_attn_" + k])
    nc = build_fused_launch()
    in_maps = []
    for c in cores:
        d = dict(base)
        d["x"] = np.ascontiguousarray(x[c])
        s = np.zeros((8, 1), np.float32); s[:c] = 1.0
        d["l3f_sel"] = s
        in_maps.append(d)
    res = run_bass_kernel_spmd(nc, in_maps, core_ids=cores)
    return np.stack([res.results[c]["xo"] for c in cores])
